# version 06
# baseline (speedup 1.0000x reference)
import os
import numpy as np
from contextlib import ExitStack

import concourse.bass as bass
import concourse.mybir as mybir
from concourse.bass_utils import run_bass_kernel_spmd

F32 = mybir.dt.float32
BF16 = mybir.dt.bfloat16
U8 = mybir.dt.uint8
AF = mybir.ActivationFunctionType
ALU = mybir.AluOpType
AX = mybir.AxisListType

T = 4096
E = 1024
NT = T // 128
NC = E // 128
NH = 16
HD = 64
LN_EPS = 1e-5
LA_EPS = 1e-6
PI = float(np.pi)

N_IN = 8192
N_PROC = 4096
N_OUT = 8192
D_PV = 512
K_IN = 1024
K_PROC = 512
K_OUT = 1024
BISECT_ITERS = 42

ENGINES = ("tensor", "vector", "scalar", "gpsimd", "sync")


class Prog:

    def __init__(self, nc, ctx):
        self.nc = nc
        self.ctx = ctx
        self.q = {e: [] for e in ENGINES}
        self.sems = {}
        self.cnt = {}

    def sem(self, name):
        if name not in self.sems:
            self.sems[name] = self.ctx.enter_context(self.nc.semaphore(name))
            self.cnt[name] = 0
        return self.sems[name]

    def op(self, eng, fn, inc=None, dma=False):
        incs = []
        tgt = None
        if inc is not None:
            for name in ([inc] if isinstance(inc, str) else inc):
                step = 16 if dma else 1
                s = self.sem(name)
                self.cnt[name] += step
                incs.append((s, step))
                if tgt is None:
                    tgt = self.cnt[name]

        def run(engine, fn=fn, incs=incs):
            instr = fn(engine)
            for s, step in incs:
                instr = instr.then_inc(s, step)
        self.q[eng].append(run)
        return tgt

    def dma(self, eng, out, in_, inc, **kw):
        return self.op(eng, lambda e: e.dma_start(out, in_, **kw), inc=inc, dma=True)

    def wait(self, eng, sem_name, tgt):
        s = self.sem(sem_name)
        self.q[eng].append(lambda engine: engine.wait_ge(s, tgt))

    def waitp(self, eng, pair):
        if pair:
            self.wait(eng, pair[0], pair[1])

    def now(self, name):
        self.sem(name)
        return self.cnt[name]

    def finalize(self):
        blk = self.ctx.enter_context(self.nc.Block())
        for eng in ENGINES:
            ops = self.q[eng]

            def make(ops):
                def body(engine):
                    for f in ops:
                        f(engine)
                return body
            getattr(blk, eng)(make(ops))


def ts(i, n=128):
    return slice(i * n, (i + 1) * n)



def build_constants(P, B):
    g = "gpsimd"
    P.op(g, lambda e: e.iota(B.iota_f[:, :], [[1, 128]], channel_multiplier=0,
                             allow_small_or_imprecise_dtypes=True), inc="g")
    P.op(g, lambda e: e.iota(B.iota_p[:, :], [[1, 1]], channel_multiplier=1,
                             allow_small_or_imprecise_dtypes=True), inc="g")
    P.op(g, lambda e: e.memset(B.ones_col[:, :], 1.0), inc="g")
    P.op(g, lambda e: e.memset(B.ones11[:, :], 1.0), inc="g")
    P.op(g, lambda e: e.memset(B.ones_row_bf[:, :], 1.0), inc="g")
    P.op(g, lambda e: e.memset(B.ones11_bf[:, :], 1.0), inc="g")
    tg = P.op(g, lambda e: e.memset(B.eps_col[:, :], LN_EPS), inc="g")
    P.wait("vector", "g", tg)
    P.op("vector", lambda e: e.tensor_scalar(B.iden[:, :], B.iota_f[:, :], B.iota_p[:, 0:1],
                                             None, op0=ALU.is_equal), inc="v")
    tv = P.op("vector", lambda e: e.tensor_scalar(B.causal[:, :], B.iota_f[:, :], B.iota_p[:, 0:1],
                                                  None, op0=ALU.is_ge), inc="v")
    B.const_ready = tv


def build_ln_xnt(P, B, x_dram, tag):
    d = f"dx{tag}"
    v = "vector"
    sc = "scalar"
    for i in range(NT):
        bi = i % 2
        xt = B.xt[bi]
        if i >= 2:
            P.wait("sync", "v", B.xt_free[bi])
        td = P.dma("sync", xt[:, :], x_dram[ts(i), :], inc=d)
        P.wait(v, d, td)
        P.op(v, lambda e, xt=xt: e.reduce_sum(B.ln_m[:, 0:1], xt[:, :], axis=AX.X), inc="v")
        P.wait(sc, d, td)
        P.waitp(sc, B.gj_free[0])
        P.op(sc, lambda e, xt=xt: e.activation(B.gelu_junk[0][:, :], xt[:, 0:512], AF.Square,
                                               accum_out=B.ln_s2[:, 0:1]), inc="s")
        tsq = P.now("s")
        P.wait(sc, "s", tsq)
        P.waitp(sc, B.gj_free[1])
        tsq = P.op(sc, lambda e, xt=xt: e.activation(B.gelu_junk[1][:, :], xt[:, 512:1024], AF.Square,
                                                     accum_out=B.ln_sb[:, 0:1]), inc="s")
        B.gj_free = [("s", tsq), ("s", tsq)]
        tv = P.now("v")
        P.wait(v, "v", tv)
        P.op(v, lambda e: e.tensor_scalar(B.ln_m[:, 0:1], B.ln_m[:, 0:1], 1.0 / E, None, op0=ALU.mult), inc="v")
        tv = P.now("v")
        P.wait(v, "v", tv)
        P.op(v, lambda e: e.tensor_tensor(B.ln_msq[:, 0:1], B.ln_m[:, 0:1], B.ln_m[:, 0:1], op=ALU.mult), inc="v")
        P.wait(v, "s", tsq)
        tv = P.now("v")
        P.wait(v, "v", tv)
        P.op(v, lambda e: e.tensor_tensor(B.ln_s2[:, 0:1], B.ln_s2[:, 0:1], B.ln_sb[:, 0:1], op=ALU.add), inc="v")
        tv = P.now("v")
        P.wait(v, "v", tv)
        tv = P.op(v, lambda e: e.scalar_tensor_tensor(B.ln_var[:, 0:1], B.ln_s2[:, 0:1], 1.0 / E,
                                                      B.ln_msq[:, 0:1], op0=ALU.mult, op1=ALU.subtract), inc="v")
        P.wait(sc, "v", tv)
        P.op(sc, lambda e: e.activation(B.ln_u[:, 0:1], B.ln_var[:, 0:1], AF.Ln, bias=B.eps_col[:, 0:1]), inc="s")
        tssc = P.now("s")
        P.wait(sc, "s", tssc)
        tssc = P.op(sc, lambda e: e.activation(B.ln_rstd[:, 0:1], B.ln_u[:, 0:1], AF.Exp, scale=-0.5), inc="s")
        P.wait(v, "s", tssc)
        P.wait(v, "m", B.xnbf_free)
        tv = P.op(v, lambda e, xt=xt: e.tensor_scalar(B.xn_bf[:, :], xt[:, :], B.ln_m[:, 0:1],
                                                      B.ln_rstd[:, 0:1], op0=ALU.subtract, op1=ALU.mult), inc="v")
        B.xt_free[bi] = tv
        P.wait("tensor", "v", tv)
        for c in range(NC):
            pb = B.ps_bf[c % 2]
            P.waitp("tensor", B.psbf_free[c % 2])
            tm = P.op("tensor", lambda e, c=c, pb=pb: e.transpose(pb[:, 0:128], B.xn_bf[:, ts(c)], B.iden[:, :]), inc="m")
            P.wait(v, "m", tm)
            tv = P.op(v, lambda e, c=c, i=i, pb=pb: e.tensor_copy(B.xnT[:, c * T + 128 * i: c * T + 128 * (i + 1)],
                                                                  pb[:, 0:128]), inc="v")
            B.psbf_free[c % 2] = ("v", tv)
        B.xnbf_free = tm
    B.xnt_ready = P.now("v")
    B.x_stream_done = P.now(d)



def build_acts_mean(P, B, ipt_dram):
    v, sc, te, g = "vector", "scalar", "tensor", "gpsimd"
    tv = P.op(v, lambda e: e.memset(B.v_sb[:, :], 0.0), inc="v")
    ipt_r = ipt_dram.rearrange("(dc p) n -> p dc n", p=128)
    vcol_free = [0, 0]
    for j in range(64):
        bj = j % 2
        ib = B.ipt_buf[bj]
        P.waitp(g, B.iptbuf_free[bj])
        td = P.dma(g, ib.rearrange("p (dc n) -> p dc n", n=128), ipt_r[:, :, ts(j)], inc="dw")
        P.wait(te, "dw", td)
        P.wait(te, "v", tv)
        last_mm = 0
        for tb in range(8):
            pb = (tb % 2)
            pst = B.ps[pb]
            P.waitp(te, B.ps_free[pb])
            for dc in range(NC):
                last_mm = P.op(te, lambda e, dc=dc, tb=tb, ib=ib, pst=pst:
                               e.matmul(pst[:, :], ib[:, ts(dc)],
                                        B.xnT[:, dc * T + tb * 512: dc * T + (tb + 1) * 512],
                                        start=(dc == 0), stop=(dc == NC - 1)), inc="m")
            B.iptbuf_free[bj] = ("m", last_mm)
            P.wait(sc, "m", last_mm)
            P.waitp(sc, vcol_free[pb])
            tsc = P.op(sc, lambda e, pb=pb, pst=pst:
                       e.activation(B.gelu_junk[pb][:, :], pst[:, :], AF.Gelu,
                                    accum_out=B.vcol[pb][:, 0:1]), inc="s")
            B.ps_free[pb] = ("s", tsc)
            P.wait(v, "s", tsc)
            tv2 = P.op(v, lambda e, j=j, pb=pb:
                       e.tensor_tensor(B.v_sb[:, j:j + 1], B.v_sb[:, j:j + 1],
                                       B.vcol[pb][:, 0:1], op=ALU.add), inc="v")
            vcol_free[pb] = ("v", tv2)
            P.wait(v, "v", tv2)
    tv = P.op(v, lambda e: e.tensor_scalar(B.v_sb[:, :], B.v_sb[:, :], 1.0 / T, None,
                                           op0=ALU.mult), inc="v")
    B.vsb_ready = tv


def partition_total(P, B, src_col_fn, red_op, out11, nwait):
    v = "vector"
    t0 = src_col_fn()
    P.wait(v, "v", t0)
    t1 = P.op(v, lambda e: e.tensor_copy(B.fold_s[0:64, 0:1], B.fold_a[64:128, 0:1]), inc="v")
    P.wait(v, "v", t1)
    t1 = P.op(v, lambda e: e.tensor_tensor(B.fold_b[0:64, 0:1], B.fold_a[0:64, 0:1],
                                           B.fold_s[0:64, 0:1], op=red_op), inc="v")
    P.wait(v, "v", t1)
    t2 = P.op(v, lambda e: e.tensor_copy(B.fold_s[0:32, 0:1], B.fold_b[32:64, 0:1]), inc="v")
    P.wait(v, "v", t2)
    t2 = P.op(v, lambda e: e.tensor_tensor(B.fold_c[0:32, 0:1], B.fold_b[0:32, 0:1],
                                           B.fold_s[0:32, 0:1], op=red_op), inc="v")
    P.wait(v, "v", t2)
    t3 = P.op(v, lambda e: e.tensor_copy(B.fold_t[0:32, 0:1], B.fold_c[0:32, 0:1]), inc="v")
    P.wait(v, "v", t3)
    t4 = P.op(v, lambda e: e.transpose(B.fold_tt[0:32, 0:32], B.fold_t[0:32, 0:32]), inc="v")
    P.wait(v, "v", t4)
    t5 = P.op(v, lambda e: e.tensor_reduce(out11, B.fold_tt[0:1, 0:32], axis=AX.X, op=red_op),
              inc="v")
    P.wait(v, "v", t5)
    return t5


def bcast_col(P, B, src11, dst_col, psb, n_rows=128):
    te, v = "tensor", "vector"
    P.wait(te, "v", P.now("v"))
    tm = P.op(te, lambda e: e.matmul(psb[0:n_rows, 0:1], B.ones_row[0:1, 0:n_rows],
                                     src11, start=True, stop=True), inc="m")
    P.wait(v, "m", tm)
    tv = P.op(v, lambda e: e.tensor_copy(dst_col, psb[0:n_rows, 0:1]), inc="v")
    P.wait(v, "v", tv)
    return tv


def build_bisect(P, B, src, F, k, thr_rep):
    v = "vector"
    psb = B.bcb
    def mx():
        return P.op(v, lambda e: e.tensor_reduce(B.fold_a[:, 0:1], src, axis=AX.X, op=ALU.max), inc="v")
    partition_total(P, B, mx, ALU.max, B.bis[0:1, 1:2], 0)

    def mn():
        t0 = P.op(v, lambda e: e.tensor_scalar(B.bis_neg[:, 0:F], src, -1.0, None, op0=ALU.mult), inc="v")
        P.wait(v, "v", t0)
        return P.op(v, lambda e: e.tensor_reduce(B.fold_a[:, 0:1], B.bis_neg[:, 0:F], axis=AX.X, op=ALU.max), inc="v")
    partition_total(P, B, mn, ALU.max, B.bis[0:1, 2:3], 0)
    t = P.op(v, lambda e: e.tensor_scalar(B.bis[0:1, 0:1], B.bis[0:1, 2:3], -1.0, -1.0,
                                          op0=ALU.mult, op1=ALU.add), inc="v")
    P.wait(v, "v", t)
    t = P.op(v, lambda e: e.tensor_scalar(B.bis[0:1, 1:2], B.bis[0:1, 1:2], 1.0, None,
                                          op0=ALU.add), inc="v")
    P.wait(v, "v", t)
    for it in range(BISECT_ITERS):
        t = P.op(v, lambda e: e.tensor_tensor(B.bis[0:1, 3:4], B.bis[0:1, 1:2], B.bis[0:1, 0:1],
                                              op=ALU.subtract), inc="v")
        P.wait(v, "v", t)
        t = P.op(v, lambda e: e.tensor_scalar(B.bis[0:1, 3:4], B.bis[0:1, 3:4], 0.5, None,
                                              op0=ALU.mult), inc="v")
        P.wait(v, "v", t)
        t = P.op(v, lambda e: e.tensor_tensor(B.bis[0:1, 4:5], B.bis[0:1, 0:1], B.bis[0:1, 3:4],
                                              op=ALU.add), inc="v")
        P.wait(v, "v", t)
        bcast_col(P, B, B.bis[0:1, 4:5], B.bis_mid[:, 0:1], psb)
        t = P.op(v, lambda e: e.tensor_scalar(B.bis_ge[:, 0:F], src, B.bis_mid[:, 0:1], None,
                                              op0=ALU.is_ge), inc="v")
        P.wait(v, "v", t)

        def cnt():
            return P.op(v, lambda e: e.tensor_reduce(B.fold_a[:, 0:1], B.bis_ge[:, 0:F],
                                                     axis=AX.X, op=ALU.add), inc="v")
        partition_total(P, B, cnt, ALU.add, B.bis[0:1, 5:6], 0)
        t = P.op(v, lambda e: e.tensor_scalar(B.bis[0:1, 6:7], B.bis[0:1, 5:6], k - 0.5, None,
                                              op0=ALU.is_ge), inc="v")
        P.wait(v, "v", t)
        t = P.op(v, lambda e: e.tensor_tensor(B.bis[0:1, 7:8], B.bis[0:1, 6:7], B.bis[0:1, 3:4],
                                              op=ALU.mult), inc="v")
        P.wait(v, "v", t)
        t = P.op(v, lambda e: e.tensor_tensor(B.bis[0:1, 0:1], B.bis[0:1, 0:1], B.bis[0:1, 7:8],
                                              op=ALU.add), inc="v")
        P.wait(v, "v", t)
        t = P.op(v, lambda e: e.tensor_tensor(B.bis[0:1, 1:2], B.bis[0:1, 4:5], B.bis[0:1, 7:8],
                                              op=ALU.add), inc="v")
        P.wait(v, "v", t)
    return bcast_col(P, B, B.bis[0:1, 0:1], thr_rep, psb)



def build_ffn_chain(P, B, piwt, pv, oiwt, op_dram):
    v, sc, te, g = "vector", "scalar", "tensor", "gpsimd"

    def matvec_stream(lhs_cols, w_dram, n_chunks, n_pb, tile_cols, accum_rows=1):
        for c in range(n_chunks):
            wb = B.wstream[c % 2]
            P.waitp(g, B.wstream_free[c % 2])
            td = P.dma(g, wb[:, 0:tile_cols], w_dram[ts(c), 0:tile_cols], inc="dw")
            P.wait(te, "dw", td)
            last = 0
            for pb in range(n_pb):
                if c == 0:
                    P.waitp(te, B.ps_free[pb])
                last = P.op(te, lambda e, c=c, pb=pb, wb=wb:
                            e.matmul(B.ps[pb][0:1, 0:512], lhs_cols[:, c:c + 1],
                                     wb[:, pb * 512:(pb + 1) * 512],
                                     start=(c == 0), stop=(c == n_chunks - 1)), inc="m")
            B.wstream_free[c % 2] = ("m", last)
        return P.now("m")

    P.wait(te, "v", B.maskedv_ready)
    tmm = matvec_stream(B.masked_bf, piwt, 64, 8, 4096)
    for pb in range(8):
        rb = B.rowbuf[pb % 2]
        P.wait(sc, "m", tmm)
        P.waitp(sc, B.rowbuf_free[pb % 2])
        tsc = P.op(sc, lambda e, pb=pb, rb=rb: e.activation(rb[0:1, :],
                                                            B.ps[pb][0:1, 0:512], AF.Gelu), inc="s")
        B.ps_free[pb] = ("s", tsc)
        P.wait(te, "s", tsc)
        if pb == 0:
            P.waitp(te, B.ps_free[0])
        for j in range(4):
            tm = P.op(te, lambda e, pb=pb, j=j, rb=rb:
                      e.matmul(B.ps[0][:, pb * 4 + j: pb * 4 + j + 1], rb[0:1, ts(j)],
                               B.ones11[0:1, 0:1], start=True, stop=True), inc="m")
        B.rowbuf_free[pb % 2] = ("m", tm)
    P.wait(v, "m", tm)
    tv = P.op(v, lambda e: e.tensor_copy(B.p_sb[:, :], B.ps[0][:, 0:32]), inc="v")
    B.ps_free[0] = ("v", tv)
    P.wait(v, "v", tv)

    build_bisect(P, B, B.p_sb[:, :], 32, K_PROC, B.thr2_rep[:, 0:1])
    def mx():
        return P.op(v, lambda e: e.tensor_reduce(B.fold_a[:, 0:1], B.p_sb[:, :], axis=AX.X,
                                                 op=ALU.max), inc="v")
    partition_total(P, B, mx, ALU.max, B.sc4[0:1, 0:1], 0)
    t = P.op(v, lambda e: e.tensor_scalar(B.sc4[0:1, 1:2], B.sc4[0:1, 0:1], -1.0, None,
                                          op0=ALU.mult), inc="v")
    P.wait(v, "v", t)
    bcast_col(P, B, B.sc4[0:1, 1:2], B.negpmax[:, 0:1], B.bcb)
    P.wait(sc, "v", P.now("v"))
    t = P.op(sc, lambda e: e.activation(B.e_sb[:, :], B.p_sb[:, :], AF.Exp,
                                        bias=B.negpmax[:, 0:1]), inc="s")
    P.wait(v, "s", t)
    t = P.op(v, lambda e: e.tensor_scalar(B.mask_f[:, 0:32], B.p_sb[:, :], B.thr2_rep[:, 0:1],
                                          None, op0=ALU.is_ge), inc="v")
    P.wait(v, "v", t)
    t = P.op(v, lambda e: e.tensor_tensor(B.e_sb[:, :], B.e_sb[:, :], B.mask_f[:, 0:32],
                                          op=ALU.mult), inc="v")
    P.wait(v, "v", t)
    t = P.op(v, lambda e: e.tensor_copy(B.e_bf[:, :], B.e_sb[:, :]), inc="v")
    P.wait(v, "v", t)
    def sm():
        return P.op(v, lambda e: e.tensor_reduce(B.fold_a[:, 0:1], B.e_sb[:, :], axis=AX.X,
                                                 op=ALU.add), inc="v")
    partition_total(P, B, sm, ALU.add, B.sc4[0:1, 2:3], 0)
    t = P.op(v, lambda e: e.reciprocal(B.sc4[0:1, 3:4], B.sc4[0:1, 2:3]), inc="v")
    P.wait(v, "v", t)

    P.wait(te, "v", P.now("v"))
    for c in range(32):
        wb = B.pv_buf[c % 2]
        P.waitp(g, B.pvbuf_free[c % 2])
        td = P.dma(g, wb[:, :], pv[ts(c), :], inc="dw")
        P.wait(te, "dw", td)
        if c == 0:
            P.waitp(te, B.ps_free[1])
        tm = P.op(te, lambda e, c=c, wb=wb: e.matmul(B.ps[1][0:1, 0:512], B.e_bf[:, c:c + 1],
                                                     wb[:, :], start=(c == 0), stop=(c == 31)), inc="m")
        B.pvbuf_free[c % 2] = ("m", tm)
    P.wait(v, "m", tm)
    tv = P.op(v, lambda e: e.tensor_scalar(B.agg_sb[0:1, :], B.ps[1][0:1, 0:512],
                                           B.sc4[0:1, 3:4], None, op0=ALU.mult), inc="v")
    B.ps_free[1] = ("v", tv)
    P.wait(v, "v", tv)
    P.wait(te, "v", tv)
    P.waitp(te, B.ps_free[2])
    for j in range(4):
        tm = P.op(te, lambda e, j=j: e.matmul(B.ps[2][:, j:j + 1], B.agg_sb[0:1, ts(j)],
                                              B.ones11_bf[0:1, 0:1], start=True, stop=True), inc="m")
    P.wait(v, "m", tm)
    tv = P.op(v, lambda e: e.tensor_copy(B.agg_col_bf[:, :], B.ps[2][:, 0:4]), inc="v")
    B.ps_free[2] = ("v", tv)
    P.wait(v, "v", tv)

    oiwt_r = oiwt.rearrange("(dc p) n -> p dc n", p=128)
    for half in range(2):
        for dc in range(4):
            wb = B.wstream[dc % 2]
            P.waitp(g, B.wstream_free[dc % 2])
            td = P.dma(g, wb[:, 0:4096], oiwt_r[:, dc, half * 4096:(half + 1) * 4096], inc="dw")
            P.wait(te, "dw", td)
            last = 0
            for pb in range(8):
                if dc == 0:
                    P.waitp(te, B.ps_free[pb])
                last = P.op(te, lambda e, dc=dc, pb=pb, wb=wb:
                            e.matmul(B.ps[pb][0:1, 0:512], B.agg_col_bf[:, dc:dc + 1],
                                     wb[:, pb * 512:(pb + 1) * 512],
                                     start=(dc == 0), stop=(dc == 3)), inc="m")
            B.wstream_free[dc % 2] = ("m", last)
        for pb in range(8):
            rb = B.rowbuf[pb % 2]
            P.wait(sc, "m", last)
            P.waitp(sc, B.rowbuf_free[pb % 2])
            tsc = P.op(sc, lambda e, pb=pb, rb=rb:
                       e.activation(rb[0:1, :], B.ps[pb][0:1, 0:512], AF.Gelu), inc="s")
            B.ps_free[pb] = ("s", tsc)
            P.wait(te, "s", tsc)
            if pb == 0:
                P.waitp(te, B.ps_free[0])
            for j in range(4):
                tm = P.op(te, lambda e, pb=pb, j=j, rb=rb:
                          e.matmul(B.ps[0][:, pb * 4 + j: pb * 4 + j + 1], rb[0:1, ts(j)],
                                   B.ones11[0:1, 0:1], start=True, stop=True), inc="m")
            B.rowbuf_free[pb % 2] = ("m", tm)
        P.wait(v, "m", tm)
        tv = P.op(v, lambda e, half=half: e.tensor_copy(B.o_sb[:, half * 32:(half + 1) * 32],
                                                        B.ps[0][:, 0:32]), inc="v")
        B.ps_free[0] = ("v", tv)
        P.wait(v, "v", tv)

    build_bisect(P, B, B.o_sb[:, :], 64, K_OUT, B.thr3_rep[:, 0:1])
    t = P.op(v, lambda e: e.tensor_scalar(B.mask_f[:, :], B.o_sb[:, :], B.thr3_rep[:, 0:1],
                                          None, op0=ALU.is_ge), inc="v")
    P.wait(v, "v", t)
    t = P.op(v, lambda e: e.tensor_tensor(B.masked_bf[:, :], B.o_sb[:, :], B.mask_f[:, :],
                                          op=ALU.mult), inc="v")
    P.wait(v, "v", t)
    P.wait(te, "v", t)
    for c in range(64):
        wb = B.wstream[c % 2]
        P.waitp(g, B.wstream_free[c % 2])
        td = P.dma(g, wb[:, 0:1024], op_dram[ts(c), :], inc="dw")
        P.wait(te, "dw", td)
        last = 0
        for pb in range(2):
            if c == 0:
                P.waitp(te, B.ps_free[pb])
            last = P.op(te, lambda e, c=c, pb=pb, wb=wb:
                        e.matmul(B.ps[pb][0:1, 0:512], B.masked_bf[:, c:c + 1],
                                 wb[:, pb * 512:(pb + 1) * 512],
                                 start=(c == 0), stop=(c == 63)), inc="m")
        B.wstream_free[c % 2] = ("m", last)
    P.wait(v, "m", last)
    tv = P.op(v, lambda e: e.tensor_copy(B.ffn_row[0:1, 0:512], B.ps[0][0:1, 0:512]), inc="v")
    B.ps_free[0] = ("v", tv)
    tv = P.op(v, lambda e: e.tensor_copy(B.ffn_row[0:1, 512:1024], B.ps[1][0:1, 0:512]), inc="v")
    B.ps_free[1] = ("v", tv)
    P.wait(v, "v", tv)
    P.wait(te, "v", tv)
    P.waitp(te, B.ps_free[2])
    P.waitp(te, B.ps_free[3])
    tm2 = []
    for j in range(2):
        tm = P.op(te, lambda e, j=j: e.matmul(B.ps[2 + j][:, 0:512], B.ones_row_bf[0:1, :],
                                              B.ffn_row[0:1, j * 512:(j + 1) * 512],
                                              start=True, stop=True), inc="m")
        tm2.append(tm)
    P.wait(v, "m", tm)
    for j in range(2):
        tv = P.op(v, lambda e, j=j: e.tensor_copy(B.ffn_rep[:, j * 512:(j + 1) * 512],
                                                  B.ps[2 + j][:, 0:512]), inc="v")
        B.ps_free[2 + j] = ("v", tv)
    P.wait(v, "v", tv)
    B.ffn_ready = tv



def barrier(P):
    tgts = [(n, P.now(n)) for n in ("m", "v", "s", "g")]
    for eng in ENGINES:
        for n, t in tgts:
            if t:
                P.wait(eng, n, t)


def build_linattn(P, B, w_attn):
    v, sc, te, g = "vector", "scalar", "tensor", "gpsimd"
    wr = w_attn.rearrange("(ec p) n -> p ec n", p=128)
    for hp in range(int(os.environ.get("DEVHP", "8"))):
        barrier(P)
        wsl = B.wstream[0]
        for j, base in enumerate((0, E, 2 * E)):
            P.dma(g, wsl[:, j * 1024:(j + 1) * 1024].rearrange("p (ec n) -> p ec n", n=128),
                  wr[:, :, base + hp * 128: base + (hp + 1) * 128], inc="dw")
        tdw = P.now("dw")
        P.wait(te, "dw", tdw)
        gj_war = [0, 0]
        for which, dst in ((0, B.qT_hp), (1, B.kT_hp)):
            for tb in range(8):
                pb = tb % 2
                pst = B.ps[pb]
                P.waitp(te, B.ps_free[pb])
                for ec in range(NC):
                    tm = P.op(te, lambda e, ec=ec, tb=tb, which=which, pst=pst:
                              e.matmul(pst[:, :], wsl[:, which * 1024 + ec * 128: which * 1024 + (ec + 1) * 128],
                                       B.xnT[:, ec * T + tb * 512: ec * T + (tb + 1) * 512],
                                       start=(ec == 0), stop=(ec == NC - 1)), inc="m")
                P.wait(v, "m", tm)
                P.waitp(v, gj_war[pb])
                t1 = P.op(v, lambda e, pb=pb, pst=pst: e.tensor_scalar(B.gelu_junk[pb][:, :], pst[:, :], 0.0,
                                                                      None, op0=ALU.min), inc="v")
                P.wait(sc, "v", t1)
                t2 = P.op(sc, lambda e, pb=pb: e.activation(B.gelu_junk[pb][:, :], B.gelu_junk[pb][:, :],
                                                            AF.Exp), inc="s")
                P.wait(v, "s", t2)
                t3 = P.op(v, lambda e, pb=pb, pst=pst, dst=dst, tb=tb:
                          e.scalar_tensor_tensor(dst[:, tb * 512:(tb + 1) * 512], pst[:, :], 0.0,
                                                 B.gelu_junk[pb][:, :], op0=ALU.max, op1=ALU.add), inc="v")
                B.ps_free[pb] = ("v", t3)
                gj_war[pb] = ("v", t3)
        tmem = P.op(v, lambda e: e.memset(B.Vt_hp[:, :], 1.0), inc="v")
        P.wait(v, "v", tmem)
        P.wait(te, "v", tmem)
        for ti in range(NT):
            pb = ti % 2
            pst = B.ps[pb]
            P.waitp(te, B.ps_free[pb])
            for ec in range(NC):
                tm = P.op(te, lambda e, ec=ec, ti=ti, pst=pst:
                          e.matmul(pst[:, 0:128], B.xnT[:, ec * T + ti * 128: ec * T + (ti + 1) * 128],
                                   wsl[:, 2048 + ec * 128: 2048 + (ec + 1) * 128],
                                   start=(ec == 0), stop=(ec == NC - 1)), inc="m")
            P.wait(v, "m", tm)
            tv = P.op(v, lambda e, ti=ti, pst=pst:
                      e.tensor_copy(B.Vt_hp[:, ti * 130: ti * 130 + 130].rearrange("p (h d) -> p h d", h=2)[:, :, 0:64],
                                    pst[:, 0:128].rearrange("p (h d) -> p h d", h=2)), inc="v")
            B.ps_free[pb] = ("v", tv)
        barrier(P)
        psS = [B.ps[6], B.ps[7]]
        psc = [B.ps[4], B.ps[5]]
        psy = [B.ps[0], B.ps[1]]
        psr = B.ps[2]
        psk = B.ps_bf[1]
        knat = B.exbp[0][:, 0:512]
        ab = [B.exbp[0][:, 512:1024], B.exbp[1][:, 0:512]]
        free = {"S0": 0, "S1": 0, "y0": 0, "y1": 0, "r": 0,
                "knat": 0, "ab0": 0, "ab1": 0, "Ssb0": 0, "Ssb1": 0, "rs": 0, "den": 0,
                "abm0": 0, "abm1": 0, "knsb": 0, "rrow": 0, "sc0": 0, "sc1": 0,
                "y0m": 0, "y1m": 0}
        for g_ in range(8):
            P.waitp(te, free["knat"])
            for j_ in range(4):
                tkn = P.op(te, lambda e, g_=g_, j_=j_:
                           e.transpose(psk[:, j_ * 128:(j_ + 1) * 128],
                                       B.kT_hp[:, g_ * 512 + j_ * 128: g_ * 512 + (j_ + 1) * 128],
                                       B.iden[:, :]), inc="m")
            P.wait(v, "m", tkn)
            P.waitp(v, free["knsb"])
            tknv = P.op(v, lambda e, knat=knat: e.tensor_copy(knat, psk[:, 0:512]), inc="v")
            free["knat"] = ("v", tknv)
            for h in range(2):
                hb = h * 64
                for j_ in range(4):
                    P.waitp(te, free[f"sc{h}"])
                    tsc_ = P.op(te, lambda e, g_=g_, j_=j_, h=h, hb=hb:
                                e.matmul(psc[h][:, :],
                                         B.kT_hp[hb:hb + 64, g_ * 512 + j_ * 128: g_ * 512 + (j_ + 1) * 128],
                                         B.qT_hp[hb:hb + 64, g_ * 512:(g_ + 1) * 512],
                                         start=True, stop=True, tile_position=(hb, 0)), inc="m")
                    P.wait(v, "m", tsc_)
                    if j_ == 0:
                        P.waitp(v, free[f"abm{h}"])
                    tab = P.op(v, lambda e, j_=j_, h=h:
                               e.tensor_tensor(ab[h][:, j_ * 128:(j_ + 1) * 128],
                                               psc[h][:, j_ * 128:(j_ + 1) * 128],
                                               B.causal[:, :], op=ALU.mult), inc="v")
                    if j_ < 3:
                        tab = P.op(v, lambda e, j_=j_, h=h:
                                   e.tensor_copy(ab[h][:, (j_ + 1) * 128: 512],
                                                 psc[h][:, (j_ + 1) * 128: 512]), inc="v")
                    free[f"sc{h}"] = ("v", tab)
                    P.wait(te, "v", tab)
                    if j_ == 0:
                        P.waitp(te, free[f"y{h}"])
                    tnum = P.op(te, lambda e, g_=g_, j_=j_, h=h:
                                e.matmul(psy[h][0:65, j_ * 128: 512],
                                         B.Vt_hp[:, (g_ * 4 + j_) * 130 + h * 65: (g_ * 4 + j_) * 130 + h * 65 + 65],
                                         ab[h][:, j_ * 128: 512],
                                         start=(j_ == 0), stop=(g_ == 0 and j_ == 3)), inc="m")
                if g_ > 0:
                    P.waitp(te, free[f"Ssb{h}"])
                    tnum = P.op(te, lambda e, g_=g_, h=h, hb=hb:
                                e.matmul(psy[h][0:65, 0:512], B.S_sb[hb:hb + 64, 0:65],
                                         B.qT_hp[hb:hb + 64, g_ * 512:(g_ + 1) * 512],
                                         start=False, stop=True, tile_position=(hb, 0)), inc="m")
                free[f"abm{h}"] = ("m", tnum)
                free[f"y{h}m"] = ("m", tnum)
                P.wait(te, "v", tknv)
                P.waitp(te, free[f"S{h}"])
                for j_ in range(4):
                    tS = P.op(te, lambda e, g_=g_, j_=j_, h=h, hb=hb, knat=knat:
                              e.matmul(psS[h][0:64, 0:65],
                                       knat[:, j_ * 128 + hb: j_ * 128 + hb + 64],
                                       B.Vt_hp[:, (g_ * 4 + j_) * 130 + h * 65: (g_ * 4 + j_) * 130 + h * 65 + 65],
                                       start=(g_ == 0 and j_ == 0), stop=(g_ == 7 and j_ == 3)), inc="m")
                if h == 1:
                    free["knsb"] = ("m", tS)
                P.wait(v, "m", tS)
                tsnap = P.op(v, lambda e, h=h, hb=hb: e.tensor_copy(B.S_sb[hb:hb + 64, 0:65],
                                                                   psS[h][0:64, 0:65]), inc="v")
                free[f"S{h}"] = ("v", tsnap)
                free[f"Ssb{h}"] = ("v", tsnap)
            for h in range(2):
                hb = h * 64
                P.wait(v, "m", free[f"y{h}m"][1])
                P.waitp(v, free["den"])
                td1 = P.op(v, lambda e, h=h: e.tensor_scalar(B.den_sb[0:1, 0:512], psy[h][64:65, 0:512],
                                                             LA_EPS, None, op0=ALU.add), inc="v")
                P.wait(v, "v", td1)
                P.waitp(v, free["rrow"])
                td2 = P.op(v, lambda e: e.reciprocal(B.rrow_sb[0:1, 0:512], B.den_sb[0:1, 0:512]), inc="v")
                free["den"] = ("v", td2)
                P.wait(te, "v", td2)
                P.waitp(te, free["r"])
                tr = P.op(te, lambda e: e.matmul(psr[0:64, 0:512], B.ones_row[0:1, 0:64],
                                                 B.rrow_sb[0:1, 0:512], start=True, stop=True), inc="m")
                free["rrow"] = ("m", tr)
                P.wait(v, "m", tr)
                P.waitp(v, free["rs"])
                trs = P.op(v, lambda e: e.tensor_copy(B.r_sb[0:64, 0:512], psr[0:64, 0:512]), inc="v")
                free["r"] = ("v", trs)
                P.wait(v, "v", trs)
                ty = P.op(v, lambda e, h=h, hb=hb, g_=g_, hp=hp:
                          e.tensor_tensor(B.yT[hb:hb + 64, hp * T + g_ * 512: hp * T + (g_ + 1) * 512],
                                          psy[h][0:64, 0:512], B.r_sb[0:64, 0:512],
                                          op=ALU.mult), inc="v")
                free[f"y{h}"] = ("v", ty)
                free["rs"] = ("v", ty)
    B.yT_ready = P.now("v")


def build_wproj_residual(P, B, w_dram, x_dram, x1_dram):
    v, sc, te, g, sy = "vector", "scalar", "tensor", "gpsimd", "sync"
    barrier(P)
    wr = w_dram.rearrange("(ec p) n -> p ec n", p=128)
    P.dma(g, B.wstream[0][:, :].rearrange("p (ec n) -> p ec n", n=1024), wr[:, 0:4, :], inc="dw")
    P.dma(g, B.wstream[1][:, :].rearrange("p (ec n) -> p ec n", n=1024), wr[:, 4:8, :], inc="dw")
    tdw = P.now("dw")
    P.wait(te, "dw", tdw)
    d = "dxf"
    for i in range(NT):
        bi = i % 2
        xt = B.xt[bi]
        P.waitp(sy, B.xt_free2[bi])
        td = P.dma(sy, xt[:, :], x_dram[ts(i), :], inc=d)
        for half in range(2):
            pb = half
            pst = B.ps[pb]
            P.waitp(te, B.ps_free[pb])
            for cc in range(NC):
                tm = P.op(te, lambda e, cc=cc, i=i, half=half, pst=pst:
                          e.matmul(pst[:, :], B.yT[:, cc * T + i * 128: cc * T + (i + 1) * 128],
                                   B.wstream[cc // 4][:, (cc % 4) * 1024 + half * 512: (cc % 4) * 1024 + (half + 1) * 512],
                                   start=(cc == 0), stop=(cc == NC - 1)), inc="m")
            P.wait(v, "m", tm)
            P.wait(v, d, td)
            t1 = P.op(v, lambda e, bi=bi, half=half, pst=pst:
                      e.tensor_tensor(B.x1_sb[bi][:, half * 512:(half + 1) * 512], pst[:, :],
                                      B.xt[bi][:, half * 512:(half + 1) * 512], op=ALU.add), inc="v")
            P.wait(v, "v", t1)
            t2 = P.op(v, lambda e, bi=bi, half=half:
                      e.tensor_tensor(B.x1_sb[bi][:, half * 512:(half + 1) * 512],
                                      B.x1_sb[bi][:, half * 512:(half + 1) * 512],
                                      B.ffn_rep[:, half * 512:(half + 1) * 512], op=ALU.add), inc="v")
            B.ps_free[pb] = ("v", t2)
        tvv = P.now("v")
        P.wait(sy, "v", tvv)
        tso = P.dma(sy, x1_dram[ts(i), :], B.x1_sb[bi][:, :], inc="dx1")
        B.xt_free2[bi] = ("dx1", tso)
    P.wait(sy, "dx1", P.now("dx1"))



def build_rope_tables(P, B, nrope_dram):
    v, sc, te, g, sy = "vector", "scalar", "tensor", "gpsimd", "sync"
    barrier(P)
    td = P.dma(sy, B.nr_sb[0:1, 0:1], nrope_dram[0:1, 0:1], inc="dh")
    P.wait(sc, "dh", td)
    t = P.op(sc, lambda e: e.activation(B.nr_sb[0:1, 1:2], B.nr_sb[0:1, 0:1], AF.Sigmoid), inc="s")
    P.wait(v, "s", t)
    t = P.op(v, lambda e: e.tensor_scalar(B.nr_sb[0:1, 2:3], B.nr_sb[0:1, 1:2], 3400.0, 8300.0,
                                          op0=ALU.mult, op1=ALU.add), inc="v")
    P.wait(sc, "v", t)
    t = P.op(sc, lambda e: e.activation(B.nr_sb[0:1, 3:4], B.nr_sb[0:1, 2:3], AF.Ln), inc="s")
    tg = P.op(g, lambda e: e.iota(B.jrow[0:1, 0:16], [[1, 16]], channel_multiplier=0,
                                  allow_small_or_imprecise_dtypes=True), inc="g")
    P.wait(v, "s", t)
    P.wait(v, "g", tg)
    t = P.op(v, lambda e: e.tensor_scalar(B.jrow[0:1, 16:32], B.jrow[0:1, 0:16], B.nr_sb[0:1, 3:4],
                                          -1.0 / 16.0, op0=ALU.mult, op1=ALU.mult), inc="v")
    P.wait(sc, "v", t)
    t = P.op(sc, lambda e: e.activation(B.jrow[0:1, 32:48], B.jrow[0:1, 16:32], AF.Exp), inc="s")
    P.wait(v, "s", t)
    t = P.op(v, lambda e: e.tensor_scalar(B.jrow[0:1, 48:64], B.jrow[0:1, 32:48], 2.0 * PI / (T - 1), None,
                                          op0=ALU.mult), inc="v")
    P.wait(v, "v", t)
    P.waitp(te, B.ps_free[5])
    P.wait(te, "v", t)
    for i in range(NT):
        pi_ = i % 2
        P.waitp(g, B.posrow_free[pi_])
        tg = P.op(g, lambda e, i=i, pi_=pi_: e.iota(B.posrow[pi_][0:1, :], [[1, 128]], base=i * 128,
                                                    channel_multiplier=0,
                                                    allow_small_or_imprecise_dtypes=True), inc="g")
        P.wait(te, "g", tg)
        tm = P.op(te, lambda e, i=i, pi_=pi_: e.matmul(B.ps[5][:, i * 16:(i + 1) * 16],
                                                       B.posrow[pi_][0:1, :], B.jrow[0:1, 48:64],
                                                       start=True, stop=True), inc="m")
        B.posrow_free[pi_] = ("m", tm)
    tm = P.now("m")
    P.wait(v, "m", tm)
    t = P.op(v, lambda e: e.tensor_scalar(B.gelu_junk[0][:, :], B.ps[5][:, :], PI, None,
                                          op0=ALU.is_gt), inc="v")
    P.wait(v, "v", t)
    t = P.op(v, lambda e: e.scalar_tensor_tensor(B.gelu_junk[1][:, :], B.gelu_junk[0][:, :],
                                                 -2.0 * PI, B.ps[5][:, :], op0=ALU.mult,
                                                 op1=ALU.add), inc="v")
    B.ps_free[5] = ("v", t)
    P.wait(sc, "v", t)
    ts1 = P.op(sc, lambda e: e.activation(B.sin_nat[:, :], B.gelu_junk[1][:, :], AF.Sin), inc="s")
    P.wait(v, "v", t)
    t = P.op(v, lambda e: e.tensor_scalar(B.gelu_junk[0][:, :], B.gelu_junk[1][:, :], PI / 2, None,
                                          op0=ALU.add), inc="v")
    P.wait(v, "v", t)
    t = P.op(v, lambda e: e.tensor_scalar(B.cos_nat[:, :], B.gelu_junk[0][:, :], PI, None,
                                          op0=ALU.is_gt), inc="v")
    P.wait(v, "v", t)
    t = P.op(v, lambda e: e.scalar_tensor_tensor(B.cos_nat[:, :], B.cos_nat[:, :], -2.0 * PI,
                                                 B.gelu_junk[0][:, :], op0=ALU.mult,
                                                 op1=ALU.add), inc="v")
    P.wait(sc, "v", t)
    tc = P.op(sc, lambda e: e.activation(B.cos_nat[:, :], B.cos_nat[:, :], AF.Sin), inc="s")
    B.rope_ready = tc


def build_l1_attn(P, B, wq, wk, wv, bq, bk, bv):
    v, sc, te, g, sy = "vector", "scalar", "tensor", "gpsimd", "sync"
    wqr = wq.rearrange("(ec p) n -> p ec n", p=128)
    wkr = wk.rearrange("(ec p) n -> p ec n", p=128)
    wvr = wv.rearrange("(ec p) n -> p ec n", p=128)
    for hp in range(int(os.environ.get("DEVHP1", "8"))):
        barrier(P)
        wsl = B.wstream[0]
        for j, wr in enumerate((wqr, wkr, wvr)):
            P.dma(g, wsl[:, j * 1024:(j + 1) * 1024].rearrange("p (ec n) -> p ec n", n=128),
                  wr[:, :, hp * 128:(hp + 1) * 128], inc="dw")
        for j, bd in enumerate((bq, bk, bv)):
            P.dma(g, B.ffn_row[0:1, j * 128:(j + 1) * 128], bd[0:1, hp * 128:(hp + 1) * 128], inc="dh")
        tdw = P.now("dw")
        tdh = P.now("dh")
        P.wait(te, "dh", tdh)
        P.waitp(te, B.ps_free[4])
        tm = P.op(te, lambda e: e.matmul(B.ps[4][:, 0:384], B.ones_row_bf[0:1, :], B.ffn_row[0:1, 0:384],
                                         start=True, stop=True), inc="m")
        P.wait(v, "m", tm)
        tbr = P.op(v, lambda e: e.tensor_copy(B.brep[:, :], B.ps[4][:, 0:384]), inc="v")
        B.ps_free[4] = ("v", tbr)
        tmem = P.op(v, lambda e: e.memset(B.qmax2[:, :], 0.0), inc="v")
        P.wait(te, "dw", tdw)
        gjw = 0
        for which, dst in ((0, B.qT_hp), (1, B.kT_hp)):
            for ti in range(NT):
                pb = ti % 2
                pst = B.ps[pb]
                P.waitp(te, B.ps_free[pb])
                for ec in range(NC):
                    tm = P.op(te, lambda e, ec=ec, ti=ti, which=which, pst=pst:
                              e.matmul(pst[:, 0:128],
                                       B.xnT[:, ec * T + ti * 128: ec * T + (ti + 1) * 128],
                                       wsl[:, which * 1024 + ec * 128: which * 1024 + (ec + 1) * 128],
                                       start=(ec == 0), stop=(ec == NC - 1)), inc="m")
                P.wait(v, "m", tm)
                P.wait(v, "v", tbr)
                P.waitp(v, B.qn_free)
                t = P.op(v, lambda e, pst=pst, which=which:
                         e.tensor_tensor(B.qn[:, :], pst[:, 0:128],
                                         B.brep[:, which * 128:(which + 1) * 128], op=ALU.add), inc="v")
                B.ps_free[pb] = ("v", t)
                P.wait(v, "v", t)
                qv = B.qn[:, :].rearrange("p (h d) -> p h d", h=2)
                cosb = B.cos_nat[:, ti * 16:(ti + 1) * 16].unsqueeze(1).to_broadcast([128, 2, 16])
                sinb = B.sin_nat[:, ti * 16:(ti + 1) * 16].unsqueeze(1).to_broadcast([128, 2, 16])
                rs = B.rope_scr[:, :].rearrange("p (k h d) -> p k h d", k=4, h=2)
                P.wait(v, "s", B.rope_ready)
                for k_, a, b_ in ((0, qv[:, :, 32:48], cosb), (1, qv[:, :, 48:64], sinb),
                                  (2, qv[:, :, 32:48], sinb), (3, qv[:, :, 48:64], cosb)):
                    t = P.op(v, lambda e, k_=k_, a=a, b_=b_, rs=rs:
                             e.tensor_tensor(rs[:, k_], a, b_, op=ALU.mult), inc="v")
                P.wait(v, "v", t)
                t = P.op(v, lambda e, qv=qv, rs=rs: e.tensor_tensor(qv[:, :, 32:48], rs[:, 0], rs[:, 1],
                                                                   op=ALU.subtract), inc="v")
                t = P.op(v, lambda e, qv=qv, rs=rs: e.tensor_tensor(qv[:, :, 48:64], rs[:, 2], rs[:, 3],
                                                                   op=ALU.add), inc="v")
                P.wait(v, "v", t)
                P.wait(sc, "v", t)
                P.waitp(sc, gjw)
                for h in range(2):
                    tq = P.op(sc, lambda e, h=h: e.activation(B.gelu_junk[0][:, 0:64],
                                                              B.qn[:, h * 64:(h + 1) * 64], AF.Square,
                                                              accum_out=B.qn2[:, h:h + 1]), inc="s")
                    P.wait(sc, "s", tq)
                P.wait(v, "s", tq)
                tn = P.op(v, lambda e, which=which: e.tensor_tensor(B.qmax2[:, 2 * which:2 * which + 2],
                                                                   B.qmax2[:, 2 * which:2 * which + 2],
                                                                   B.qn2[:, 0:2], op=ALU.max), inc="v")
                gjw = ("v", tn)
                t = P.op(v, lambda e: e.tensor_copy(B.qbf[:, :], B.qn[:, :]), inc="v")
                B.qn_free = ("v", t)
                P.wait(te, "v", t)
                P.waitp(te, B.psbf_free[pb])
                tt = P.op(te, lambda e, pb=pb: e.transpose(B.ps_bf[pb][:, 0:128], B.qbf[:, :],
                                                           B.iden[:, :]), inc="m")
                P.wait(v, "m", tt)
                tv2 = P.op(v, lambda e, pb=pb, ti=ti, dst=dst:
                           e.tensor_copy(dst[:, ti * 128:(ti + 1) * 128], B.ps_bf[pb][:, 0:128]), inc="v")
                B.psbf_free[pb] = ("v", tv2)
                B.qbf_free = ("m", tt)
                P.waitp(v, B.qbf_free)
        tmem2 = P.op(v, lambda e: e.memset(B.Vt_hp[:, :], 1.0), inc="v")
        P.wait(v, "v", tmem2)
        P.wait(te, "v", tmem2)
        for ti in range(NT):
            pb = ti % 2
            pst = B.ps[pb]
            P.waitp(te, B.ps_free[pb])
            for ec in range(NC):
                tm = P.op(te, lambda e, ec=ec, ti=ti, pst=pst:
                          e.matmul(pst[:, 0:128], B.xnT[:, ec * T + ti * 128: ec * T + (ti + 1) * 128],
                                   wsl[:, 2048 + ec * 128: 2048 + (ec + 1) * 128],
                                   start=(ec == 0), stop=(ec == NC - 1)), inc="m")
            P.wait(v, "m", tm)
            tv = P.op(v, lambda e, ti=ti, pst=pst:
                      e.tensor_tensor(B.Vt_hp[:, ti * 130: ti * 130 + 130].rearrange("p (h d) -> p h d", h=2)[:, :, 0:64],
                                      pst[:, 0:128].rearrange("p (h d) -> p h d", h=2),
                                      B.brep[:, 256:384].rearrange("p (h d) -> p h d", h=2), op=ALU.add), inc="v")
            B.ps_free[pb] = ("v", tv)
        barrier(P)

        def mq():
            return P.op(v, lambda e: e.tensor_reduce(B.fold_a[:, 0:1], B.qmax2[:, 0:2], axis=AX.X,
                                                     op=ALU.max), inc="v")
        partition_total(P, B, mq, ALU.max, B.nr_sb[0:1, 4:5], 0)

        def mk():
            return P.op(v, lambda e: e.tensor_reduce(B.fold_a[:, 0:1], B.qmax2[:, 2:4], axis=AX.X,
                                                     op=ALU.max), inc="v")
        partition_total(P, B, mk, ALU.max, B.nr_sb[0:1, 5:6], 0)
        t = P.op(v, lambda e: e.tensor_tensor(B.nr_sb[0:1, 6:7], B.nr_sb[0:1, 4:5],
                                              B.nr_sb[0:1, 5:6], op=ALU.mult), inc="v")
        P.wait(sc, "v", t)
        t = P.op(sc, lambda e: e.activation(B.nr_sb[0:1, 6:7], B.nr_sb[0:1, 6:7], AF.Ln), inc="s")
        P.wait(sc, "s", t)
        t = P.op(sc, lambda e: e.activation(B.nr_sb[0:1, 6:7], B.nr_sb[0:1, 6:7], AF.Exp,
                                            scale=0.5), inc="s")
        P.wait(v, "s", t)
        t = P.op(v, lambda e: e.tensor_scalar(B.nr_sb[0:1, 6:7], B.nr_sb[0:1, 6:7], -0.125, None,
                                              op0=ALU.mult), inc="v")
        P.wait(v, "v", t)
        bcast_col(P, B, B.nr_sb[0:1, 6:7], B.negb[:, 0:1], B.bcb)
        barrier(P)
        exb = B.exbp
        psy = [B.ps[0], B.ps[1]]
        psr = B.ps[2]
        tex = [0, 0]
        tav = [[0, 0], [0, 0]]
        tsc_ = [0, 0]
        tyq = [0, 0]
        free = {"r": 0, "rs": 0, "rrow": 0}

        def emit_av(c, qb):
            sb = c & 1
            for h in range(2):
                if c == 0:
                    P.waitp(te, tyq[h])
                P.wait(te, "s", tex[sb])
                tav[h][sb] = ("m", P.op(te, lambda e, c=c, h=h, sb=sb:
                                        e.matmul(psy[h][0:65, :],
                                                 B.Vt_hp[:, c * 130 + h * 65: c * 130 + h * 65 + 65],
                                                 exb[sb][:, h * 512:(h + 1) * 512],
                                                 start=(c == 0), stop=(c == NT - 1)), inc="m"))
        for qb in range(8):
            for c in range(NT):
                sb = c & 1
                for h in range(2):
                    hb = h * 64
                    P.waitp(te, tex[sb] and ("s", tex[sb]))
                    tsc_[h] = P.op(te, lambda e, c=c, h=h, hb=hb, qb=qb, sb=sb:
                                   e.matmul(B.ps_pair[sb][:, h * 512:(h + 1) * 512],
                                            B.kT_hp[hb:hb + 64, c * 128:(c + 1) * 128],
                                            B.qT_hp[hb:hb + 64, qb * 512:(qb + 1) * 512],
                                            start=True, stop=True, tile_position=(hb, 0)), inc="m")
                P.wait(sc, "m", tsc_[1])
                P.waitp(sc, tav[0][sb])
                P.waitp(sc, tav[1][sb])
                tex[sb] = P.op(sc, lambda e, sb=sb:
                               e.activation(exb[sb][:, :], B.ps_pair[sb][:, :], AF.Exp,
                                            bias=B.negb[:, 0:1], scale=0.125), inc="s")
                if c >= 1:
                    emit_av(c - 1, qb)
            emit_av(NT - 1, qb)
            rrow1 = B.gelu_junk[1][0:1, :]
            rsb1 = B.gelu_junk[0][0:64, :]
            for h in range(2):
                hb = h * 64
                P.wait(v, "m", tav[h][(NT - 1) & 1][1])
                P.waitp(v, free["rrow"])
                td2 = P.op(v, lambda e, h=h, rrow1=rrow1: e.reciprocal(rrow1, psy[h][64:65, :]), inc="v")
                P.wait(te, "v", td2)
                P.waitp(te, free["r"])
                tr = P.op(te, lambda e, rrow1=rrow1: e.matmul(psr[0:64, :], B.ones_row[0:1, 0:64],
                                                              rrow1, start=True, stop=True), inc="m")
                free["rrow"] = ("m", tr)
                P.wait(v, "m", tr)
                P.waitp(v, free["rs"])
                trs = P.op(v, lambda e, rsb1=rsb1: e.tensor_copy(rsb1, psr[0:64, :]), inc="v")
                free["r"] = ("v", trs)
                P.wait(v, "v", trs)
                ty = P.op(v, lambda e, h=h, hb=hb, qb=qb, hp=hp, rsb1=rsb1:
                          e.tensor_tensor(B.yT[hb:hb + 64, hp * T + qb * 512: hp * T + (qb + 1) * 512],
                                          psy[h][0:64, :], rsb1, op=ALU.mult), inc="v")
                tyq[h] = ("v", ty)
                free["rs"] = ("v", ty)
    B.yT_ready = P.now("v")


def build_l1_out(P, B, wo, w1t, vr, ct, bo, b1, x1_dram, out_dram):
    v, sc, te, g, sy = "vector", "scalar", "tensor", "gpsimd", "sync"
    barrier(P)
    P.dma(g, B.vr_bf[:, :].rearrange("p (ec r) -> p ec r", r=32),
          vr.rearrange("(ec p) r -> p ec r", p=128), inc="dw")
    P.dma(g, B.ct_bf[0:32, :], ct[0:32, :], inc="dw")
    tdw = P.now("dw")
    P.wait(te, "dw", tdw)
    for tb in range(8):
        pb = tb % 2
        pst = B.ps[pb]
        P.waitp(te, B.ps_free[pb])
        for ec in range(NC):
            tm = P.op(te, lambda e, ec=ec, tb=tb, pst=pst:
                      e.matmul(pst[0:32, :], B.vr_bf[:, ec * 32:(ec + 1) * 32],
                               B.xnT[:, ec * T + tb * 512: ec * T + (tb + 1) * 512],
                               start=(ec == 0), stop=(ec == NC - 1)), inc="m")
        P.wait(v, "m", tm)
        tv = P.op(v, lambda e, tb=tb, pst=pst: e.tensor_copy(B.xvrT[0:32, tb * 512:(tb + 1) * 512],
                                                             pst[0:32, :]), inc="v")
        B.ps_free[pb] = ("v", tv)
    barrier(P)
    wor = wo.rearrange("(ec p) n -> p ec n", p=128)
    w1r = w1t.rearrange("(ec p) n -> p ec n", p=128)
    halves = (1, 0) if os.environ.get("DEVJH", "") == "1" else (0, 1)
    for half in halves:
        P.waitp(g, B.wstream_free[0])
        P.waitp(g, B.wstream_free[1])
        P.dma(g, B.wstream[0][:, :].rearrange("p (ec n) -> p ec n", n=512),
              wor[:, :, half * 512:(half + 1) * 512], inc="dw")
        P.dma(g, B.wstream[1][:, :].rearrange("p (ec n) -> p ec n", n=512),
              w1r[:, :, half * 512:(half + 1) * 512], inc="dw")
        tdw = P.now("dw")
        P.wait(te, "dw", tdw)
        P.waitp(te, B.ps_free[4])
        P.waitp(g, B.frow_free)
        P.dma(g, B.ffn_row[0:1, 0:512], bo[0:1, half * 512:(half + 1) * 512], inc="dh")
        P.dma(g, B.ffn_row[0:1, 512:1024], b1[0:1, half * 512:(half + 1) * 512], inc="dh")
        tdh = P.now("dh")
        P.wait(te, "dh", tdh)
        tm = P.op(te, lambda e: e.matmul(B.ps[4][:, 0:512], B.ones_row_bf[0:1, :], B.ffn_row[0:1, 0:512],
                                         start=True, stop=False), inc="m")
        tm = P.op(te, lambda e: e.matmul(B.ps[4][:, 0:512], B.ones_row_bf[0:1, :], B.ffn_row[0:1, 512:1024],
                                         start=False, stop=True), inc="m")
        B.frow_free = ("m", tm)
        P.wait(v, "m", tm)
        trep = P.op(v, lambda e: e.tensor_copy(B.ffn_rep[:, 0:512], B.ps[4][:, 0:512]), inc="v")
        B.ps_free[4] = ("v", trep)
        d = "dxj"
        for i in range(NT):
            bi = i % 2
            xt = B.xt[bi]
            P.waitp(sy, B.xtj_free[bi])
            td = P.dma(sy, xt[:, 0:512], x1_dram[ts(i), half * 512:(half + 1) * 512], inc=d)
            pb = 2 + (i % 2)
            pst = B.ps[pb]
            P.waitp(te, B.ps_free[pb])
            DEVJ = os.environ.get("DEVJ", "all")
            first = True
            if DEVJ in ("all", "attn"):
                for cc in range(NC):
                    P.op(te, lambda e, cc=cc, i=i, pst=pst, first=first and cc == 0:
                         e.matmul(pst[:, :], B.yT[:, cc * T + i * 128: cc * T + (i + 1) * 128],
                                  B.wstream[0][:, cc * 512:(cc + 1) * 512],
                                  start=first and cc == 0, stop=False), inc="m")
                first = False
            if DEVJ in ("all", "ffn", "base", "corr0"):
                for cc in range(NC):
                    P.op(te, lambda e, cc=cc, i=i, pst=pst, st=first and cc == 0,
                         sp=(DEVJ == "base" and cc == NC - 1):
                         e.matmul(pst[:, :], B.xnT[:, cc * T + i * 128: cc * T + (i + 1) * 128],
                                  B.wstream[1][:, cc * 512:(cc + 1) * 512],
                                  start=st, stop=sp), inc="m")
                first = False
            if DEVJ != "base":
                tm = P.op(te, lambda e, i=i, pst=pst, st=first, half=half:
                          e.matmul(pst[:, :], B.xvrT[0:32, i * 128:(i + 1) * 128],
                                   B.ct_bf[0:32, half * 512:(half + 1) * 512],
                                   start=st, stop=True), inc="m")
            else:
                tm = P.now("m")
            P.wait(v, "m", tm)
            P.wait(v, d, td)
            P.wait(v, "v", trep)
            P.waitp(v, B.x1sbj_free)
            t1 = P.op(v, lambda e, bi=bi, pst=pst: e.tensor_tensor(B.x1_sb[0][:, 0:512], pst[:, :],
                                                                  B.xt[bi][:, 0:512], op=ALU.add), inc="v")
            P.wait(v, "v", t1)
            t2 = P.op(v, lambda e, bi=bi: e.tensor_tensor(B.x1_sb[0][:, 0:512], B.x1_sb[0][:, 0:512],
                                                          B.ffn_rep[:, 0:512], op=ALU.add), inc="v")
            B.ps_free[pb] = ("v", t2)
            P.wait(sy, "v", t2)
            tso = P.dma(sy, out_dram[ts(i), half * 512:(half + 1) * 512], B.x1_sb[0][:, 0:512], inc="dout")
            B.xtj_free[bi] = ("dout", tso)
            B.x1sbj_free = ("dout", tso)
        tlast = P.now("m")
        B.wstream_free = [("m", tlast), ("m", tlast)]
    P.wait(sy, "dout", P.now("dout"))




class Bufs:
    pass


STAGES = int(os.environ.get('STAGES', '99'))


def build_core_program(debug_taps=()):
    nc = bass.Bass("TRN2", target_bir_lowering=False)
    ctx = ExitStack()
    B = Bufs()

    x = nc.dram_tensor("x", [T, E], F32, kind="ExternalInput").ap()
    out = nc.dram_tensor("out", [T, E], F32, kind="ExternalOutput").ap()

    taps = {}

    with ctx:
        B.xnT = ctx.enter_context(nc.sbuf_tensor("xnT", [128, NC * T], BF16))
        B.xt = [ctx.enter_context(nc.sbuf_tensor(f"xt{k}", [128, E], F32)) for k in range(2)]
        B.xn_bf = ctx.enter_context(nc.sbuf_tensor("xn_bf", [128, E], BF16))
        B.ln_m = ctx.enter_context(nc.sbuf_tensor("ln_m", [128, 1], F32))
        B.ln_sb = ctx.enter_context(nc.sbuf_tensor("ln_sb", [128, 1], F32))
        B.ln_s2 = ctx.enter_context(nc.sbuf_tensor("ln_s2", [128, 1], F32))
        B.ln_msq = ctx.enter_context(nc.sbuf_tensor("ln_msq", [128, 1], F32))
        B.ln_var = ctx.enter_context(nc.sbuf_tensor("ln_var", [128, 1], F32))
        B.ln_u = ctx.enter_context(nc.sbuf_tensor("ln_u", [128, 1], F32))
        B.ln_rstd = ctx.enter_context(nc.sbuf_tensor("ln_rstd", [128, 1], F32))
        B.iota_f = ctx.enter_context(nc.sbuf_tensor("iota_f", [128, 128], F32))
        B.iota_p = ctx.enter_context(nc.sbuf_tensor("iota_p", [128, 1], F32))
        B.iden = ctx.enter_context(nc.sbuf_tensor("iden", [128, 128], BF16))
        B.causal = ctx.enter_context(nc.sbuf_tensor("causal", [128, 128], BF16))
        B.ones_col = ctx.enter_context(nc.sbuf_tensor("ones_col", [128, 1], F32))
        B.eps_col = ctx.enter_context(nc.sbuf_tensor("eps_col", [128, 1], F32))

        B.v_sb = ctx.enter_context(nc.sbuf_tensor("v_sb", [128, 64], F32))
        B.iptbuf_free = [0, 0]
        B.gelu_junk = [ctx.enter_context(nc.sbuf_tensor(f"gjunk{k}", [128, 512], F32)) for k in range(2)]
        B.gj_free = [0, 0]
        B.vcol = [ctx.enter_context(nc.sbuf_tensor(f"vcol{k}", [128, 1], F32)) for k in range(2)]
        B.fold_a = ctx.enter_context(nc.sbuf_tensor("fold_a", [128, 1], F32))
        B.fold_b = ctx.enter_context(nc.sbuf_tensor("fold_b", [64, 1], F32))
        B.fold_s = ctx.enter_context(nc.sbuf_tensor("fold_s", [64, 1], F32))
        B.fold_c = ctx.enter_context(nc.sbuf_tensor("fold_c", [32, 1], F32))
        B.fold_t = ctx.enter_context(nc.sbuf_tensor("fold_t", [32, 32], F32))
        B.fold_tt = ctx.enter_context(nc.sbuf_tensor("fold_tt", [32, 32], F32))
        B.bis = ctx.enter_context(nc.sbuf_tensor("bis", [1, 8], F32))
        B.bis_neg = ctx.enter_context(nc.sbuf_tensor("bis_neg", [128, 64], F32))
        B.bis_ge = ctx.enter_context(nc.sbuf_tensor("bis_ge", [128, 64], F32))
        B.bis_mid = ctx.enter_context(nc.sbuf_tensor("bis_mid", [128, 1], F32))
        B.ones_row = ctx.enter_context(nc.sbuf_tensor("ones_row", [1, 128], F32))
        B.thr_rep = ctx.enter_context(nc.sbuf_tensor("thr_rep", [128, 1], F32))
        B.masked_bf = ctx.enter_context(nc.sbuf_tensor("masked_bf", [128, 64], BF16))
        B.mask_f = ctx.enter_context(nc.sbuf_tensor("mask_f", [128, 64], F32))
        B.wstream = [ctx.enter_context(nc.sbuf_tensor(f"wstr{k}", [128, 4096], BF16)) for k in range(2)]
        B.wstream_free = [0, 0]
        B.ipt_buf = [B.wstream[k][:, 0:1024] for k in range(2)]
        B.pv_buf = [ctx.enter_context(nc.sbuf_tensor(f"pvb{k}", [128, 512], BF16)) for k in range(2)]
        B.pvbuf_free = [0, 0]
        B.rowbuf = [ctx.enter_context(nc.sbuf_tensor(f"rowb{k}", [1, 512], F32)) for k in range(2)]
        B.rowbuf_free = [0, 0]
        B.p_sb = ctx.enter_context(nc.sbuf_tensor("p_sb", [128, 32], F32))
        B.e_sb = ctx.enter_context(nc.sbuf_tensor("e_sb", [128, 32], F32))
        B.e_bf = ctx.enter_context(nc.sbuf_tensor("e_bf", [128, 32], BF16))
        B.negpmax = ctx.enter_context(nc.sbuf_tensor("negpmax", [128, 1], F32))
        B.sc4 = ctx.enter_context(nc.sbuf_tensor("sc4", [1, 4], F32))
        B.agg_sb = ctx.enter_context(nc.sbuf_tensor("agg_sb", [1, 512], BF16))
        B.agg_col_bf = ctx.enter_context(nc.sbuf_tensor("agg_col_bf", [128, 4], BF16))
        B.o_sb = ctx.enter_context(nc.sbuf_tensor("o_sb", [128, 64], F32))
        B.thr2_rep = ctx.enter_context(nc.sbuf_tensor("thr2_rep", [128, 1], F32))
        B.thr3_rep = ctx.enter_context(nc.sbuf_tensor("thr3_rep", [128, 1], F32))
        B.ffn_row = ctx.enter_context(nc.sbuf_tensor("ffn_row", [1, 1024], BF16))
        B.ones_row_bf = ctx.enter_context(nc.sbuf_tensor("ones_row_bf", [1, 128], BF16))
        B.ffn_rep = ctx.enter_context(nc.sbuf_tensor("ffn_rep", [128, 1024], F32))
        B.ones11 = ctx.enter_context(nc.sbuf_tensor("ones11", [1, 1], F32))
        B.ones11_bf = ctx.enter_context(nc.sbuf_tensor("ones11_bf", [1, 1], BF16))
        B.yT = ctx.enter_context(nc.sbuf_tensor("yT", [128, NC * T], BF16))
        B.qT_hp = ctx.enter_context(nc.sbuf_tensor("qT_hp", [128, T], BF16))
        B.kT_hp = ctx.enter_context(nc.sbuf_tensor("kT_hp", [128, T], BF16))
        B.Vt_hp = ctx.enter_context(nc.sbuf_tensor("Vt_hp", [128, NT * 130], BF16))
        B.knat_sb = ctx.enter_context(nc.sbuf_tensor("knat_sb", [128, 128], BF16))
        B.S_sb = ctx.enter_context(nc.sbuf_tensor("S_sb", [128, 65], BF16))
        B.ab_sb = [ctx.enter_context(nc.sbuf_tensor(f"ab{k}", [128, 128], BF16)) for k in range(2)]
        B.den_sb = B.gelu_junk[1][0:1, 0:512]
        B.rrow_sb = B.wstream[0][:, 3072:4096].bitcast(F32)[0:1, 0:512]
        B.r_sb = B.gelu_junk[0][0:64, 0:512]
        B.x1_sb = [ctx.enter_context(nc.sbuf_tensor("x1sb0", [128, E], F32))] * 2
        B.xt_free2 = [0, 0]
        B.xtj_free = [0, 0]
        B.x1sbj_free = 0
        B.frow_free = 0
        B.qn_free = 0
        B.qbf_free = 0
        B.nr_sb = ctx.enter_context(nc.sbuf_tensor("nr_sb", [1, 8], F32))
        B.jrow = ctx.enter_context(nc.sbuf_tensor("jrow", [1, 64], F32))
        B.posrow = [ctx.enter_context(nc.sbuf_tensor(f"posr{k}", [1, 128], F32)) for k in range(2)]
        B.posrow_free = [0, 0]
        B.qn2 = ctx.enter_context(nc.sbuf_tensor("qn2", [128, 2], F32))
        B.qmax2 = ctx.enter_context(nc.sbuf_tensor("qmax2", [128, 4], F32))
        B.negb = ctx.enter_context(nc.sbuf_tensor("negb", [128, 2], F32))
        B.cos_nat = B.wstream[1][:, 0:1024].bitcast(F32)
        B.sin_nat = B.wstream[1][:, 1024:2048].bitcast(F32)
        B.exbp = [B.wstream[1][:, 2048:3072], B.wstream[1][:, 3072:4096]]
        B.qn = B.wstream[0][:, 3840:4096].bitcast(F32)
        B.rope_scr = B.iota_f[:, :]
        B.qbf = B.mask_f[:, :].bitcast(BF16)
        B.brep = B.wstream[0][:, 3072:3840].bitcast(F32)
        B.vr_bf = B.kT_hp[:, 0:256]
        B.ct_bf = B.qT_hp[0:32, 0:1024]
        B.xvrT = B.Vt_hp[0:32, 0:4096]

        B.ps = [ctx.enter_context(nc.psum_tensor(f"ps{k}", [128, 512], F32)) for k in range(4)]
        B.ps_pair = [ctx.enter_context(nc.psum_tensor(f"pspair{k}", [128, 1024], F32)) for k in range(2)]
        B.ps += [B.ps_pair[0][:, 0:512], B.ps_pair[0][:, 512:1024],
                 B.ps_pair[1][:, 0:512], B.ps_pair[1][:, 512:1024]]
        B.ps_free = [0] * 8
        B.bcb = B.ps[4]
        B.ps_bf = [B.ps[2][:, :].bitcast(BF16), B.ps[3][:, :].bitcast(BF16)]
        B.psbf_free = [0, 0]
        B.xt_free = [0, 0]
        B.xnbf_free = 0

        P = Prog(nc, ctx)
        build_constants(P, B)
        P.op("vector", lambda e: e.memset(B.ones_row[:, :], 1.0), inc="v")
        P.wait("tensor", "v", B.const_ready)
        build_ln_xnt(P, B, x, tag="0")
        if STAGES >= 2:
            ipt = nc.dram_tensor("ipt", [E, N_IN], F32, kind="ExternalInput").ap()
            build_acts_mean(P, B, ipt)
            build_bisect(P, B, B.v_sb[:, :], 64, K_IN, B.thr_rep[:, 0:1])
            v = "vector"
            t = P.op(v, lambda e: e.tensor_scalar(B.mask_f[:, :], B.v_sb[:, :], B.thr_rep[:, 0:1],
                                                  None, op0=ALU.is_ge), inc="v")
            P.wait(v, "v", t)
            t = P.op(v, lambda e: e.tensor_tensor(B.masked_bf[:, :], B.v_sb[:, :], B.mask_f[:, :],
                                                  op=ALU.mult), inc="v")
            P.wait(v, "v", t)
            B.maskedv_ready = t
        if STAGES >= 3:
            piwt = nc.dram_tensor("piwt", [N_IN, N_PROC], F32, kind="ExternalInput").ap()
            pv = nc.dram_tensor("pv", [N_PROC, D_PV], F32, kind="ExternalInput").ap()
            oiwt = nc.dram_tensor("oiwt", [D_PV, N_OUT], F32, kind="ExternalInput").ap()
            op_d = nc.dram_tensor("op", [N_OUT, E], F32, kind="ExternalInput").ap()
            build_ffn_chain(P, B, piwt, pv, oiwt, op_d)
        if STAGES >= 4:
            w_attn = nc.dram_tensor("w_attn", [E, 3 * E], F32, kind="ExternalInput").ap()
            build_linattn(P, B, w_attn)
        if STAGES >= 5:
            w_proj = nc.dram_tensor("w_proj", [E, E], F32, kind="ExternalInput").ap()
            x1 = nc.dram_tensor("x1", [T, E], F32, kind="Internal").ap()
            build_wproj_residual(P, B, w_proj, x, x1)
            B.x1_ap = x1
        if STAGES >= 6:
            barrier(P)
            build_ln_xnt(P, B, x1, tag="g1")
        if STAGES >= 7:
            nrope = nc.dram_tensor("nrope", [1, 1], F32, kind="ExternalInput").ap()
            wq = nc.dram_tensor("wq", [E, E], F32, kind="ExternalInput").ap()
            wk = nc.dram_tensor("wk", [E, E], F32, kind="ExternalInput").ap()
            wv = nc.dram_tensor("wv", [E, E], F32, kind="ExternalInput").ap()
            bq = nc.dram_tensor("bq", [1, E], F32, kind="ExternalInput").ap()
            bk = nc.dram_tensor("bk", [1, E], F32, kind="ExternalInput").ap()
            bv = nc.dram_tensor("bv", [1, E], F32, kind="ExternalInput").ap()
            build_rope_tables(P, B, nrope)
            build_l1_attn(P, B, wq, wk, wv, bq, bk, bv)
        if STAGES >= 8:
            wo = nc.dram_tensor("wo", [E, E], F32, kind="ExternalInput").ap()
            w1t = nc.dram_tensor("w1t", [E, E], F32, kind="ExternalInput").ap()
            vr = nc.dram_tensor("vr", [E, 32], F32, kind="ExternalInput").ap()
            ct = nc.dram_tensor("ct", [32, E], F32, kind="ExternalInput").ap()
            bo = nc.dram_tensor("bo", [1, E], F32, kind="ExternalInput").ap()
            b1 = nc.dram_tensor("b1", [1, E], F32, kind="ExternalInput").ap()
            build_l1_out(P, B, wo, w1t, vr, ct, bo, b1, x1, out)

        for name in debug_taps:
            if name == "vmean":
                tp = nc.dram_tensor("tap_vmean", [128, 64], F32, kind="ExternalOutput").ap()
                P.wait("sync", "v", P.now("v"))
                P.dma("sync", tp[:, :], B.v_sb[:, :], inc="do")
                tp2 = nc.dram_tensor("tap_thr", [128, 1], F32, kind="ExternalOutput").ap()
                P.dma("sync", tp2[:, :], B.thr_rep[:, :], inc="do")
                tp3 = nc.dram_tensor("tap_masked", [128, 64], F32, kind="ExternalOutput").ap()
                P.wait("gpsimd", "v", P.now("v"))
                P.dma("gpsimd", tp3[:, :], B.masked_bf[:, :], inc="do")
            if name == "ffn":
                tp = nc.dram_tensor("tap_pacts", [128, 32], F32, kind="ExternalOutput").ap()
                tp2 = nc.dram_tensor("tap_agg", [1, 512], F32, kind="ExternalOutput").ap()
                tp3 = nc.dram_tensor("tap_oacts", [128, 64], F32, kind="ExternalOutput").ap()
                tp4 = nc.dram_tensor("tap_ffnrep", [128, 1024], F32, kind="ExternalOutput").ap()
                P.wait("sync", "v", P.now("v"))
                P.wait("sync", "s", P.now("s"))
                P.dma("sync", tp[:, :], B.p_sb[:, :], inc="do")
                P.dma("sync", tp2[:, :], B.agg_sb[:, :], inc="do")
                P.dma("sync", tp3[:, :], B.o_sb[:, :], inc="do")
                P.dma("sync", tp4[:, :], B.ffn_rep[:, :], inc="do")
            if name == "yT":
                tp = nc.dram_tensor("tap_yT", [128, NC * T], F32, kind="ExternalOutput").ap()
                P.wait("gpsimd", "v", B.yT_ready)
                for c in range(NC):
                    P.dma("gpsimd", tp[:, c * T:(c + 1) * T], B.yT[:, c * T:(c + 1) * T], inc="do")
            if name == "x1":
                tp = nc.dram_tensor("tap_x1", [T, E], F32, kind="ExternalOutput").ap()
                P.wait("sync", "dx1", P.now("dx1"))
                for i in range(8):
                    P.dma("sync", tp[ts(i, 512), :], B.x1_ap[ts(i, 512), :], inc="do")
            if name == "xnT":
                tp = nc.dram_tensor("tap_xnT", [128, NC * T], F32, kind="ExternalOutput").ap()
                P.wait("gpsimd", "v", B.xnt_ready)
                for c in range(NC):
                    P.dma("gpsimd", tp[:, c * T:(c + 1) * T], B.xnT[:, c * T:(c + 1) * T], inc="do")
                taps[name] = "tap_xnT"

        if STAGES < 8:
            P.wait("sync", "dx0", B.x_stream_done)
            for i in range(NT):
                P.dma("sync", out[ts(i), :], x[ts(i), :], inc="do")
        P.wait("sync", "do", P.now("do"))
        P.finalize()
    return nc, taps


def _layout_inputs(inputs):
    w = {
        "w_attn": inputs["L0_attn_w_attn"],
        "w_proj": inputs["L0_attn_w_proj"],
        "ipt": np.ascontiguousarray(inputs["L0_ffn_input_patterns"].T),
        "piwt": np.ascontiguousarray(inputs["L0_ffn_process_input_weights"].T),
        "pv": inputs["L0_ffn_process_values"],
        "oiwt": np.ascontiguousarray(inputs["L0_ffn_output_input_weights"].T),
        "op": inputs["L0_ffn_output_patterns"],
        "wq": inputs["L1_attn_Wq"], "wk": inputs["L1_attn_Wk"],
        "wv": inputs["L1_attn_Wv"], "wo": inputs["L1_attn_Wo"],
        "bq": inputs["L1_attn_bq"].reshape(1, E), "bk": inputs["L1_attn_bk"].reshape(1, E),
        "bv": inputs["L1_attn_bv"].reshape(1, E), "bo": inputs["L1_attn_bo"].reshape(1, E),
        "nrope": inputs["L1_attn_nrope_freq"].reshape(1, 1),
        "w1t": np.ascontiguousarray(inputs["L1_ffn_W"].T),
        "b1": inputs["L1_ffn_b"].reshape(1, E),
        "vr": inputs["L1_ffn_V_r"],
        "ct": np.ascontiguousarray(inputs["L1_ffn_C"].T),
    }
    w = {k: np.ascontiguousarray(v, dtype=np.float32) for k, v in w.items()}
    xs = inputs["L0_attn_x"]
    return [dict(w, x=np.ascontiguousarray(xs[b])) for b in range(8)]


def kernel(**inputs):
    in_maps = _layout_inputs(inputs)
    nc, _ = build_core_program()
    res = run_bass_kernel_spmd(nc, in_maps, core_ids=list(range(8)))
    return np.stack([res.results[b]["out"] for b in range(8)], axis=0)


if __name__ == "__main__":
    pass


# version 08
# speedup vs baseline: 1.1710x; 1.1710x over previous
import os
import numpy as np
from contextlib import ExitStack

import concourse.bass as bass
import concourse.mybir as mybir
from concourse.bass_utils import run_bass_kernel_spmd

F32 = mybir.dt.float32
BF16 = mybir.dt.bfloat16
U8 = mybir.dt.uint8
AF = mybir.ActivationFunctionType
ALU = mybir.AluOpType
AX = mybir.AxisListType

T = 4096
E = 1024
NT = T // 128
NC = E // 128
NH = 16
HD = 64
LN_EPS = 1e-5
LA_EPS = 1e-6
PI = float(np.pi)

N_IN = 8192
N_PROC = 4096
N_OUT = 8192
D_PV = 512
K_IN = 1024
K_PROC = 512
K_OUT = 1024
BISECT_ITERS = 42

ENGINES = ("tensor", "vector", "scalar", "gpsimd", "sync")


class Prog:

    def __init__(self, nc, ctx):
        self.nc = nc
        self.ctx = ctx
        self.q = {e: [] for e in ENGINES}
        self.sems = {}
        self.cnt = {}

    def sem(self, name):
        if name not in self.sems:
            self.sems[name] = self.ctx.enter_context(self.nc.semaphore(name))
            self.cnt[name] = 0
        return self.sems[name]

    def op(self, eng, fn, inc=None, dma=False):
        incs = []
        tgt = None
        if inc is not None:
            for name in ([inc] if isinstance(inc, str) else inc):
                step = 16 if dma else 1
                s = self.sem(name)
                self.cnt[name] += step
                incs.append((s, step))
                if tgt is None:
                    tgt = self.cnt[name]

        def run(engine, fn=fn, incs=incs):
            instr = fn(engine)
            for s, step in incs:
                instr = instr.then_inc(s, step)
        self.q[eng].append(run)
        return tgt

    def dma(self, eng, out, in_, inc, **kw):
        return self.op(eng, lambda e: e.dma_start(out, in_, **kw), inc=inc, dma=True)

    def wait(self, eng, sem_name, tgt):
        s = self.sem(sem_name)
        self.q[eng].append(lambda engine: engine.wait_ge(s, tgt))

    def waitp(self, eng, pair):
        if pair:
            self.wait(eng, pair[0], pair[1])

    def now(self, name):
        self.sem(name)
        return self.cnt[name]

    def finalize(self):
        blk = self.ctx.enter_context(self.nc.Block())
        for eng in ENGINES:
            ops = self.q[eng]

            def make(ops):
                def body(engine):
                    for f in ops:
                        f(engine)
                return body
            getattr(blk, eng)(make(ops))


def ts(i, n=128):
    return slice(i * n, (i + 1) * n)



def build_constants(P, B):
    g = "gpsimd"
    P.op(g, lambda e: e.iota(B.iota_f[:, :], [[1, 128]], channel_multiplier=0,
                             allow_small_or_imprecise_dtypes=True), inc="g")
    P.op(g, lambda e: e.iota(B.iota_p[:, :], [[1, 1]], channel_multiplier=1,
                             allow_small_or_imprecise_dtypes=True), inc="g")
    P.op(g, lambda e: e.memset(B.ones_col[:, :], 1.0), inc="g")
    P.op(g, lambda e: e.memset(B.ones11[:, :], 1.0), inc="g")
    P.op(g, lambda e: e.memset(B.ones_row_bf[:, :], 1.0), inc="g")
    P.op(g, lambda e: e.memset(B.ones11_bf[:, :], 1.0), inc="g")
    tg = P.op(g, lambda e: e.memset(B.eps_col[:, :], LN_EPS), inc="g")
    P.wait("vector", "g", tg)
    P.op("vector", lambda e: e.tensor_scalar(B.iden[:, :], B.iota_f[:, :], B.iota_p[:, 0:1],
                                             None, op0=ALU.is_equal), inc="v")
    tv = P.op("vector", lambda e: e.tensor_scalar(B.causal[:, :], B.iota_f[:, :], B.iota_p[:, 0:1],
                                                  None, op0=ALU.is_ge), inc="v")
    B.const_ready = tv


def build_ln_xnt(P, B, x_dram, tag):
    d = f"dx{tag}"
    v = "vector"
    sc = "scalar"
    for i in range(NT):
        bi = i % 2
        xt = B.xt[bi]
        if i >= 2:
            P.wait("sync", "v", B.xt_free[bi])
        td = P.dma("sync", xt[:, :], x_dram[ts(i), :], inc=d)
        P.wait(v, d, td)
        P.op(v, lambda e, xt=xt: e.reduce_sum(B.ln_m[:, 0:1], xt[:, :], axis=AX.X), inc="v")
        P.wait(sc, d, td)
        P.waitp(sc, B.gj_free[0])
        P.op(sc, lambda e, xt=xt: e.activation(B.gelu_junk[0][:, :], xt[:, 0:512], AF.Square,
                                               accum_out=B.ln_s2[:, 0:1]), inc="s")
        tsq = P.now("s")
        P.wait(sc, "s", tsq)
        P.waitp(sc, B.gj_free[1])
        tsq = P.op(sc, lambda e, xt=xt: e.activation(B.gelu_junk[1][:, :], xt[:, 512:1024], AF.Square,
                                                     accum_out=B.ln_sb[:, 0:1]), inc="s")
        B.gj_free = [("s", tsq), ("s", tsq)]
        tv = P.now("v")
        P.wait(v, "v", tv)
        P.op(v, lambda e: e.tensor_scalar(B.ln_m[:, 0:1], B.ln_m[:, 0:1], 1.0 / E, None, op0=ALU.mult), inc="v")
        tv = P.now("v")
        P.wait(v, "v", tv)
        P.op(v, lambda e: e.tensor_tensor(B.ln_msq[:, 0:1], B.ln_m[:, 0:1], B.ln_m[:, 0:1], op=ALU.mult), inc="v")
        P.wait(v, "s", tsq)
        tv = P.now("v")
        P.wait(v, "v", tv)
        P.op(v, lambda e: e.tensor_tensor(B.ln_s2[:, 0:1], B.ln_s2[:, 0:1], B.ln_sb[:, 0:1], op=ALU.add), inc="v")
        tv = P.now("v")
        P.wait(v, "v", tv)
        tv = P.op(v, lambda e: e.scalar_tensor_tensor(B.ln_var[:, 0:1], B.ln_s2[:, 0:1], 1.0 / E,
                                                      B.ln_msq[:, 0:1], op0=ALU.mult, op1=ALU.subtract), inc="v")
        P.wait(sc, "v", tv)
        P.op(sc, lambda e: e.activation(B.ln_u[:, 0:1], B.ln_var[:, 0:1], AF.Ln, bias=B.eps_col[:, 0:1]), inc="s")
        tssc = P.now("s")
        P.wait(sc, "s", tssc)
        tssc = P.op(sc, lambda e: e.activation(B.ln_rstd[:, 0:1], B.ln_u[:, 0:1], AF.Exp, scale=-0.5), inc="s")
        P.wait(v, "s", tssc)
        P.wait(v, "m", B.xnbf_free)
        tv = P.op(v, lambda e, xt=xt: e.tensor_scalar(B.xn_bf[:, :], xt[:, :], B.ln_m[:, 0:1],
                                                      B.ln_rstd[:, 0:1], op0=ALU.subtract, op1=ALU.mult), inc="v")
        B.xt_free[bi] = tv
        P.wait("tensor", "v", tv)
        for c in range(NC):
            pb = B.ps_bf[c % 2]
            P.waitp("tensor", B.psbf_free[c % 2])
            tm = P.op("tensor", lambda e, c=c, pb=pb: e.transpose(pb[:, 0:128], B.xn_bf[:, ts(c)], B.iden[:, :]), inc="m")
            P.wait(v, "m", tm)
            tv = P.op(v, lambda e, c=c, i=i, pb=pb: e.tensor_copy(B.xnT[:, c * T + 128 * i: c * T + 128 * (i + 1)],
                                                                  pb[:, 0:128]), inc="v")
            B.psbf_free[c % 2] = ("v", tv)
        B.xnbf_free = tm
    B.xnt_ready = P.now("v")
    B.x_stream_done = P.now(d)



def build_acts_mean(P, B, ipt_dram):
    v, sc, te, g = "vector", "scalar", "tensor", "gpsimd"
    tv = P.op(v, lambda e: e.memset(B.v_sb[:, :], 0.0), inc="v")
    ipt_r = ipt_dram.rearrange("(dc p) n -> p dc n", p=128)
    vcol_free = [0, 0]
    for j in range(64):
        bj = j % 2
        ib = B.ipt_buf[bj]
        P.waitp(g, B.iptbuf_free[bj])
        td = P.dma(g, ib.rearrange("p (dc n) -> p dc n", n=128), ipt_r[:, :, ts(j)], inc="dw")
        P.wait(te, "dw", td)
        P.wait(te, "v", tv)
        last_mm = 0
        for tb in range(8):
            pb = (tb % 2)
            pst = B.ps[pb]
            P.waitp(te, B.ps_free[pb])
            for dc in range(NC):
                last_mm = P.op(te, lambda e, dc=dc, tb=tb, ib=ib, pst=pst:
                               e.matmul(pst[:, :], ib[:, ts(dc)],
                                        B.xnT[:, dc * T + tb * 512: dc * T + (tb + 1) * 512],
                                        start=(dc == 0), stop=(dc == NC - 1)), inc="m")
            B.iptbuf_free[bj] = ("m", last_mm)
            P.wait(sc, "m", last_mm)
            P.waitp(sc, vcol_free[pb])
            tsc = P.op(sc, lambda e, pb=pb, pst=pst:
                       e.activation(B.gelu_junk[pb][:, :], pst[:, :], AF.Gelu,
                                    accum_out=B.vcol[pb][:, 0:1]), inc="s")
            B.ps_free[pb] = ("s", tsc)
            P.wait(v, "s", tsc)
            tv2 = P.op(v, lambda e, j=j, pb=pb:
                       e.tensor_tensor(B.v_sb[:, j:j + 1], B.v_sb[:, j:j + 1],
                                       B.vcol[pb][:, 0:1], op=ALU.add), inc="v")
            vcol_free[pb] = ("v", tv2)
            P.wait(v, "v", tv2)
    tv = P.op(v, lambda e: e.tensor_scalar(B.v_sb[:, :], B.v_sb[:, :], 1.0 / T, None,
                                           op0=ALU.mult), inc="v")
    B.vsb_ready = tv


def partition_total(P, B, src_col_fn, red_op, out11, nwait):
    v = "vector"
    t0 = src_col_fn()
    P.wait(v, "v", t0)
    t1 = P.op(v, lambda e: e.tensor_copy(B.fold_s[0:64, 0:1], B.fold_a[64:128, 0:1]), inc="v")
    P.wait(v, "v", t1)
    t1 = P.op(v, lambda e: e.tensor_tensor(B.fold_b[0:64, 0:1], B.fold_a[0:64, 0:1],
                                           B.fold_s[0:64, 0:1], op=red_op), inc="v")
    P.wait(v, "v", t1)
    t2 = P.op(v, lambda e: e.tensor_copy(B.fold_s[0:32, 0:1], B.fold_b[32:64, 0:1]), inc="v")
    P.wait(v, "v", t2)
    t2 = P.op(v, lambda e: e.tensor_tensor(B.fold_c[0:32, 0:1], B.fold_b[0:32, 0:1],
                                           B.fold_s[0:32, 0:1], op=red_op), inc="v")
    P.wait(v, "v", t2)
    t3 = P.op(v, lambda e: e.tensor_copy(B.fold_t[0:32, 0:1], B.fold_c[0:32, 0:1]), inc="v")
    P.wait(v, "v", t3)
    t4 = P.op(v, lambda e: e.transpose(B.fold_tt[0:32, 0:32], B.fold_t[0:32, 0:32]), inc="v")
    P.wait(v, "v", t4)
    t5 = P.op(v, lambda e: e.tensor_reduce(out11, B.fold_tt[0:1, 0:32], axis=AX.X, op=red_op),
              inc="v")
    P.wait(v, "v", t5)
    return t5


def bcast_col(P, B, src11, dst_col, psb, n_rows=128):
    te, v = "tensor", "vector"
    P.wait(te, "v", P.now("v"))
    tm = P.op(te, lambda e: e.matmul(psb[0:n_rows, 0:1], B.ones_row[0:1, 0:n_rows],
                                     src11, start=True, stop=True), inc="m")
    P.wait(v, "m", tm)
    tv = P.op(v, lambda e: e.tensor_copy(dst_col, psb[0:n_rows, 0:1]), inc="v")
    P.wait(v, "v", tv)
    return tv


def build_bisect(P, B, src, F, k, thr_rep):
    v = "vector"
    psb = B.bcb
    def mx():
        return P.op(v, lambda e: e.tensor_reduce(B.fold_a[:, 0:1], src, axis=AX.X, op=ALU.max), inc="v")
    partition_total(P, B, mx, ALU.max, B.bis[0:1, 1:2], 0)

    def mn():
        t0 = P.op(v, lambda e: e.tensor_scalar(B.bis_neg[:, 0:F], src, -1.0, None, op0=ALU.mult), inc="v")
        P.wait(v, "v", t0)
        return P.op(v, lambda e: e.tensor_reduce(B.fold_a[:, 0:1], B.bis_neg[:, 0:F], axis=AX.X, op=ALU.max), inc="v")
    partition_total(P, B, mn, ALU.max, B.bis[0:1, 2:3], 0)
    t = P.op(v, lambda e: e.tensor_scalar(B.bis[0:1, 0:1], B.bis[0:1, 2:3], -1.0, -1.0,
                                          op0=ALU.mult, op1=ALU.add), inc="v")
    P.wait(v, "v", t)
    t = P.op(v, lambda e: e.tensor_scalar(B.bis[0:1, 1:2], B.bis[0:1, 1:2], 1.0, None,
                                          op0=ALU.add), inc="v")
    P.wait(v, "v", t)
    for it in range(BISECT_ITERS):
        t = P.op(v, lambda e: e.tensor_tensor(B.bis[0:1, 3:4], B.bis[0:1, 1:2], B.bis[0:1, 0:1],
                                              op=ALU.subtract), inc="v")
        P.wait(v, "v", t)
        t = P.op(v, lambda e: e.tensor_scalar(B.bis[0:1, 3:4], B.bis[0:1, 3:4], 0.5, None,
                                              op0=ALU.mult), inc="v")
        P.wait(v, "v", t)
        t = P.op(v, lambda e: e.tensor_tensor(B.bis[0:1, 4:5], B.bis[0:1, 0:1], B.bis[0:1, 3:4],
                                              op=ALU.add), inc="v")
        P.wait(v, "v", t)
        bcast_col(P, B, B.bis[0:1, 4:5], B.bis_mid[:, 0:1], psb)
        t = P.op(v, lambda e: e.tensor_scalar(B.bis_ge[:, 0:F], src, B.bis_mid[:, 0:1], None,
                                              op0=ALU.is_ge), inc="v")
        P.wait(v, "v", t)

        def cnt():
            return P.op(v, lambda e: e.tensor_reduce(B.fold_a[:, 0:1], B.bis_ge[:, 0:F],
                                                     axis=AX.X, op=ALU.add), inc="v")
        partition_total(P, B, cnt, ALU.add, B.bis[0:1, 5:6], 0)
        t = P.op(v, lambda e: e.tensor_scalar(B.bis[0:1, 6:7], B.bis[0:1, 5:6], k - 0.5, None,
                                              op0=ALU.is_ge), inc="v")
        P.wait(v, "v", t)
        t = P.op(v, lambda e: e.tensor_tensor(B.bis[0:1, 7:8], B.bis[0:1, 6:7], B.bis[0:1, 3:4],
                                              op=ALU.mult), inc="v")
        P.wait(v, "v", t)
        t = P.op(v, lambda e: e.tensor_tensor(B.bis[0:1, 0:1], B.bis[0:1, 0:1], B.bis[0:1, 7:8],
                                              op=ALU.add), inc="v")
        P.wait(v, "v", t)
        t = P.op(v, lambda e: e.tensor_tensor(B.bis[0:1, 1:2], B.bis[0:1, 4:5], B.bis[0:1, 7:8],
                                              op=ALU.add), inc="v")
        P.wait(v, "v", t)
    return bcast_col(P, B, B.bis[0:1, 0:1], thr_rep, psb)



def build_ffn_chain(P, B, piwt, pv, oiwt, op_dram):
    v, sc, te, g = "vector", "scalar", "tensor", "gpsimd"

    def matvec_stream(lhs_cols, w_dram, n_chunks, n_pb, tile_cols, accum_rows=1):
        for c in range(n_chunks):
            wb = B.wstream[c % 2]
            P.waitp(g, B.wstream_free[c % 2])
            td = P.dma(g, wb[:, 0:tile_cols], w_dram[ts(c), 0:tile_cols], inc="dw")
            P.wait(te, "dw", td)
            last = 0
            for pb in range(n_pb):
                if c == 0:
                    P.waitp(te, B.ps_free[pb])
                last = P.op(te, lambda e, c=c, pb=pb, wb=wb:
                            e.matmul(B.ps[pb][0:1, 0:512], lhs_cols[:, c:c + 1],
                                     wb[:, pb * 512:(pb + 1) * 512],
                                     start=(c == 0), stop=(c == n_chunks - 1)), inc="m")
            B.wstream_free[c % 2] = ("m", last)
        return P.now("m")

    P.wait(te, "v", B.maskedv_ready)
    tmm = matvec_stream(B.masked_bf, piwt, 64, 8, 4096)
    for pb in range(8):
        rb = B.rowbuf[pb % 2]
        P.wait(sc, "m", tmm)
        P.waitp(sc, B.rowbuf_free[pb % 2])
        tsc = P.op(sc, lambda e, pb=pb, rb=rb: e.activation(rb[0:1, :],
                                                            B.ps[pb][0:1, 0:512], AF.Gelu), inc="s")
        B.ps_free[pb] = ("s", tsc)
        P.wait(te, "s", tsc)
        if pb == 0:
            P.waitp(te, B.ps_free[0])
        for j in range(4):
            tm = P.op(te, lambda e, pb=pb, j=j, rb=rb:
                      e.matmul(B.ps[0][:, pb * 4 + j: pb * 4 + j + 1], rb[0:1, ts(j)],
                               B.ones11[0:1, 0:1], start=True, stop=True), inc="m")
        B.rowbuf_free[pb % 2] = ("m", tm)
    P.wait(v, "m", tm)
    tv = P.op(v, lambda e: e.tensor_copy(B.p_sb[:, :], B.ps[0][:, 0:32]), inc="v")
    B.ps_free[0] = ("v", tv)
    P.wait(v, "v", tv)

    build_bisect(P, B, B.p_sb[:, :], 32, K_PROC, B.thr2_rep[:, 0:1])
    def mx():
        return P.op(v, lambda e: e.tensor_reduce(B.fold_a[:, 0:1], B.p_sb[:, :], axis=AX.X,
                                                 op=ALU.max), inc="v")
    partition_total(P, B, mx, ALU.max, B.sc4[0:1, 0:1], 0)
    t = P.op(v, lambda e: e.tensor_scalar(B.sc4[0:1, 1:2], B.sc4[0:1, 0:1], -1.0, None,
                                          op0=ALU.mult), inc="v")
    P.wait(v, "v", t)
    bcast_col(P, B, B.sc4[0:1, 1:2], B.negpmax[:, 0:1], B.bcb)
    P.wait(sc, "v", P.now("v"))
    t = P.op(sc, lambda e: e.activation(B.e_sb[:, :], B.p_sb[:, :], AF.Exp,
                                        bias=B.negpmax[:, 0:1]), inc="s")
    P.wait(v, "s", t)
    t = P.op(v, lambda e: e.tensor_scalar(B.mask_f[:, 0:32], B.p_sb[:, :], B.thr2_rep[:, 0:1],
                                          None, op0=ALU.is_ge), inc="v")
    P.wait(v, "v", t)
    t = P.op(v, lambda e: e.tensor_tensor(B.e_sb[:, :], B.e_sb[:, :], B.mask_f[:, 0:32],
                                          op=ALU.mult), inc="v")
    P.wait(v, "v", t)
    t = P.op(v, lambda e: e.tensor_copy(B.e_bf[:, :], B.e_sb[:, :]), inc="v")
    P.wait(v, "v", t)
    def sm():
        return P.op(v, lambda e: e.tensor_reduce(B.fold_a[:, 0:1], B.e_sb[:, :], axis=AX.X,
                                                 op=ALU.add), inc="v")
    partition_total(P, B, sm, ALU.add, B.sc4[0:1, 2:3], 0)
    t = P.op(v, lambda e: e.reciprocal(B.sc4[0:1, 3:4], B.sc4[0:1, 2:3]), inc="v")
    P.wait(v, "v", t)

    P.wait(te, "v", P.now("v"))
    for c in range(32):
        wb = B.pv_buf[c % 2]
        P.waitp(g, B.pvbuf_free[c % 2])
        td = P.dma(g, wb[:, :], pv[ts(c), :], inc="dw")
        P.wait(te, "dw", td)
        if c == 0:
            P.waitp(te, B.ps_free[1])
        tm = P.op(te, lambda e, c=c, wb=wb: e.matmul(B.ps[1][0:1, 0:512], B.e_bf[:, c:c + 1],
                                                     wb[:, :], start=(c == 0), stop=(c == 31)), inc="m")
        B.pvbuf_free[c % 2] = ("m", tm)
    P.wait(v, "m", tm)
    tv = P.op(v, lambda e: e.tensor_scalar(B.agg_sb[0:1, :], B.ps[1][0:1, 0:512],
                                           B.sc4[0:1, 3:4], None, op0=ALU.mult), inc="v")
    B.ps_free[1] = ("v", tv)
    P.wait(v, "v", tv)
    P.wait(te, "v", tv)
    P.waitp(te, B.ps_free[2])
    for j in range(4):
        tm = P.op(te, lambda e, j=j: e.matmul(B.ps[2][:, j:j + 1], B.agg_sb[0:1, ts(j)],
                                              B.ones11_bf[0:1, 0:1], start=True, stop=True), inc="m")
    P.wait(v, "m", tm)
    tv = P.op(v, lambda e: e.tensor_copy(B.agg_col_bf[:, :], B.ps[2][:, 0:4]), inc="v")
    B.ps_free[2] = ("v", tv)
    P.wait(v, "v", tv)

    oiwt_r = oiwt.rearrange("(dc p) n -> p dc n", p=128)
    for half in range(2):
        for dc in range(4):
            wb = B.wstream[dc % 2]
            P.waitp(g, B.wstream_free[dc % 2])
            td = P.dma(g, wb[:, 0:4096], oiwt_r[:, dc, half * 4096:(half + 1) * 4096], inc="dw")
            P.wait(te, "dw", td)
            last = 0
            for pb in range(8):
                if dc == 0:
                    P.waitp(te, B.ps_free[pb])
                last = P.op(te, lambda e, dc=dc, pb=pb, wb=wb:
                            e.matmul(B.ps[pb][0:1, 0:512], B.agg_col_bf[:, dc:dc + 1],
                                     wb[:, pb * 512:(pb + 1) * 512],
                                     start=(dc == 0), stop=(dc == 3)), inc="m")
            B.wstream_free[dc % 2] = ("m", last)
        for pb in range(8):
            rb = B.rowbuf[pb % 2]
            P.wait(sc, "m", last)
            P.waitp(sc, B.rowbuf_free[pb % 2])
            tsc = P.op(sc, lambda e, pb=pb, rb=rb:
                       e.activation(rb[0:1, :], B.ps[pb][0:1, 0:512], AF.Gelu), inc="s")
            B.ps_free[pb] = ("s", tsc)
            P.wait(te, "s", tsc)
            if pb == 0:
                P.waitp(te, B.ps_free[0])
            for j in range(4):
                tm = P.op(te, lambda e, pb=pb, j=j, rb=rb:
                          e.matmul(B.ps[0][:, pb * 4 + j: pb * 4 + j + 1], rb[0:1, ts(j)],
                                   B.ones11[0:1, 0:1], start=True, stop=True), inc="m")
            B.rowbuf_free[pb % 2] = ("m", tm)
        P.wait(v, "m", tm)
        tv = P.op(v, lambda e, half=half: e.tensor_copy(B.o_sb[:, half * 32:(half + 1) * 32],
                                                        B.ps[0][:, 0:32]), inc="v")
        B.ps_free[0] = ("v", tv)
        P.wait(v, "v", tv)

    build_bisect(P, B, B.o_sb[:, :], 64, K_OUT, B.thr3_rep[:, 0:1])
    t = P.op(v, lambda e: e.tensor_scalar(B.mask_f[:, :], B.o_sb[:, :], B.thr3_rep[:, 0:1],
                                          None, op0=ALU.is_ge), inc="v")
    P.wait(v, "v", t)
    t = P.op(v, lambda e: e.tensor_tensor(B.masked_bf[:, :], B.o_sb[:, :], B.mask_f[:, :],
                                          op=ALU.mult), inc="v")
    P.wait(v, "v", t)
    P.wait(te, "v", t)
    for c in range(64):
        wb = B.wstream[c % 2]
        P.waitp(g, B.wstream_free[c % 2])
        td = P.dma(g, wb[:, 0:1024], op_dram[ts(c), :], inc="dw")
        P.wait(te, "dw", td)
        last = 0
        for pb in range(2):
            if c == 0:
                P.waitp(te, B.ps_free[pb])
            last = P.op(te, lambda e, c=c, pb=pb, wb=wb:
                        e.matmul(B.ps[pb][0:1, 0:512], B.masked_bf[:, c:c + 1],
                                 wb[:, pb * 512:(pb + 1) * 512],
                                 start=(c == 0), stop=(c == 63)), inc="m")
        B.wstream_free[c % 2] = ("m", last)
    P.wait(v, "m", last)
    tv = P.op(v, lambda e: e.tensor_copy(B.ffn_row[0:1, 0:512], B.ps[0][0:1, 0:512]), inc="v")
    B.ps_free[0] = ("v", tv)
    tv = P.op(v, lambda e: e.tensor_copy(B.ffn_row[0:1, 512:1024], B.ps[1][0:1, 0:512]), inc="v")
    B.ps_free[1] = ("v", tv)
    P.wait(v, "v", tv)
    P.wait(te, "v", tv)
    P.waitp(te, B.ps_free[2])
    P.waitp(te, B.ps_free[3])
    tm2 = []
    for j in range(2):
        tm = P.op(te, lambda e, j=j: e.matmul(B.ps[2 + j][:, 0:512], B.ones_row_bf[0:1, :],
                                              B.ffn_row[0:1, j * 512:(j + 1) * 512],
                                              start=True, stop=True), inc="m")
        tm2.append(tm)
    P.wait(v, "m", tm)
    for j in range(2):
        tv = P.op(v, lambda e, j=j: e.tensor_copy(B.ffn_rep[:, j * 512:(j + 1) * 512],
                                                  B.ps[2 + j][:, 0:512]), inc="v")
        B.ps_free[2 + j] = ("v", tv)
    P.wait(v, "v", tv)
    B.ffn_ready = tv



def barrier(P):
    tgts = [(n, P.now(n)) for n in ("m", "v", "s", "g")]
    for eng in ENGINES:
        for n, t in tgts:
            if t:
                P.wait(eng, n, t)


def build_linattn(P, B, w_attn):
    v, sc, te, g = "vector", "scalar", "tensor", "gpsimd"
    wr = w_attn.rearrange("(ec p) n -> p ec n", p=128)
    for hp in range(int(os.environ.get("DEVHP", "8"))):
        barrier(P)
        wsl = B.wstream[0]
        for j, base in enumerate((0, E, 2 * E)):
            P.dma(g, wsl[:, j * 1024:(j + 1) * 1024].rearrange("p (ec n) -> p ec n", n=128),
                  wr[:, :, base + hp * 128: base + (hp + 1) * 128], inc="dw")
        tdw = P.now("dw")
        P.wait(te, "dw", tdw)
        gj_war = [0, 0]
        for which, dst in ((0, B.qT_hp), (1, B.kT_hp)):
            for tb in range(8):
                pb = tb % 2
                pst = B.ps[pb]
                P.waitp(te, B.ps_free[pb])
                for ec in range(NC):
                    tm = P.op(te, lambda e, ec=ec, tb=tb, which=which, pst=pst:
                              e.matmul(pst[:, :], wsl[:, which * 1024 + ec * 128: which * 1024 + (ec + 1) * 128],
                                       B.xnT[:, ec * T + tb * 512: ec * T + (tb + 1) * 512],
                                       start=(ec == 0), stop=(ec == NC - 1)), inc="m")
                P.wait(v, "m", tm)
                P.waitp(v, gj_war[pb])
                t1 = P.op(v, lambda e, pb=pb, pst=pst: e.tensor_scalar(B.gelu_junk[pb][:, :], pst[:, :], 0.0,
                                                                      None, op0=ALU.min), inc="v")
                P.wait(sc, "v", t1)
                t2 = P.op(sc, lambda e, pb=pb: e.activation(B.gelu_junk[pb][:, :], B.gelu_junk[pb][:, :],
                                                            AF.Exp), inc="s")
                P.wait(v, "s", t2)
                t3 = P.op(v, lambda e, pb=pb, pst=pst, dst=dst, tb=tb:
                          e.scalar_tensor_tensor(dst[:, tb * 512:(tb + 1) * 512], pst[:, :], 0.0,
                                                 B.gelu_junk[pb][:, :], op0=ALU.max, op1=ALU.add), inc="v")
                B.ps_free[pb] = ("v", t3)
                gj_war[pb] = ("v", t3)
        tmem = P.op(v, lambda e: e.memset(B.Vt_hp[:, :], 1.0), inc="v")
        P.wait(v, "v", tmem)
        P.wait(te, "v", tmem)
        for ti in range(NT):
            pb = ti % 2
            pst = B.ps[pb]
            P.waitp(te, B.ps_free[pb])
            for ec in range(NC):
                tm = P.op(te, lambda e, ec=ec, ti=ti, pst=pst:
                          e.matmul(pst[:, 0:128], B.xnT[:, ec * T + ti * 128: ec * T + (ti + 1) * 128],
                                   wsl[:, 2048 + ec * 128: 2048 + (ec + 1) * 128],
                                   start=(ec == 0), stop=(ec == NC - 1)), inc="m")
            P.wait(v, "m", tm)
            tv = P.op(v, lambda e, ti=ti, pst=pst:
                      e.tensor_copy(B.Vt_hp[:, ti * 130: ti * 130 + 130].rearrange("p (h d) -> p h d", h=2)[:, :, 0:64],
                                    pst[:, 0:128].rearrange("p (h d) -> p h d", h=2)), inc="v")
            B.ps_free[pb] = ("v", tv)
        barrier(P)
        psS = [B.ps[6], B.ps[7]]
        psc = [B.ps[4], B.ps[5]]
        psy = [B.ps[0], B.ps[1]]
        psr = B.ps[2]
        psk = B.ps_bf[1]
        knat = B.exbp[0][:, 0:512]
        ab = [B.exbp[0][:, 512:1024], B.exbp[1][:, 0:512]]
        free = {"S0": 0, "S1": 0, "y0": 0, "y1": 0, "r": 0,
                "knat": 0, "ab0": 0, "ab1": 0, "Ssb0": 0, "Ssb1": 0, "rs": 0, "den": 0,
                "abm0": 0, "abm1": 0, "knsb": 0, "rrow": 0, "sc0": 0, "sc1": 0,
                "y0m": 0, "y1m": 0}
        for g_ in range(8):
            P.waitp(te, free["knat"])
            for j_ in range(4):
                tkn = P.op(te, lambda e, g_=g_, j_=j_:
                           e.transpose(psk[:, j_ * 128:(j_ + 1) * 128],
                                       B.kT_hp[:, g_ * 512 + j_ * 128: g_ * 512 + (j_ + 1) * 128],
                                       B.iden[:, :]), inc="m")
            P.wait(v, "m", tkn)
            P.waitp(v, free["knsb"])
            tknv = P.op(v, lambda e, knat=knat: e.tensor_copy(knat, psk[:, 0:512]), inc="v")
            free["knat"] = ("v", tknv)
            for h in range(2):
                hb = h * 64
                for j_ in range(4):
                    P.waitp(te, free[f"sc{h}"])
                    tsc_ = P.op(te, lambda e, g_=g_, j_=j_, h=h, hb=hb:
                                e.matmul(psc[h][:, :],
                                         B.kT_hp[hb:hb + 64, g_ * 512 + j_ * 128: g_ * 512 + (j_ + 1) * 128],
                                         B.qT_hp[hb:hb + 64, g_ * 512:(g_ + 1) * 512],
                                         start=True, stop=True, tile_position=(hb, 0)), inc="m")
                    P.wait(v, "m", tsc_)
                    if j_ == 0:
                        P.waitp(v, free[f"abm{h}"])
                    tab = P.op(v, lambda e, j_=j_, h=h:
                               e.tensor_tensor(ab[h][:, j_ * 128:(j_ + 1) * 128],
                                               psc[h][:, j_ * 128:(j_ + 1) * 128],
                                               B.causal[:, :], op=ALU.mult), inc="v")
                    if j_ < 3:
                        tab = P.op(v, lambda e, j_=j_, h=h:
                                   e.tensor_copy(ab[h][:, (j_ + 1) * 128: 512],
                                                 psc[h][:, (j_ + 1) * 128: 512]), inc="v")
                    free[f"sc{h}"] = ("v", tab)
                    P.wait(te, "v", tab)
                    if j_ == 0:
                        P.waitp(te, free[f"y{h}"])
                    tnum = P.op(te, lambda e, g_=g_, j_=j_, h=h:
                                e.matmul(psy[h][0:65, j_ * 128: 512],
                                         B.Vt_hp[:, (g_ * 4 + j_) * 130 + h * 65: (g_ * 4 + j_) * 130 + h * 65 + 65],
                                         ab[h][:, j_ * 128: 512],
                                         start=(j_ == 0), stop=(g_ == 0 and j_ == 3)), inc="m")
                if g_ > 0:
                    P.waitp(te, free[f"Ssb{h}"])
                    tnum = P.op(te, lambda e, g_=g_, h=h, hb=hb:
                                e.matmul(psy[h][0:65, 0:512], B.S_sb[hb:hb + 64, 0:65],
                                         B.qT_hp[hb:hb + 64, g_ * 512:(g_ + 1) * 512],
                                         start=False, stop=True, tile_position=(hb, 0)), inc="m")
                free[f"abm{h}"] = ("m", tnum)
                free[f"y{h}m"] = ("m", tnum)
                P.wait(te, "v", tknv)
                P.waitp(te, free[f"S{h}"])
                for j_ in range(4):
                    tS = P.op(te, lambda e, g_=g_, j_=j_, h=h, hb=hb, knat=knat:
                              e.matmul(psS[h][0:64, 0:65],
                                       knat[:, j_ * 128 + hb: j_ * 128 + hb + 64],
                                       B.Vt_hp[:, (g_ * 4 + j_) * 130 + h * 65: (g_ * 4 + j_) * 130 + h * 65 + 65],
                                       start=(g_ == 0 and j_ == 0), stop=(g_ == 7 and j_ == 3)), inc="m")
                if h == 1:
                    free["knsb"] = ("m", tS)
                P.wait(v, "m", tS)
                tsnap = P.op(v, lambda e, h=h, hb=hb: e.tensor_copy(B.S_sb[hb:hb + 64, 0:65],
                                                                   psS[h][0:64, 0:65]), inc="v")
                free[f"S{h}"] = ("v", tsnap)
                free[f"Ssb{h}"] = ("v", tsnap)
            for h in range(2):
                hb = h * 64
                P.wait(v, "m", free[f"y{h}m"][1])
                P.waitp(v, free["den"])
                td1 = P.op(v, lambda e, h=h: e.tensor_scalar(B.den_sb[0:1, 0:512], psy[h][64:65, 0:512],
                                                             LA_EPS, None, op0=ALU.add), inc="v")
                P.wait(sc, "v", td1)
                P.waitp(sc, free["rrow"])
                td2 = P.op(sc, lambda e: e.activation(B.den_sb[0:1, 0:512], B.den_sb[0:1, 0:512],
                                                      AF.Ln), inc="s")
                P.wait(sc, "s", td2)
                td2 = P.op(sc, lambda e: e.activation(B.rrow_sb[0:1, 0:512], B.den_sb[0:1, 0:512],
                                                      AF.Exp, scale=-1.0), inc="s")
                free["den"] = ("s", td2)
                P.wait(te, "s", td2)
                P.waitp(te, free["r"])
                tr = P.op(te, lambda e: e.matmul(psr[0:64, 0:512], B.ones_row[0:1, 0:64],
                                                 B.rrow_sb[0:1, 0:512], start=True, stop=True), inc="m")
                free["rrow"] = ("m", tr)
                P.wait(v, "m", tr)
                P.waitp(v, free["rs"])
                trs = P.op(v, lambda e: e.tensor_copy(B.r_sb[0:64, 0:512], psr[0:64, 0:512]), inc="v")
                free["r"] = ("v", trs)
                P.wait(v, "v", trs)
                ty = P.op(v, lambda e, h=h, hb=hb, g_=g_, hp=hp:
                          e.tensor_tensor(B.yT[hb:hb + 64, hp * T + g_ * 512: hp * T + (g_ + 1) * 512],
                                          psy[h][0:64, 0:512], B.r_sb[0:64, 0:512],
                                          op=ALU.mult), inc="v")
                free[f"y{h}"] = ("v", ty)
                free["rs"] = ("v", ty)
    B.yT_ready = P.now("v")


def build_wproj_residual(P, B, w_dram, x_dram, x1_dram):
    v, sc, te, g, sy = "vector", "scalar", "tensor", "gpsimd", "sync"
    barrier(P)
    wr = w_dram.rearrange("(ec p) n -> p ec n", p=128)
    P.dma(g, B.wstream[0][:, :].rearrange("p (ec n) -> p ec n", n=1024), wr[:, 0:4, :], inc="dw")
    P.dma(g, B.wstream[1][:, :].rearrange("p (ec n) -> p ec n", n=1024), wr[:, 4:8, :], inc="dw")
    tdw = P.now("dw")
    P.wait(te, "dw", tdw)
    d = "dxf"
    for i in range(NT):
        bi = i % 2
        xt = B.xt[bi]
        P.waitp(sy, B.xt_free2[bi])
        td = P.dma(sy, xt[:, :], x_dram[ts(i), :], inc=d)
        for half in range(2):
            pb = half
            pst = B.ps[pb]
            P.waitp(te, B.ps_free[pb])
            for cc in range(NC):
                tm = P.op(te, lambda e, cc=cc, i=i, half=half, pst=pst:
                          e.matmul(pst[:, :], B.yT[:, cc * T + i * 128: cc * T + (i + 1) * 128],
                                   B.wstream[cc // 4][:, (cc % 4) * 1024 + half * 512: (cc % 4) * 1024 + (half + 1) * 512],
                                   start=(cc == 0), stop=(cc == NC - 1)), inc="m")
            P.wait(v, "m", tm)
            P.wait(v, d, td)
            t1 = P.op(v, lambda e, bi=bi, half=half, pst=pst:
                      e.tensor_tensor(B.x1_sb[bi][:, half * 512:(half + 1) * 512], pst[:, :],
                                      B.xt[bi][:, half * 512:(half + 1) * 512], op=ALU.add), inc="v")
            P.wait(v, "v", t1)
            t2 = P.op(v, lambda e, bi=bi, half=half:
                      e.tensor_tensor(B.x1_sb[bi][:, half * 512:(half + 1) * 512],
                                      B.x1_sb[bi][:, half * 512:(half + 1) * 512],
                                      B.ffn_rep[:, half * 512:(half + 1) * 512], op=ALU.add), inc="v")
            B.ps_free[pb] = ("v", t2)
        tvv = P.now("v")
        P.wait(sy, "v", tvv)
        tso = P.dma(sy, x1_dram[ts(i), :], B.x1_sb[bi][:, :], inc="dx1")
        B.xt_free2[bi] = ("dx1", tso)
    P.wait(sy, "dx1", P.now("dx1"))



def build_rope_tables(P, B, nrope_dram):
    v, sc, te, g, sy = "vector", "scalar", "tensor", "gpsimd", "sync"
    barrier(P)
    td = P.dma(sy, B.nr_sb[0:1, 0:1], nrope_dram[0:1, 0:1], inc="dh")
    P.wait(sc, "dh", td)
    t = P.op(sc, lambda e: e.activation(B.nr_sb[0:1, 1:2], B.nr_sb[0:1, 0:1], AF.Sigmoid), inc="s")
    P.wait(v, "s", t)
    t = P.op(v, lambda e: e.tensor_scalar(B.nr_sb[0:1, 2:3], B.nr_sb[0:1, 1:2], 3400.0, 8300.0,
                                          op0=ALU.mult, op1=ALU.add), inc="v")
    P.wait(sc, "v", t)
    t = P.op(sc, lambda e: e.activation(B.nr_sb[0:1, 3:4], B.nr_sb[0:1, 2:3], AF.Ln), inc="s")
    tg = P.op(g, lambda e: e.iota(B.jrow[0:1, 0:16], [[1, 16]], channel_multiplier=0,
                                  allow_small_or_imprecise_dtypes=True), inc="g")
    P.wait(v, "s", t)
    P.wait(v, "g", tg)
    t = P.op(v, lambda e: e.tensor_scalar(B.jrow[0:1, 16:32], B.jrow[0:1, 0:16], B.nr_sb[0:1, 3:4],
                                          -1.0 / 16.0, op0=ALU.mult, op1=ALU.mult), inc="v")
    P.wait(sc, "v", t)
    t = P.op(sc, lambda e: e.activation(B.jrow[0:1, 32:48], B.jrow[0:1, 16:32], AF.Exp), inc="s")
    P.wait(v, "s", t)
    t = P.op(v, lambda e: e.tensor_scalar(B.jrow[0:1, 48:64], B.jrow[0:1, 32:48], 2.0 * PI / (T - 1), None,
                                          op0=ALU.mult), inc="v")
    P.wait(v, "v", t)
    P.waitp(te, B.ps_free[5])
    P.wait(te, "v", t)
    for i in range(NT):
        pi_ = i % 2
        P.waitp(g, B.posrow_free[pi_])
        tg = P.op(g, lambda e, i=i, pi_=pi_: e.iota(B.posrow[pi_][0:1, :], [[1, 128]], base=i * 128,
                                                    channel_multiplier=0,
                                                    allow_small_or_imprecise_dtypes=True), inc="g")
        P.wait(te, "g", tg)
        tm = P.op(te, lambda e, i=i, pi_=pi_: e.matmul(B.ps[5][:, i * 16:(i + 1) * 16],
                                                       B.posrow[pi_][0:1, :], B.jrow[0:1, 48:64],
                                                       start=True, stop=True), inc="m")
        B.posrow_free[pi_] = ("m", tm)
    tm = P.now("m")
    P.wait(v, "m", tm)
    t = P.op(v, lambda e: e.tensor_scalar(B.gelu_junk[0][:, :], B.ps[5][:, :], PI, None,
                                          op0=ALU.is_gt), inc="v")
    P.wait(v, "v", t)
    t = P.op(v, lambda e: e.scalar_tensor_tensor(B.gelu_junk[1][:, :], B.gelu_junk[0][:, :],
                                                 -2.0 * PI, B.ps[5][:, :], op0=ALU.mult,
                                                 op1=ALU.add), inc="v")
    B.ps_free[5] = ("v", t)
    P.wait(sc, "v", t)
    ts1 = P.op(sc, lambda e: e.activation(B.sin_nat[:, :], B.gelu_junk[1][:, :], AF.Sin), inc="s")
    P.wait(v, "v", t)
    t = P.op(v, lambda e: e.tensor_scalar(B.gelu_junk[0][:, :], B.gelu_junk[1][:, :], PI / 2, None,
                                          op0=ALU.add), inc="v")
    P.wait(v, "v", t)
    t = P.op(v, lambda e: e.tensor_scalar(B.cos_nat[:, :], B.gelu_junk[0][:, :], PI, None,
                                          op0=ALU.is_gt), inc="v")
    P.wait(v, "v", t)
    t = P.op(v, lambda e: e.scalar_tensor_tensor(B.cos_nat[:, :], B.cos_nat[:, :], -2.0 * PI,
                                                 B.gelu_junk[0][:, :], op0=ALU.mult,
                                                 op1=ALU.add), inc="v")
    P.wait(sc, "v", t)
    tc = P.op(sc, lambda e: e.activation(B.cos_nat[:, :], B.cos_nat[:, :], AF.Sin), inc="s")
    B.rope_ready = tc


def build_l1_attn(P, B, wq, wk, wv, bq, bk, bv):
    v, sc, te, g, sy = "vector", "scalar", "tensor", "gpsimd", "sync"
    wqr = wq.rearrange("(ec p) n -> p ec n", p=128)
    wkr = wk.rearrange("(ec p) n -> p ec n", p=128)
    wvr = wv.rearrange("(ec p) n -> p ec n", p=128)
    for hp in range(int(os.environ.get("DEVHP1", "8"))):
        barrier(P)
        wsl = B.wstream[0]
        for j, wr in enumerate((wqr, wkr, wvr)):
            P.dma(g, wsl[:, j * 1024:(j + 1) * 1024].rearrange("p (ec n) -> p ec n", n=128),
                  wr[:, :, hp * 128:(hp + 1) * 128], inc="dw")
        for j, bd in enumerate((bq, bk, bv)):
            P.dma(g, B.ffn_row[0:1, j * 128:(j + 1) * 128], bd[0:1, hp * 128:(hp + 1) * 128], inc="dh")
        tdw = P.now("dw")
        tdh = P.now("dh")
        P.wait(te, "dh", tdh)
        P.waitp(te, B.ps_free[4])
        tm = P.op(te, lambda e: e.matmul(B.ps[4][:, 0:384], B.ones_row_bf[0:1, :], B.ffn_row[0:1, 0:384],
                                         start=True, stop=True), inc="m")
        P.wait(v, "m", tm)
        tbr = P.op(v, lambda e: e.tensor_copy(B.brep[:, :], B.ps[4][:, 0:384]), inc="v")
        B.ps_free[4] = ("v", tbr)
        tmem = P.op(v, lambda e: e.memset(B.qmax2[:, :], 0.0), inc="v")
        P.wait(te, "dw", tdw)
        gj_war = 0
        qn4 = B.exbp[0][:, :].bitcast(F32)
        qbf4 = B.exbp[1][:, 0:512]
        rs4 = [B.gelu_junk[0][:, k * 128:(k + 1) * 128].rearrange("p (a h d) -> p a h d", a=4, h=2)
               for k in range(4)]
        pending_tp = []

        def flush_tp():
            for (dstq, tq_, tcast) in pending_tp:
                P.wait(te, "v", tcast)
                for j in range(4):
                    pbq = j % 2
                    P.waitp(te, B.psbf_free[pbq])
                    tt = P.op(te, lambda e, j=j, pbq=pbq, qbf4=qbf4:
                              e.transpose(B.ps_bf[pbq][:, 0:128], qbf4[:, j * 128:(j + 1) * 128],
                                          B.iden[:, :]), inc="m")
                    B.qbf_free = ("m", tt)
                    P.wait(v, "m", tt)
                    tv2 = P.op(v, lambda e, pbq=pbq, tq_=tq_, j=j, dstq=dstq:
                               e.tensor_copy(dstq[:, (tq_ * 4 + j) * 128:(tq_ * 4 + j + 1) * 128],
                                             B.ps_bf[pbq][:, 0:128]), inc="v")
                    B.psbf_free[pbq] = ("v", tv2)
            del pending_tp[:]
        for which, dst in ((0, B.qT_hp), (1, B.kT_hp)):
            for tq_ in range(8):
                tev = 0
                for j in range(4):
                    ti = tq_ * 4 + j
                    pb = j % 2
                    pst = B.ps[pb]
                    P.waitp(te, B.ps_free[pb])
                    for ec in range(NC):
                        tm = P.op(te, lambda e, ec=ec, ti=ti, which=which, pst=pst:
                                  e.matmul(pst[:, 0:128],
                                           B.xnT[:, ec * T + ti * 128: ec * T + (ti + 1) * 128],
                                           wsl[:, which * 1024 + ec * 128: which * 1024 + (ec + 1) * 128],
                                           start=(ec == 0), stop=(ec == NC - 1)), inc="m")
                    P.wait(v, "m", tm)
                    if j == 0:
                        P.wait(v, "v", tbr)
                        P.waitp(v, B.qn_free)
                    tev = P.op(v, lambda e, pst=pst, which=which, j=j, qn4=qn4:
                               e.tensor_tensor(qn4[:, j * 128:(j + 1) * 128], pst[:, 0:128],
                                               B.brep[:, which * 128:(which + 1) * 128], op=ALU.add), inc="v")
                    B.ps_free[pb] = ("v", tev)
                flush_tp()
                qv = qn4.rearrange("p (a h d) -> p a h d", a=4, h=2)
                P.wait(v, "v", tev)
                P.wait(v, "s", B.rope_ready)
                P.waitp(v, gj_war)
                for a_ in range(4):
                    ti = tq_ * 4 + a_
                    cosb = B.cos_nat[:, ti * 16:(ti + 1) * 16].unsqueeze(1).to_broadcast([128, 2, 16])
                    sinb = B.sin_nat[:, ti * 16:(ti + 1) * 16].unsqueeze(1).to_broadcast([128, 2, 16])
                    x1s = qv[:, a_, :, 32:48]
                    x2s = qv[:, a_, :, 48:64]
                    for k_, aa, b_ in ((0, x1s, cosb), (1, x2s, sinb), (2, x1s, sinb), (3, x2s, cosb)):
                        t = P.op(v, lambda e, k_=k_, aa=aa, b_=b_, a_=a_:
                                 e.tensor_tensor(rs4[k_][:, a_], aa, b_, op=ALU.mult), inc="v")
                P.wait(v, "v", t)
                for a_ in range(4):
                    t = P.op(v, lambda e, qv=qv, a_=a_: e.tensor_tensor(qv[:, a_, :, 32:48], rs4[0][:, a_],
                                                                       rs4[1][:, a_], op=ALU.subtract), inc="v")
                    t = P.op(v, lambda e, qv=qv, a_=a_: e.tensor_tensor(qv[:, a_, :, 48:64], rs4[2][:, a_],
                                                                       rs4[3][:, a_], op=ALU.add), inc="v")
                P.wait(v, "v", t)
                P.wait(sc, "v", t)
                for h in range(2):
                    tq2 = P.op(sc, lambda e, h=h, qv=qv:
                               e.activation(B.gelu_junk[1][:, 0:256].rearrange("p (a d) -> p a d", a=4),
                                            qv[:, :, h, :], AF.Square,
                                            accum_out=B.qn2[:, h:h + 1]), inc="s")
                    P.wait(sc, "s", tq2)
                P.wait(v, "s", tq2)
                tn = P.op(v, lambda e, which=which: e.tensor_tensor(B.qmax2[:, 2 * which:2 * which + 2],
                                                                   B.qmax2[:, 2 * which:2 * which + 2],
                                                                   B.qn2[:, 0:2], op=ALU.max), inc="v")
                gj_war = ("v", tn)
                P.waitp(v, B.qbf_free)
                tcast = P.op(v, lambda e, qn4=qn4, qbf4=qbf4: e.tensor_copy(qbf4, qn4), inc="v")
                B.qn_free = ("v", tcast)
                pending_tp.append((dst, tq_, tcast))
                if os.environ.get("DEVNODEFER", "") == "1":
                    flush_tp()
            flush_tp()
        tmem2 = P.op(v, lambda e: e.memset(B.Vt_hp[:, :], 1.0), inc="v")
        P.wait(v, "v", tmem2)
        P.wait(te, "v", tmem2)
        for ti in range(NT):
            pb = ti % 2
            pst = B.ps[pb]
            P.waitp(te, B.ps_free[pb])
            for ec in range(NC):
                tm = P.op(te, lambda e, ec=ec, ti=ti, pst=pst:
                          e.matmul(pst[:, 0:128], B.xnT[:, ec * T + ti * 128: ec * T + (ti + 1) * 128],
                                   wsl[:, 2048 + ec * 128: 2048 + (ec + 1) * 128],
                                   start=(ec == 0), stop=(ec == NC - 1)), inc="m")
            P.wait(v, "m", tm)
            tv = P.op(v, lambda e, ti=ti, pst=pst:
                      e.tensor_tensor(B.Vt_hp[:, ti * 130: ti * 130 + 130].rearrange("p (h d) -> p h d", h=2)[:, :, 0:64],
                                      pst[:, 0:128].rearrange("p (h d) -> p h d", h=2),
                                      B.brep[:, 256:384].rearrange("p (h d) -> p h d", h=2), op=ALU.add), inc="v")
            B.ps_free[pb] = ("v", tv)
        barrier(P)

        def mq():
            return P.op(v, lambda e: e.tensor_reduce(B.fold_a[:, 0:1], B.qmax2[:, 0:2], axis=AX.X,
                                                     op=ALU.max), inc="v")
        partition_total(P, B, mq, ALU.max, B.nr_sb[0:1, 4:5], 0)

        def mk():
            return P.op(v, lambda e: e.tensor_reduce(B.fold_a[:, 0:1], B.qmax2[:, 2:4], axis=AX.X,
                                                     op=ALU.max), inc="v")
        partition_total(P, B, mk, ALU.max, B.nr_sb[0:1, 5:6], 0)
        t = P.op(v, lambda e: e.tensor_tensor(B.nr_sb[0:1, 6:7], B.nr_sb[0:1, 4:5],
                                              B.nr_sb[0:1, 5:6], op=ALU.mult), inc="v")
        P.wait(sc, "v", t)
        t = P.op(sc, lambda e: e.activation(B.nr_sb[0:1, 6:7], B.nr_sb[0:1, 6:7], AF.Ln), inc="s")
        P.wait(sc, "s", t)
        t = P.op(sc, lambda e: e.activation(B.nr_sb[0:1, 6:7], B.nr_sb[0:1, 6:7], AF.Exp,
                                            scale=0.5), inc="s")
        P.wait(v, "s", t)
        t = P.op(v, lambda e: e.tensor_scalar(B.nr_sb[0:1, 6:7], B.nr_sb[0:1, 6:7], -0.125, None,
                                              op0=ALU.mult), inc="v")
        P.wait(v, "v", t)
        bcast_col(P, B, B.nr_sb[0:1, 6:7], B.negb[:, 0:1], B.bcb)
        barrier(P)
        exb = B.exbp
        psy = [B.ps[0], B.ps[1]]
        psr = B.ps[2]
        tex = [0, 0]
        tav = [[0, 0], [0, 0]]
        tsc_ = [0, 0]
        tyq = [0, 0]
        free = {"r": 0, "rs": 0, "rrow": 0}

        def emit_av(c, qb):
            sb = c & 1
            for h in range(2):
                if c == 0:
                    P.waitp(te, tyq[h])
                P.wait(te, "s", tex[sb])
                tav[h][sb] = ("m", P.op(te, lambda e, c=c, h=h, sb=sb:
                                        e.matmul(psy[h][0:65, :],
                                                 B.Vt_hp[:, c * 130 + h * 65: c * 130 + h * 65 + 65],
                                                 exb[sb][:, h * 512:(h + 1) * 512],
                                                 start=(c == 0), stop=(c == NT - 1)), inc="m"))
        for qb in range(8):
            for c in range(NT):
                sb = c & 1
                for h in range(2):
                    hb = h * 64
                    P.waitp(te, tex[sb] and ("s", tex[sb]))
                    tsc_[h] = P.op(te, lambda e, c=c, h=h, hb=hb, qb=qb, sb=sb:
                                   e.matmul(B.ps_pair[sb][:, h * 512:(h + 1) * 512],
                                            B.kT_hp[hb:hb + 64, c * 128:(c + 1) * 128],
                                            B.qT_hp[hb:hb + 64, qb * 512:(qb + 1) * 512],
                                            start=True, stop=True, tile_position=(hb, 0)), inc="m")
                P.wait(sc, "m", tsc_[1])
                P.waitp(sc, tav[0][sb])
                P.waitp(sc, tav[1][sb])
                tex[sb] = P.op(sc, lambda e, sb=sb:
                               e.activation(exb[sb][:, :], B.ps_pair[sb][:, :], AF.Exp,
                                            bias=B.negb[:, 0:1], scale=0.125), inc="s")
                if c >= 1:
                    emit_av(c - 1, qb)
            emit_av(NT - 1, qb)
            rrow1 = B.gelu_junk[1][0:1, :]
            rsb1 = B.gelu_junk[0][0:64, :]
            for h in range(2):
                hb = h * 64
                P.wait(sc, "m", tav[h][(NT - 1) & 1][1])
                P.waitp(sc, free["rrow"])
                td2 = P.op(sc, lambda e, h=h, rrow1=rrow1: e.activation(rrow1, psy[h][64:65, :],
                                                                       AF.Ln), inc="s")
                P.wait(sc, "s", td2)
                td2 = P.op(sc, lambda e, rrow1=rrow1: e.activation(rrow1, rrow1, AF.Exp,
                                                                  scale=-1.0), inc="s")
                P.wait(te, "s", td2)
                P.waitp(te, free["r"])
                tr = P.op(te, lambda e, rrow1=rrow1: e.matmul(psr[0:64, :], B.ones_row[0:1, 0:64],
                                                              rrow1, start=True, stop=True), inc="m")
                free["rrow"] = ("m", tr)
                P.wait(v, "m", tr)
                P.waitp(v, free["rs"])
                trs = P.op(v, lambda e, rsb1=rsb1: e.tensor_copy(rsb1, psr[0:64, :]), inc="v")
                free["r"] = ("v", trs)
                P.wait(v, "v", trs)
                ty = P.op(v, lambda e, h=h, hb=hb, qb=qb, hp=hp, rsb1=rsb1:
                          e.tensor_tensor(B.yT[hb:hb + 64, hp * T + qb * 512: hp * T + (qb + 1) * 512],
                                          psy[h][0:64, :], rsb1, op=ALU.mult), inc="v")
                tyq[h] = ("v", ty)
                free["rs"] = ("v", ty)
    B.yT_ready = P.now("v")


def build_l1_out(P, B, wo, w1t, vr, ct, bo, b1, x1_dram, out_dram):
    v, sc, te, g, sy = "vector", "scalar", "tensor", "gpsimd", "sync"
    barrier(P)
    P.dma(g, B.vr_bf[:, :].rearrange("p (ec r) -> p ec r", r=32),
          vr.rearrange("(ec p) r -> p ec r", p=128), inc="dw")
    P.dma(g, B.ct_bf[0:32, :], ct[0:32, :], inc="dw")
    tdw = P.now("dw")
    P.wait(te, "dw", tdw)
    for tb in range(8):
        pb = tb % 2
        pst = B.ps[pb]
        P.waitp(te, B.ps_free[pb])
        for ec in range(NC):
            tm = P.op(te, lambda e, ec=ec, tb=tb, pst=pst:
                      e.matmul(pst[0:32, :], B.vr_bf[:, ec * 32:(ec + 1) * 32],
                               B.xnT[:, ec * T + tb * 512: ec * T + (tb + 1) * 512],
                               start=(ec == 0), stop=(ec == NC - 1)), inc="m")
        P.wait(v, "m", tm)
        tv = P.op(v, lambda e, tb=tb, pst=pst: e.tensor_copy(B.xvrT[0:32, tb * 512:(tb + 1) * 512],
                                                             pst[0:32, :]), inc="v")
        B.ps_free[pb] = ("v", tv)
    barrier(P)
    wor = wo.rearrange("(ec p) n -> p ec n", p=128)
    w1r = w1t.rearrange("(ec p) n -> p ec n", p=128)
    halves = (1, 0) if os.environ.get("DEVJH", "") == "1" else (0, 1)
    for half in halves:
        P.waitp(g, B.wstream_free[0])
        P.waitp(g, B.wstream_free[1])
        P.dma(g, B.wstream[0][:, :].rearrange("p (ec n) -> p ec n", n=512),
              wor[:, :, half * 512:(half + 1) * 512], inc="dw")
        P.dma(g, B.wstream[1][:, :].rearrange("p (ec n) -> p ec n", n=512),
              w1r[:, :, half * 512:(half + 1) * 512], inc="dw")
        tdw = P.now("dw")
        P.wait(te, "dw", tdw)
        P.waitp(te, B.ps_free[4])
        P.waitp(g, B.frow_free)
        P.dma(g, B.ffn_row[0:1, 0:512], bo[0:1, half * 512:(half + 1) * 512], inc="dh")
        P.dma(g, B.ffn_row[0:1, 512:1024], b1[0:1, half * 512:(half + 1) * 512], inc="dh")
        tdh = P.now("dh")
        P.wait(te, "dh", tdh)
        tm = P.op(te, lambda e: e.matmul(B.ps[4][:, 0:512], B.ones_row_bf[0:1, :], B.ffn_row[0:1, 0:512],
                                         start=True, stop=False), inc="m")
        tm = P.op(te, lambda e: e.matmul(B.ps[4][:, 0:512], B.ones_row_bf[0:1, :], B.ffn_row[0:1, 512:1024],
                                         start=False, stop=True), inc="m")
        B.frow_free = ("m", tm)
        P.wait(v, "m", tm)
        trep = P.op(v, lambda e: e.tensor_copy(B.ffn_rep[:, 0:512], B.ps[4][:, 0:512]), inc="v")
        B.ps_free[4] = ("v", trep)
        d = "dxj"
        for i in range(NT):
            bi = i % 2
            xt = B.xt[bi]
            P.waitp(sy, B.xtj_free[bi])
            td = P.dma(sy, xt[:, 0:512], x1_dram[ts(i), half * 512:(half + 1) * 512], inc=d)
            pb = 2 + (i % 2)
            pst = B.ps[pb]
            P.waitp(te, B.ps_free[pb])
            DEVJ = os.environ.get("DEVJ", "all")
            first = True
            if DEVJ in ("all", "attn"):
                for cc in range(NC):
                    P.op(te, lambda e, cc=cc, i=i, pst=pst, first=first and cc == 0:
                         e.matmul(pst[:, :], B.yT[:, cc * T + i * 128: cc * T + (i + 1) * 128],
                                  B.wstream[0][:, cc * 512:(cc + 1) * 512],
                                  start=first and cc == 0, stop=False), inc="m")
                first = False
            if DEVJ in ("all", "ffn", "base", "corr0"):
                for cc in range(NC):
                    P.op(te, lambda e, cc=cc, i=i, pst=pst, st=first and cc == 0,
                         sp=(DEVJ == "base" and cc == NC - 1):
                         e.matmul(pst[:, :], B.xnT[:, cc * T + i * 128: cc * T + (i + 1) * 128],
                                  B.wstream[1][:, cc * 512:(cc + 1) * 512],
                                  start=st, stop=sp), inc="m")
                first = False
            if DEVJ != "base":
                tm = P.op(te, lambda e, i=i, pst=pst, st=first, half=half:
                          e.matmul(pst[:, :], B.xvrT[0:32, i * 128:(i + 1) * 128],
                                   B.ct_bf[0:32, half * 512:(half + 1) * 512],
                                   start=st, stop=True), inc="m")
            else:
                tm = P.now("m")
            P.wait(v, "m", tm)
            P.wait(v, d, td)
            P.wait(v, "v", trep)
            P.waitp(v, B.x1sbj_free)
            t1 = P.op(v, lambda e, bi=bi, pst=pst: e.tensor_tensor(B.x1_sb[0][:, 0:512], pst[:, :],
                                                                  B.xt[bi][:, 0:512], op=ALU.add), inc="v")
            P.wait(v, "v", t1)
            t2 = P.op(v, lambda e, bi=bi: e.tensor_tensor(B.x1_sb[0][:, 0:512], B.x1_sb[0][:, 0:512],
                                                          B.ffn_rep[:, 0:512], op=ALU.add), inc="v")
            B.ps_free[pb] = ("v", t2)
            P.wait(sy, "v", t2)
            tso = P.dma(sy, out_dram[ts(i), half * 512:(half + 1) * 512], B.x1_sb[0][:, 0:512], inc="dout")
            B.xtj_free[bi] = ("dout", tso)
            B.x1sbj_free = ("dout", tso)
        tlast = P.now("m")
        B.wstream_free = [("m", tlast), ("m", tlast)]
    P.wait(sy, "dout", P.now("dout"))




class Bufs:
    pass


STAGES = int(os.environ.get('STAGES', '99'))


def build_core_program(debug_taps=()):
    nc = bass.Bass("TRN2", target_bir_lowering=False)
    ctx = ExitStack()
    B = Bufs()

    x = nc.dram_tensor("x", [T, E], F32, kind="ExternalInput").ap()
    out = nc.dram_tensor("out", [T, E], F32, kind="ExternalOutput").ap()

    taps = {}

    with ctx:
        B.xnT = ctx.enter_context(nc.sbuf_tensor("xnT", [128, NC * T], BF16))
        B.xt = [ctx.enter_context(nc.sbuf_tensor(f"xt{k}", [128, E], F32)) for k in range(2)]
        B.xn_bf = ctx.enter_context(nc.sbuf_tensor("xn_bf", [128, E], BF16))
        B.ln_m = ctx.enter_context(nc.sbuf_tensor("ln_m", [128, 1], F32))
        B.ln_sb = ctx.enter_context(nc.sbuf_tensor("ln_sb", [128, 1], F32))
        B.ln_s2 = ctx.enter_context(nc.sbuf_tensor("ln_s2", [128, 1], F32))
        B.ln_msq = ctx.enter_context(nc.sbuf_tensor("ln_msq", [128, 1], F32))
        B.ln_var = ctx.enter_context(nc.sbuf_tensor("ln_var", [128, 1], F32))
        B.ln_u = ctx.enter_context(nc.sbuf_tensor("ln_u", [128, 1], F32))
        B.ln_rstd = ctx.enter_context(nc.sbuf_tensor("ln_rstd", [128, 1], F32))
        B.iota_f = ctx.enter_context(nc.sbuf_tensor("iota_f", [128, 128], F32))
        B.iota_p = ctx.enter_context(nc.sbuf_tensor("iota_p", [128, 1], F32))
        B.iden = ctx.enter_context(nc.sbuf_tensor("iden", [128, 128], BF16))
        B.causal = ctx.enter_context(nc.sbuf_tensor("causal", [128, 128], BF16))
        B.ones_col = ctx.enter_context(nc.sbuf_tensor("ones_col", [128, 1], F32))
        B.eps_col = ctx.enter_context(nc.sbuf_tensor("eps_col", [128, 1], F32))

        B.v_sb = ctx.enter_context(nc.sbuf_tensor("v_sb", [128, 64], F32))
        B.iptbuf_free = [0, 0]
        B.gelu_junk = [ctx.enter_context(nc.sbuf_tensor(f"gjunk{k}", [128, 512], F32)) for k in range(2)]
        B.gj_free = [0, 0]
        B.vcol = [ctx.enter_context(nc.sbuf_tensor(f"vcol{k}", [128, 1], F32)) for k in range(2)]
        B.fold_a = ctx.enter_context(nc.sbuf_tensor("fold_a", [128, 1], F32))
        B.fold_b = ctx.enter_context(nc.sbuf_tensor("fold_b", [64, 1], F32))
        B.fold_s = ctx.enter_context(nc.sbuf_tensor("fold_s", [64, 1], F32))
        B.fold_c = ctx.enter_context(nc.sbuf_tensor("fold_c", [32, 1], F32))
        B.fold_t = ctx.enter_context(nc.sbuf_tensor("fold_t", [32, 32], F32))
        B.fold_tt = ctx.enter_context(nc.sbuf_tensor("fold_tt", [32, 32], F32))
        B.bis = ctx.enter_context(nc.sbuf_tensor("bis", [1, 8], F32))
        B.bis_neg = ctx.enter_context(nc.sbuf_tensor("bis_neg", [128, 64], F32))
        B.bis_ge = ctx.enter_context(nc.sbuf_tensor("bis_ge", [128, 64], F32))
        B.bis_mid = ctx.enter_context(nc.sbuf_tensor("bis_mid", [128, 1], F32))
        B.ones_row = ctx.enter_context(nc.sbuf_tensor("ones_row", [1, 128], F32))
        B.thr_rep = ctx.enter_context(nc.sbuf_tensor("thr_rep", [128, 1], F32))
        B.masked_bf = ctx.enter_context(nc.sbuf_tensor("masked_bf", [128, 64], BF16))
        B.mask_f = ctx.enter_context(nc.sbuf_tensor("mask_f", [128, 64], F32))
        B.wstream = [ctx.enter_context(nc.sbuf_tensor(f"wstr{k}", [128, 4096], BF16)) for k in range(2)]
        B.wstream_free = [0, 0]
        B.ipt_buf = [B.wstream[k][:, 0:1024] for k in range(2)]
        B.pv_buf = [ctx.enter_context(nc.sbuf_tensor(f"pvb{k}", [128, 512], BF16)) for k in range(2)]
        B.pvbuf_free = [0, 0]
        B.rowbuf = [ctx.enter_context(nc.sbuf_tensor(f"rowb{k}", [1, 512], F32)) for k in range(2)]
        B.rowbuf_free = [0, 0]
        B.p_sb = ctx.enter_context(nc.sbuf_tensor("p_sb", [128, 32], F32))
        B.e_sb = ctx.enter_context(nc.sbuf_tensor("e_sb", [128, 32], F32))
        B.e_bf = ctx.enter_context(nc.sbuf_tensor("e_bf", [128, 32], BF16))
        B.negpmax = ctx.enter_context(nc.sbuf_tensor("negpmax", [128, 1], F32))
        B.sc4 = ctx.enter_context(nc.sbuf_tensor("sc4", [1, 4], F32))
        B.agg_sb = ctx.enter_context(nc.sbuf_tensor("agg_sb", [1, 512], BF16))
        B.agg_col_bf = ctx.enter_context(nc.sbuf_tensor("agg_col_bf", [128, 4], BF16))
        B.o_sb = ctx.enter_context(nc.sbuf_tensor("o_sb", [128, 64], F32))
        B.thr2_rep = ctx.enter_context(nc.sbuf_tensor("thr2_rep", [128, 1], F32))
        B.thr3_rep = ctx.enter_context(nc.sbuf_tensor("thr3_rep", [128, 1], F32))
        B.ffn_row = ctx.enter_context(nc.sbuf_tensor("ffn_row", [1, 1024], BF16))
        B.ones_row_bf = ctx.enter_context(nc.sbuf_tensor("ones_row_bf", [1, 128], BF16))
        B.ffn_rep = ctx.enter_context(nc.sbuf_tensor("ffn_rep", [128, 1024], F32))
        B.ones11 = ctx.enter_context(nc.sbuf_tensor("ones11", [1, 1], F32))
        B.ones11_bf = ctx.enter_context(nc.sbuf_tensor("ones11_bf", [1, 1], BF16))
        B.yT = ctx.enter_context(nc.sbuf_tensor("yT", [128, NC * T], BF16))
        B.qT_hp = ctx.enter_context(nc.sbuf_tensor("qT_hp", [128, T], BF16))
        B.kT_hp = ctx.enter_context(nc.sbuf_tensor("kT_hp", [128, T], BF16))
        B.Vt_hp = ctx.enter_context(nc.sbuf_tensor("Vt_hp", [128, NT * 130], BF16))
        B.knat_sb = ctx.enter_context(nc.sbuf_tensor("knat_sb", [128, 128], BF16))
        B.S_sb = ctx.enter_context(nc.sbuf_tensor("S_sb", [128, 65], BF16))
        B.ab_sb = [ctx.enter_context(nc.sbuf_tensor(f"ab{k}", [128, 128], BF16)) for k in range(2)]
        B.den_sb = B.gelu_junk[1][0:1, 0:512]
        B.rrow_sb = B.wstream[0][:, 3072:4096].bitcast(F32)[0:1, 0:512]
        B.r_sb = B.gelu_junk[0][0:64, 0:512]
        B.x1_sb = [ctx.enter_context(nc.sbuf_tensor("x1sb0", [128, E], F32))] * 2
        B.xt_free2 = [0, 0]
        B.xtj_free = [0, 0]
        B.x1sbj_free = 0
        B.frow_free = 0
        B.qn_free = 0
        B.qbf_free = 0
        B.nr_sb = ctx.enter_context(nc.sbuf_tensor("nr_sb", [1, 8], F32))
        B.jrow = ctx.enter_context(nc.sbuf_tensor("jrow", [1, 64], F32))
        B.posrow = [ctx.enter_context(nc.sbuf_tensor(f"posr{k}", [1, 128], F32)) for k in range(2)]
        B.posrow_free = [0, 0]
        B.qn2 = ctx.enter_context(nc.sbuf_tensor("qn2", [128, 2], F32))
        B.qmax2 = ctx.enter_context(nc.sbuf_tensor("qmax2", [128, 4], F32))
        B.negb = ctx.enter_context(nc.sbuf_tensor("negb", [128, 2], F32))
        B.cos_nat = B.wstream[1][:, 0:1024].bitcast(F32)
        B.sin_nat = B.wstream[1][:, 1024:2048].bitcast(F32)
        B.exbp = [B.wstream[1][:, 2048:3072], B.wstream[1][:, 3072:4096]]
        B.qn = B.wstream[0][:, 3840:4096].bitcast(F32)
        B.rope_scr = B.iota_f[:, :]
        B.qbf = B.mask_f[:, :].bitcast(BF16)
        B.brep = B.wstream[0][:, 3072:3840].bitcast(F32)
        B.vr_bf = B.kT_hp[:, 0:256]
        B.ct_bf = B.qT_hp[0:32, 0:1024]
        B.xvrT = B.Vt_hp[0:32, 0:4096]

        B.ps = [ctx.enter_context(nc.psum_tensor(f"ps{k}", [128, 512], F32)) for k in range(4)]
        B.ps_pair = [ctx.enter_context(nc.psum_tensor(f"pspair{k}", [128, 1024], F32)) for k in range(2)]
        B.ps += [B.ps_pair[0][:, 0:512], B.ps_pair[0][:, 512:1024],
                 B.ps_pair[1][:, 0:512], B.ps_pair[1][:, 512:1024]]
        B.ps_free = [0] * 8
        B.bcb = B.ps[4]
        B.ps_bf = [B.ps[2][:, :].bitcast(BF16), B.ps[3][:, :].bitcast(BF16)]
        B.psbf_free = [0, 0]
        B.xt_free = [0, 0]
        B.xnbf_free = 0

        P = Prog(nc, ctx)
        build_constants(P, B)
        P.op("vector", lambda e: e.memset(B.ones_row[:, :], 1.0), inc="v")
        P.wait("tensor", "v", B.const_ready)
        build_ln_xnt(P, B, x, tag="0")
        if STAGES >= 2:
            ipt = nc.dram_tensor("ipt", [E, N_IN], F32, kind="ExternalInput").ap()
            build_acts_mean(P, B, ipt)
            build_bisect(P, B, B.v_sb[:, :], 64, K_IN, B.thr_rep[:, 0:1])
            v = "vector"
            t = P.op(v, lambda e: e.tensor_scalar(B.mask_f[:, :], B.v_sb[:, :], B.thr_rep[:, 0:1],
                                                  None, op0=ALU.is_ge), inc="v")
            P.wait(v, "v", t)
            t = P.op(v, lambda e: e.tensor_tensor(B.masked_bf[:, :], B.v_sb[:, :], B.mask_f[:, :],
                                                  op=ALU.mult), inc="v")
            P.wait(v, "v", t)
            B.maskedv_ready = t
        if STAGES >= 3:
            piwt = nc.dram_tensor("piwt", [N_IN, N_PROC], F32, kind="ExternalInput").ap()
            pv = nc.dram_tensor("pv", [N_PROC, D_PV], F32, kind="ExternalInput").ap()
            oiwt = nc.dram_tensor("oiwt", [D_PV, N_OUT], F32, kind="ExternalInput").ap()
            op_d = nc.dram_tensor("op", [N_OUT, E], F32, kind="ExternalInput").ap()
            build_ffn_chain(P, B, piwt, pv, oiwt, op_d)
        if STAGES >= 4:
            w_attn = nc.dram_tensor("w_attn", [E, 3 * E], F32, kind="ExternalInput").ap()
            build_linattn(P, B, w_attn)
        if STAGES >= 5:
            w_proj = nc.dram_tensor("w_proj", [E, E], F32, kind="ExternalInput").ap()
            x1 = nc.dram_tensor("x1", [T, E], F32, kind="Internal").ap()
            build_wproj_residual(P, B, w_proj, x, x1)
            B.x1_ap = x1
        if STAGES >= 6:
            barrier(P)
            build_ln_xnt(P, B, x1, tag="g1")
        if STAGES >= 7:
            nrope = nc.dram_tensor("nrope", [1, 1], F32, kind="ExternalInput").ap()
            wq = nc.dram_tensor("wq", [E, E], F32, kind="ExternalInput").ap()
            wk = nc.dram_tensor("wk", [E, E], F32, kind="ExternalInput").ap()
            wv = nc.dram_tensor("wv", [E, E], F32, kind="ExternalInput").ap()
            bq = nc.dram_tensor("bq", [1, E], F32, kind="ExternalInput").ap()
            bk = nc.dram_tensor("bk", [1, E], F32, kind="ExternalInput").ap()
            bv = nc.dram_tensor("bv", [1, E], F32, kind="ExternalInput").ap()
            build_rope_tables(P, B, nrope)
            build_l1_attn(P, B, wq, wk, wv, bq, bk, bv)
        if STAGES >= 8:
            wo = nc.dram_tensor("wo", [E, E], F32, kind="ExternalInput").ap()
            w1t = nc.dram_tensor("w1t", [E, E], F32, kind="ExternalInput").ap()
            vr = nc.dram_tensor("vr", [E, 32], F32, kind="ExternalInput").ap()
            ct = nc.dram_tensor("ct", [32, E], F32, kind="ExternalInput").ap()
            bo = nc.dram_tensor("bo", [1, E], F32, kind="ExternalInput").ap()
            b1 = nc.dram_tensor("b1", [1, E], F32, kind="ExternalInput").ap()
            build_l1_out(P, B, wo, w1t, vr, ct, bo, b1, x1, out)

        for name in debug_taps:
            if name == "vmean":
                tp = nc.dram_tensor("tap_vmean", [128, 64], F32, kind="ExternalOutput").ap()
                P.wait("sync", "v", P.now("v"))
                P.dma("sync", tp[:, :], B.v_sb[:, :], inc="do")
                tp2 = nc.dram_tensor("tap_thr", [128, 1], F32, kind="ExternalOutput").ap()
                P.dma("sync", tp2[:, :], B.thr_rep[:, :], inc="do")
                tp3 = nc.dram_tensor("tap_masked", [128, 64], F32, kind="ExternalOutput").ap()
                P.wait("gpsimd", "v", P.now("v"))
                P.dma("gpsimd", tp3[:, :], B.masked_bf[:, :], inc="do")
            if name == "ffn":
                tp = nc.dram_tensor("tap_pacts", [128, 32], F32, kind="ExternalOutput").ap()
                tp2 = nc.dram_tensor("tap_agg", [1, 512], F32, kind="ExternalOutput").ap()
                tp3 = nc.dram_tensor("tap_oacts", [128, 64], F32, kind="ExternalOutput").ap()
                tp4 = nc.dram_tensor("tap_ffnrep", [128, 1024], F32, kind="ExternalOutput").ap()
                P.wait("sync", "v", P.now("v"))
                P.wait("sync", "s", P.now("s"))
                P.dma("sync", tp[:, :], B.p_sb[:, :], inc="do")
                P.dma("sync", tp2[:, :], B.agg_sb[:, :], inc="do")
                P.dma("sync", tp3[:, :], B.o_sb[:, :], inc="do")
                P.dma("sync", tp4[:, :], B.ffn_rep[:, :], inc="do")
            if name == "yT":
                tp = nc.dram_tensor("tap_yT", [128, NC * T], F32, kind="ExternalOutput").ap()
                P.wait("gpsimd", "v", B.yT_ready)
                for c in range(NC):
                    P.dma("gpsimd", tp[:, c * T:(c + 1) * T], B.yT[:, c * T:(c + 1) * T], inc="do")
            if name == "x1":
                tp = nc.dram_tensor("tap_x1", [T, E], F32, kind="ExternalOutput").ap()
                P.wait("sync", "dx1", P.now("dx1"))
                for i in range(8):
                    P.dma("sync", tp[ts(i, 512), :], B.x1_ap[ts(i, 512), :], inc="do")
            if name == "qT":
                tp = nc.dram_tensor("tap_qT", [128, T], F32, kind="ExternalOutput").ap()
                tp2 = nc.dram_tensor("tap_kT", [128, T], F32, kind="ExternalOutput").ap()
                P.wait("gpsimd", "v", P.now("v"))
                P.dma("gpsimd", tp[:, :], B.qT_hp[:, :], inc="do")
                P.dma("gpsimd", tp2[:, :], B.kT_hp[:, :], inc="do")
            if name == "xnT":
                tp = nc.dram_tensor("tap_xnT", [128, NC * T], F32, kind="ExternalOutput").ap()
                P.wait("gpsimd", "v", B.xnt_ready)
                for c in range(NC):
                    P.dma("gpsimd", tp[:, c * T:(c + 1) * T], B.xnT[:, c * T:(c + 1) * T], inc="do")
                taps[name] = "tap_xnT"

        if STAGES < 8:
            P.wait("sync", "dx0", B.x_stream_done)
            for i in range(NT):
                P.dma("sync", out[ts(i), :], x[ts(i), :], inc="do")
        P.wait("sync", "do", P.now("do"))
        P.finalize()
    return nc, taps


def _layout_inputs(inputs):
    w = {
        "w_attn": inputs["L0_attn_w_attn"],
        "w_proj": inputs["L0_attn_w_proj"],
        "ipt": np.ascontiguousarray(inputs["L0_ffn_input_patterns"].T),
        "piwt": np.ascontiguousarray(inputs["L0_ffn_process_input_weights"].T),
        "pv": inputs["L0_ffn_process_values"],
        "oiwt": np.ascontiguousarray(inputs["L0_ffn_output_input_weights"].T),
        "op": inputs["L0_ffn_output_patterns"],
        "wq": inputs["L1_attn_Wq"], "wk": inputs["L1_attn_Wk"],
        "wv": inputs["L1_attn_Wv"], "wo": inputs["L1_attn_Wo"],
        "bq": inputs["L1_attn_bq"].reshape(1, E), "bk": inputs["L1_attn_bk"].reshape(1, E),
        "bv": inputs["L1_attn_bv"].reshape(1, E), "bo": inputs["L1_attn_bo"].reshape(1, E),
        "nrope": inputs["L1_attn_nrope_freq"].reshape(1, 1),
        "w1t": np.ascontiguousarray(inputs["L1_ffn_W"].T),
        "b1": inputs["L1_ffn_b"].reshape(1, E),
        "vr": inputs["L1_ffn_V_r"],
        "ct": np.ascontiguousarray(inputs["L1_ffn_C"].T),
    }
    w = {k: np.ascontiguousarray(v, dtype=np.float32) for k, v in w.items()}
    xs = inputs["L0_attn_x"]
    return [dict(w, x=np.ascontiguousarray(xs[b])) for b in range(8)]


def kernel(**inputs):
    in_maps = _layout_inputs(inputs)
    nc, _ = build_core_program()
    res = run_bass_kernel_spmd(nc, in_maps, core_ids=list(range(8)))
    return np.stack([res.results[b]["out"] for b in range(8)], axis=0)


if __name__ == "__main__":
    pass


# version 10
# speedup vs baseline: 1.2456x; 1.0637x over previous
import os
import numpy as np
from contextlib import ExitStack

import concourse.bass as bass
import concourse.mybir as mybir
from concourse.bass_utils import run_bass_kernel_spmd

F32 = mybir.dt.float32
BF16 = mybir.dt.bfloat16
U8 = mybir.dt.uint8
AF = mybir.ActivationFunctionType
ALU = mybir.AluOpType
AX = mybir.AxisListType

T = 4096
E = 1024
NT = T // 128
NC = E // 128
NH = 16
HD = 64
LN_EPS = 1e-5
LA_EPS = 1e-6
PI = float(np.pi)

N_IN = 8192
N_PROC = 4096
N_OUT = 8192
D_PV = 512
K_IN = 1024
K_PROC = 512
K_OUT = 1024
BISECT_ITERS = 30

ENGINES = ("tensor", "vector", "scalar", "gpsimd", "sync")


class Prog:

    def __init__(self, nc, ctx):
        self.nc = nc
        self.ctx = ctx
        self.q = {e: [] for e in ENGINES}
        self.sems = {}
        self.cnt = {}

    def sem(self, name):
        if name not in self.sems:
            self.sems[name] = self.ctx.enter_context(self.nc.semaphore(name))
            self.cnt[name] = 0
        return self.sems[name]

    def op(self, eng, fn, inc=None, dma=False):
        incs = []
        tgt = None
        if inc is not None:
            for name in ([inc] if isinstance(inc, str) else inc):
                step = 16 if dma else 1
                s = self.sem(name)
                self.cnt[name] += step
                incs.append((s, step))
                if tgt is None:
                    tgt = self.cnt[name]

        def run(engine, fn=fn, incs=incs):
            instr = fn(engine)
            for s, step in incs:
                instr = instr.then_inc(s, step)
        self.q[eng].append(run)
        return tgt

    def dma(self, eng, out, in_, inc, **kw):
        return self.op(eng, lambda e: e.dma_start(out, in_, **kw), inc=inc, dma=True)

    def wait(self, eng, sem_name, tgt):
        s = self.sem(sem_name)
        self.q[eng].append(lambda engine: engine.wait_ge(s, tgt))

    def waitp(self, eng, pair):
        if pair:
            self.wait(eng, pair[0], pair[1])

    def now(self, name):
        self.sem(name)
        return self.cnt[name]

    def finalize(self):
        blk = self.ctx.enter_context(self.nc.Block())
        for eng in ENGINES:
            ops = self.q[eng]

            def make(ops):
                def body(engine):
                    for f in ops:
                        f(engine)
                return body
            getattr(blk, eng)(make(ops))


def ts(i, n=128):
    return slice(i * n, (i + 1) * n)



def build_constants(P, B):
    g = "gpsimd"
    P.op(g, lambda e: e.iota(B.iota_f[:, :], [[1, 128]], channel_multiplier=0,
                             allow_small_or_imprecise_dtypes=True), inc="g")
    P.op(g, lambda e: e.iota(B.iota_p[:, :], [[1, 1]], channel_multiplier=1,
                             allow_small_or_imprecise_dtypes=True), inc="g")
    P.op(g, lambda e: e.memset(B.ones_col[:, :], 1.0), inc="g")
    P.op(g, lambda e: e.memset(B.ones11[:, :], 1.0), inc="g")
    P.op(g, lambda e: e.memset(B.ones_row_bf[:, :], 1.0), inc="g")
    P.op(g, lambda e: e.memset(B.ones11_bf[:, :], 1.0), inc="g")
    tg = P.op(g, lambda e: e.memset(B.eps_col[:, :], LN_EPS), inc="g")
    P.wait("vector", "g", tg)
    P.op("vector", lambda e: e.tensor_scalar(B.iden[:, :], B.iota_f[:, :], B.iota_p[:, 0:1],
                                             None, op0=ALU.is_equal), inc="v")
    tcz = P.op("vector", lambda e: e.tensor_scalar(B.causal[:, :], B.iota_f[:, :], B.iota_p[:, 0:1],
                                                   None, op0=ALU.is_ge), inc="v")
    P.wait("vector", "v", tcz)
    P.op("vector", lambda e: e.tensor_copy(B.mask_buf[:, 0:128], B.causal[:, :]), inc="v")
    tv = P.op("vector", lambda e: e.memset(B.mask_buf[:, 128:512], 1.0), inc="v")
    B.const_ready = tv


def build_ln_xnt(P, B, x_dram, tag):
    d = f"dx{tag}"
    v = "vector"
    sc = "scalar"
    for i in range(NT):
        bi = i % 2
        xt = B.xt[bi]
        if i >= 2:
            P.wait("sync", "v", B.xt_free[bi])
        td = P.dma("sync", xt[:, :], x_dram[ts(i), :], inc=d)
        P.wait(v, d, td)
        P.op(v, lambda e, xt=xt: e.reduce_sum(B.ln_m[:, 0:1], xt[:, :], axis=AX.X), inc="v")
        P.wait(sc, d, td)
        P.waitp(sc, B.gj_free[0])
        P.op(sc, lambda e, xt=xt: e.activation(B.gelu_junk[0][:, :], xt[:, 0:512], AF.Square,
                                               accum_out=B.ln_s2[:, 0:1]), inc="s")
        tsq = P.now("s")
        P.wait(sc, "s", tsq)
        P.waitp(sc, B.gj_free[1])
        tsq = P.op(sc, lambda e, xt=xt: e.activation(B.gelu_junk[1][:, :], xt[:, 512:1024], AF.Square,
                                                     accum_out=B.ln_sb[:, 0:1]), inc="s")
        B.gj_free = [("s", tsq), ("s", tsq)]
        tv = P.now("v")
        P.wait(v, "v", tv)
        P.op(v, lambda e: e.tensor_scalar(B.ln_m[:, 0:1], B.ln_m[:, 0:1], 1.0 / E, None, op0=ALU.mult), inc="v")
        tv = P.now("v")
        P.wait(v, "v", tv)
        P.op(v, lambda e: e.tensor_tensor(B.ln_msq[:, 0:1], B.ln_m[:, 0:1], B.ln_m[:, 0:1], op=ALU.mult), inc="v")
        P.wait(v, "s", tsq)
        tv = P.now("v")
        P.wait(v, "v", tv)
        P.op(v, lambda e: e.tensor_tensor(B.ln_s2[:, 0:1], B.ln_s2[:, 0:1], B.ln_sb[:, 0:1], op=ALU.add), inc="v")
        tv = P.now("v")
        P.wait(v, "v", tv)
        tv = P.op(v, lambda e: e.scalar_tensor_tensor(B.ln_var[:, 0:1], B.ln_s2[:, 0:1], 1.0 / E,
                                                      B.ln_msq[:, 0:1], op0=ALU.mult, op1=ALU.subtract), inc="v")
        P.wait(sc, "v", tv)
        P.op(sc, lambda e: e.activation(B.ln_u[:, 0:1], B.ln_var[:, 0:1], AF.Ln, bias=B.eps_col[:, 0:1]), inc="s")
        tssc = P.now("s")
        P.wait(sc, "s", tssc)
        tssc = P.op(sc, lambda e: e.activation(B.ln_rstd[:, 0:1], B.ln_u[:, 0:1], AF.Exp, scale=-0.5), inc="s")
        P.wait(v, "s", tssc)
        P.wait(v, "m", B.xnbf_free)
        tv = P.op(v, lambda e, xt=xt: e.tensor_scalar(B.xn_bf[:, :], xt[:, :], B.ln_m[:, 0:1],
                                                      B.ln_rstd[:, 0:1], op0=ALU.subtract, op1=ALU.mult), inc="v")
        B.xt_free[bi] = tv
        P.wait("tensor", "v", tv)
        for c in range(NC):
            pb = B.ps_bf[c % 2]
            P.waitp("tensor", B.psbf_free[c % 2])
            tm = P.op("tensor", lambda e, c=c, pb=pb: e.transpose(pb[:, 0:128], B.xn_bf[:, ts(c)], B.iden[:, :]), inc="m")
            P.wait(v, "m", tm)
            tv = P.op(v, lambda e, c=c, i=i, pb=pb: e.tensor_copy(B.xnT[:, c * T + 128 * i: c * T + 128 * (i + 1)],
                                                                  pb[:, 0:128]), inc="v")
            B.psbf_free[c % 2] = ("v", tv)
        B.xnbf_free = tm
    B.xnt_ready = P.now("v")
    B.x_stream_done = P.now(d)



def build_acts_mean(P, B, ipt_dram):
    v, sc, te, g = "vector", "scalar", "tensor", "gpsimd"
    tv = P.op(v, lambda e: e.memset(B.v_sb[:, :], 0.0), inc="v")
    ipt_r = ipt_dram.rearrange("(dc p) n -> p dc n", p=128)
    vcol_free = [0, 0]
    for j in range(64):
        bj = j % 2
        ib = B.ipt_buf[bj]
        P.waitp(g, B.iptbuf_free[bj])
        td = P.dma(g, ib.rearrange("p (dc n) -> p dc n", n=128), ipt_r[:, :, ts(j)], inc="dw")
        P.wait(te, "dw", td)
        P.wait(te, "v", tv)
        last_mm = 0
        for tb in range(8):
            pb = (tb % 2)
            pst = B.ps[pb]
            P.waitp(te, B.ps_free[pb])
            for dc in range(NC):
                last_mm = P.op(te, lambda e, dc=dc, tb=tb, ib=ib, pst=pst:
                               e.matmul(pst[:, :], ib[:, ts(dc)],
                                        B.xnT[:, dc * T + tb * 512: dc * T + (tb + 1) * 512],
                                        start=(dc == 0), stop=(dc == NC - 1)), inc="m")
            B.iptbuf_free[bj] = ("m", last_mm)
            P.wait(sc, "m", last_mm)
            P.waitp(sc, vcol_free[pb])
            tsc = P.op(sc, lambda e, pb=pb, pst=pst:
                       e.activation(B.gelu_junk[pb][:, :], pst[:, :], AF.Gelu,
                                    accum_out=B.vcol[pb][:, 0:1]), inc="s")
            B.ps_free[pb] = ("s", tsc)
            P.wait(v, "s", tsc)
            tv2 = P.op(v, lambda e, j=j, pb=pb:
                       e.tensor_tensor(B.v_sb[:, j:j + 1], B.v_sb[:, j:j + 1],
                                       B.vcol[pb][:, 0:1], op=ALU.add), inc="v")
            vcol_free[pb] = ("v", tv2)
            P.wait(v, "v", tv2)
    tv = P.op(v, lambda e: e.tensor_scalar(B.v_sb[:, :], B.v_sb[:, :], 1.0 / T, None,
                                           op0=ALU.mult), inc="v")
    B.vsb_ready = tv


def partition_total(P, B, src_col_fn, red_op, out11, nwait):
    v = "vector"
    t0 = src_col_fn()
    P.wait(v, "v", t0)
    t1 = P.op(v, lambda e: e.tensor_copy(B.fold_s[0:64, 0:1], B.fold_a[64:128, 0:1]), inc="v")
    P.wait(v, "v", t1)
    t1 = P.op(v, lambda e: e.tensor_tensor(B.fold_b[0:64, 0:1], B.fold_a[0:64, 0:1],
                                           B.fold_s[0:64, 0:1], op=red_op), inc="v")
    P.wait(v, "v", t1)
    t2 = P.op(v, lambda e: e.tensor_copy(B.fold_s[0:32, 0:1], B.fold_b[32:64, 0:1]), inc="v")
    P.wait(v, "v", t2)
    t2 = P.op(v, lambda e: e.tensor_tensor(B.fold_c[0:32, 0:1], B.fold_b[0:32, 0:1],
                                           B.fold_s[0:32, 0:1], op=red_op), inc="v")
    P.wait(v, "v", t2)
    t3 = P.op(v, lambda e: e.tensor_copy(B.fold_t[0:32, 0:1], B.fold_c[0:32, 0:1]), inc="v")
    P.wait(v, "v", t3)
    t4 = P.op(v, lambda e: e.transpose(B.fold_tt[0:32, 0:32], B.fold_t[0:32, 0:32]), inc="v")
    P.wait(v, "v", t4)
    t5 = P.op(v, lambda e: e.tensor_reduce(out11, B.fold_tt[0:1, 0:32], axis=AX.X, op=red_op),
              inc="v")
    P.wait(v, "v", t5)
    return t5


def bcast_col(P, B, src11, dst_col, psb, n_rows=128):
    te, v = "tensor", "vector"
    P.wait(te, "v", P.now("v"))
    tm = P.op(te, lambda e: e.matmul(psb[0:n_rows, 0:1], B.ones_row[0:1, 0:n_rows],
                                     src11, start=True, stop=True), inc="m")
    P.wait(v, "m", tm)
    tv = P.op(v, lambda e: e.tensor_copy(dst_col, psb[0:n_rows, 0:1]), inc="v")
    P.wait(v, "v", tv)
    return tv


def build_bisect(P, B, src, F, k, thr_rep):
    v = "vector"
    psb = B.bcb
    def mx():
        return P.op(v, lambda e: e.tensor_reduce(B.fold_a[:, 0:1], src, axis=AX.X, op=ALU.max), inc="v")
    partition_total(P, B, mx, ALU.max, B.bis[0:1, 1:2], 0)

    def mn():
        t0 = P.op(v, lambda e: e.tensor_scalar(B.bis_neg[:, 0:F], src, -1.0, None, op0=ALU.mult), inc="v")
        P.wait(v, "v", t0)
        return P.op(v, lambda e: e.tensor_reduce(B.fold_a[:, 0:1], B.bis_neg[:, 0:F], axis=AX.X, op=ALU.max), inc="v")
    partition_total(P, B, mn, ALU.max, B.bis[0:1, 2:3], 0)
    t = P.op(v, lambda e: e.tensor_scalar(B.bis[0:1, 0:1], B.bis[0:1, 2:3], -1.0, -1.0,
                                          op0=ALU.mult, op1=ALU.add), inc="v")
    P.wait(v, "v", t)
    t = P.op(v, lambda e: e.tensor_scalar(B.bis[0:1, 1:2], B.bis[0:1, 1:2], 1.0, None,
                                          op0=ALU.add), inc="v")
    P.wait(v, "v", t)
    for it in range(BISECT_ITERS):
        t = P.op(v, lambda e: e.tensor_tensor(B.bis[0:1, 3:4], B.bis[0:1, 1:2], B.bis[0:1, 0:1],
                                              op=ALU.subtract), inc="v")
        P.wait(v, "v", t)
        t = P.op(v, lambda e: e.tensor_scalar(B.bis[0:1, 3:4], B.bis[0:1, 3:4], 0.5, None,
                                              op0=ALU.mult), inc="v")
        P.wait(v, "v", t)
        t = P.op(v, lambda e: e.tensor_tensor(B.bis[0:1, 4:5], B.bis[0:1, 0:1], B.bis[0:1, 3:4],
                                              op=ALU.add), inc="v")
        P.wait(v, "v", t)
        bcast_col(P, B, B.bis[0:1, 4:5], B.bis_mid[:, 0:1], psb)
        t = P.op(v, lambda e: e.tensor_scalar(B.bis_ge[:, 0:F], src, B.bis_mid[:, 0:1], None,
                                              op0=ALU.is_ge), inc="v")
        P.wait(v, "v", t)

        def cnt():
            return P.op(v, lambda e: e.tensor_reduce(B.fold_a[:, 0:1], B.bis_ge[:, 0:F],
                                                     axis=AX.X, op=ALU.add), inc="v")
        partition_total(P, B, cnt, ALU.add, B.bis[0:1, 5:6], 0)
        t = P.op(v, lambda e: e.tensor_scalar(B.bis[0:1, 6:7], B.bis[0:1, 5:6], k - 0.5, None,
                                              op0=ALU.is_ge), inc="v")
        P.wait(v, "v", t)
        t = P.op(v, lambda e: e.tensor_tensor(B.bis[0:1, 7:8], B.bis[0:1, 6:7], B.bis[0:1, 3:4],
                                              op=ALU.mult), inc="v")
        P.wait(v, "v", t)
        t = P.op(v, lambda e: e.tensor_tensor(B.bis[0:1, 0:1], B.bis[0:1, 0:1], B.bis[0:1, 7:8],
                                              op=ALU.add), inc="v")
        P.wait(v, "v", t)
        t = P.op(v, lambda e: e.tensor_tensor(B.bis[0:1, 1:2], B.bis[0:1, 4:5], B.bis[0:1, 7:8],
                                              op=ALU.add), inc="v")
        P.wait(v, "v", t)
    return bcast_col(P, B, B.bis[0:1, 0:1], thr_rep, psb)



def build_ffn_chain(P, B, piwt, pv, oiwt, op_dram):
    v, sc, te, g = "vector", "scalar", "tensor", "gpsimd"

    def matvec_stream(lhs_cols, w_dram, n_chunks, n_pb, tile_cols, accum_rows=1):
        for c in range(n_chunks):
            wb = B.wstream[c % 2]
            P.waitp(g, B.wstream_free[c % 2])
            td = P.dma(g, wb[:, 0:tile_cols], w_dram[ts(c), 0:tile_cols], inc="dw")
            P.wait(te, "dw", td)
            last = 0
            for pb in range(n_pb):
                if c == 0:
                    P.waitp(te, B.ps_free[pb])
                last = P.op(te, lambda e, c=c, pb=pb, wb=wb:
                            e.matmul(B.ps[pb][0:1, 0:512], lhs_cols[:, c:c + 1],
                                     wb[:, pb * 512:(pb + 1) * 512],
                                     start=(c == 0), stop=(c == n_chunks - 1)), inc="m")
            B.wstream_free[c % 2] = ("m", last)
        return P.now("m")

    P.wait(te, "v", B.maskedv_ready)
    tmm = matvec_stream(B.masked_bf, piwt, 64, 8, 4096)
    for pb in range(8):
        rb = B.rowbuf[pb % 2]
        P.wait(sc, "m", tmm)
        P.waitp(sc, B.rowbuf_free[pb % 2])
        tsc = P.op(sc, lambda e, pb=pb, rb=rb: e.activation(rb[0:1, :],
                                                            B.ps[pb][0:1, 0:512], AF.Gelu), inc="s")
        B.ps_free[pb] = ("s", tsc)
        P.wait(te, "s", tsc)
        if pb == 0:
            P.waitp(te, B.ps_free[0])
        for j in range(4):
            tm = P.op(te, lambda e, pb=pb, j=j, rb=rb:
                      e.matmul(B.ps[0][:, pb * 4 + j: pb * 4 + j + 1], rb[0:1, ts(j)],
                               B.ones11[0:1, 0:1], start=True, stop=True), inc="m")
        B.rowbuf_free[pb % 2] = ("m", tm)
    P.wait(v, "m", tm)
    tv = P.op(v, lambda e: e.tensor_copy(B.p_sb[:, :], B.ps[0][:, 0:32]), inc="v")
    B.ps_free[0] = ("v", tv)
    P.wait(v, "v", tv)

    build_bisect(P, B, B.p_sb[:, :], 32, K_PROC, B.thr2_rep[:, 0:1])
    def mx():
        return P.op(v, lambda e: e.tensor_reduce(B.fold_a[:, 0:1], B.p_sb[:, :], axis=AX.X,
                                                 op=ALU.max), inc="v")
    partition_total(P, B, mx, ALU.max, B.sc4[0:1, 0:1], 0)
    t = P.op(v, lambda e: e.tensor_scalar(B.sc4[0:1, 1:2], B.sc4[0:1, 0:1], -1.0, None,
                                          op0=ALU.mult), inc="v")
    P.wait(v, "v", t)
    bcast_col(P, B, B.sc4[0:1, 1:2], B.negpmax[:, 0:1], B.bcb)
    P.wait(sc, "v", P.now("v"))
    t = P.op(sc, lambda e: e.activation(B.e_sb[:, :], B.p_sb[:, :], AF.Exp,
                                        bias=B.negpmax[:, 0:1]), inc="s")
    P.wait(v, "s", t)
    t = P.op(v, lambda e: e.tensor_scalar(B.mask_f[:, 0:32], B.p_sb[:, :], B.thr2_rep[:, 0:1],
                                          None, op0=ALU.is_ge), inc="v")
    P.wait(v, "v", t)
    t = P.op(v, lambda e: e.tensor_tensor(B.e_sb[:, :], B.e_sb[:, :], B.mask_f[:, 0:32],
                                          op=ALU.mult), inc="v")
    P.wait(v, "v", t)
    t = P.op(v, lambda e: e.tensor_copy(B.e_bf[:, :], B.e_sb[:, :]), inc="v")
    P.wait(v, "v", t)
    def sm():
        return P.op(v, lambda e: e.tensor_reduce(B.fold_a[:, 0:1], B.e_sb[:, :], axis=AX.X,
                                                 op=ALU.add), inc="v")
    partition_total(P, B, sm, ALU.add, B.sc4[0:1, 2:3], 0)
    t = P.op(v, lambda e: e.reciprocal(B.sc4[0:1, 3:4], B.sc4[0:1, 2:3]), inc="v")
    P.wait(v, "v", t)

    P.wait(te, "v", P.now("v"))
    for c in range(32):
        wb = B.pv_buf[c % 2]
        P.waitp(g, B.pvbuf_free[c % 2])
        td = P.dma(g, wb[:, :], pv[ts(c), :], inc="dw")
        P.wait(te, "dw", td)
        if c == 0:
            P.waitp(te, B.ps_free[1])
        tm = P.op(te, lambda e, c=c, wb=wb: e.matmul(B.ps[1][0:1, 0:512], B.e_bf[:, c:c + 1],
                                                     wb[:, :], start=(c == 0), stop=(c == 31)), inc="m")
        B.pvbuf_free[c % 2] = ("m", tm)
    P.wait(v, "m", tm)
    tv = P.op(v, lambda e: e.tensor_scalar(B.agg_sb[0:1, :], B.ps[1][0:1, 0:512],
                                           B.sc4[0:1, 3:4], None, op0=ALU.mult), inc="v")
    B.ps_free[1] = ("v", tv)
    P.wait(v, "v", tv)
    P.wait(te, "v", tv)
    P.waitp(te, B.ps_free[2])
    for j in range(4):
        tm = P.op(te, lambda e, j=j: e.matmul(B.ps[2][:, j:j + 1], B.agg_sb[0:1, ts(j)],
                                              B.ones11_bf[0:1, 0:1], start=True, stop=True), inc="m")
    P.wait(v, "m", tm)
    tv = P.op(v, lambda e: e.tensor_copy(B.agg_col_bf[:, :], B.ps[2][:, 0:4]), inc="v")
    B.ps_free[2] = ("v", tv)
    P.wait(v, "v", tv)

    oiwt_r = oiwt.rearrange("(dc p) n -> p dc n", p=128)
    for half in range(2):
        for dc in range(4):
            wb = B.wstream[dc % 2]
            P.waitp(g, B.wstream_free[dc % 2])
            td = P.dma(g, wb[:, 0:4096], oiwt_r[:, dc, half * 4096:(half + 1) * 4096], inc="dw")
            P.wait(te, "dw", td)
            last = 0
            for pb in range(8):
                if dc == 0:
                    P.waitp(te, B.ps_free[pb])
                last = P.op(te, lambda e, dc=dc, pb=pb, wb=wb:
                            e.matmul(B.ps[pb][0:1, 0:512], B.agg_col_bf[:, dc:dc + 1],
                                     wb[:, pb * 512:(pb + 1) * 512],
                                     start=(dc == 0), stop=(dc == 3)), inc="m")
            B.wstream_free[dc % 2] = ("m", last)
        for pb in range(8):
            rb = B.rowbuf[pb % 2]
            P.wait(sc, "m", last)
            P.waitp(sc, B.rowbuf_free[pb % 2])
            tsc = P.op(sc, lambda e, pb=pb, rb=rb:
                       e.activation(rb[0:1, :], B.ps[pb][0:1, 0:512], AF.Gelu), inc="s")
            B.ps_free[pb] = ("s", tsc)
            P.wait(te, "s", tsc)
            if pb == 0:
                P.waitp(te, B.ps_free[0])
            for j in range(4):
                tm = P.op(te, lambda e, pb=pb, j=j, rb=rb:
                          e.matmul(B.ps[0][:, pb * 4 + j: pb * 4 + j + 1], rb[0:1, ts(j)],
                                   B.ones11[0:1, 0:1], start=True, stop=True), inc="m")
            B.rowbuf_free[pb % 2] = ("m", tm)
        P.wait(v, "m", tm)
        tv = P.op(v, lambda e, half=half: e.tensor_copy(B.o_sb[:, half * 32:(half + 1) * 32],
                                                        B.ps[0][:, 0:32]), inc="v")
        B.ps_free[0] = ("v", tv)
        P.wait(v, "v", tv)

    build_bisect(P, B, B.o_sb[:, :], 64, K_OUT, B.thr3_rep[:, 0:1])
    t = P.op(v, lambda e: e.tensor_scalar(B.mask_f[:, :], B.o_sb[:, :], B.thr3_rep[:, 0:1],
                                          None, op0=ALU.is_ge), inc="v")
    P.wait(v, "v", t)
    t = P.op(v, lambda e: e.tensor_tensor(B.masked_bf[:, :], B.o_sb[:, :], B.mask_f[:, :],
                                          op=ALU.mult), inc="v")
    P.wait(v, "v", t)
    P.wait(te, "v", t)
    for c in range(64):
        wb = B.wstream[c % 2]
        P.waitp(g, B.wstream_free[c % 2])
        td = P.dma(g, wb[:, 0:1024], op_dram[ts(c), :], inc="dw")
        P.wait(te, "dw", td)
        last = 0
        for pb in range(2):
            if c == 0:
                P.waitp(te, B.ps_free[pb])
            last = P.op(te, lambda e, c=c, pb=pb, wb=wb:
                        e.matmul(B.ps[pb][0:1, 0:512], B.masked_bf[:, c:c + 1],
                                 wb[:, pb * 512:(pb + 1) * 512],
                                 start=(c == 0), stop=(c == 63)), inc="m")
        B.wstream_free[c % 2] = ("m", last)
    P.wait(v, "m", last)
    tv = P.op(v, lambda e: e.tensor_copy(B.ffn_row[0:1, 0:512], B.ps[0][0:1, 0:512]), inc="v")
    B.ps_free[0] = ("v", tv)
    tv = P.op(v, lambda e: e.tensor_copy(B.ffn_row[0:1, 512:1024], B.ps[1][0:1, 0:512]), inc="v")
    B.ps_free[1] = ("v", tv)
    P.wait(v, "v", tv)
    P.wait(te, "v", tv)
    P.waitp(te, B.ps_free[2])
    P.waitp(te, B.ps_free[3])
    tm2 = []
    for j in range(2):
        tm = P.op(te, lambda e, j=j: e.matmul(B.ps[2 + j][:, 0:512], B.ones_row_bf[0:1, :],
                                              B.ffn_row[0:1, j * 512:(j + 1) * 512],
                                              start=True, stop=True), inc="m")
        tm2.append(tm)
    P.wait(v, "m", tm)
    for j in range(2):
        tv = P.op(v, lambda e, j=j: e.tensor_copy(B.ffn_rep[:, j * 512:(j + 1) * 512],
                                                  B.ps[2 + j][:, 0:512]), inc="v")
        B.ps_free[2 + j] = ("v", tv)
    P.wait(v, "v", tv)
    B.ffn_ready = tv



def barrier(P):
    tgts = [(n, P.now(n)) for n in ("m", "v", "s", "g")]
    for eng in ENGINES:
        for n, t in tgts:
            if t:
                P.wait(eng, n, t)


def build_linattn(P, B, w_attn):
    v, sc, te, g = "vector", "scalar", "tensor", "gpsimd"
    wr = w_attn.rearrange("(ec p) n -> p ec n", p=128)
    for hp in range(int(os.environ.get("DEVHP", "8"))):
        barrier(P)
        wsl = B.wstream[0]
        for j, base in enumerate((0, E, 2 * E)):
            P.dma(g, wsl[:, j * 1024:(j + 1) * 1024].rearrange("p (ec n) -> p ec n", n=128),
                  wr[:, :, base + hp * 128: base + (hp + 1) * 128], inc="dw")
        tdw = P.now("dw")
        P.wait(te, "dw", tdw)
        gj_war = [0, 0]
        for which, dst in ((0, B.qT_hp), (1, B.kT_hp)):
            for tb in range(8):
                pb = tb % 2
                pst = B.ps[pb]
                P.waitp(te, B.ps_free[pb])
                for ec in range(NC):
                    tm = P.op(te, lambda e, ec=ec, tb=tb, which=which, pst=pst:
                              e.matmul(pst[:, :], wsl[:, which * 1024 + ec * 128: which * 1024 + (ec + 1) * 128],
                                       B.xnT[:, ec * T + tb * 512: ec * T + (tb + 1) * 512],
                                       start=(ec == 0), stop=(ec == NC - 1)), inc="m")
                P.wait(v, "m", tm)
                P.waitp(v, gj_war[pb])
                t1 = P.op(v, lambda e, pb=pb, pst=pst: e.tensor_scalar(B.gelu_junk[pb][:, :], pst[:, :], 0.0,
                                                                      None, op0=ALU.min), inc="v")
                P.wait(sc, "v", t1)
                t2 = P.op(sc, lambda e, pb=pb: e.activation(B.gelu_junk[pb][:, :], B.gelu_junk[pb][:, :],
                                                            AF.Exp), inc="s")
                P.wait(v, "s", t2)
                t3 = P.op(v, lambda e, pb=pb, pst=pst, dst=dst, tb=tb:
                          e.scalar_tensor_tensor(dst[:, tb * 512:(tb + 1) * 512], pst[:, :], 0.0,
                                                 B.gelu_junk[pb][:, :], op0=ALU.max, op1=ALU.add), inc="v")
                B.ps_free[pb] = ("v", t3)
                gj_war[pb] = ("v", t3)
        tmem = P.op(v, lambda e: e.memset(B.Vt_hp[:, :], 1.0), inc="v")
        P.wait(v, "v", tmem)
        P.wait(te, "v", tmem)
        for ti in range(NT):
            pb = ti % 2
            pst = B.ps[pb]
            P.waitp(te, B.ps_free[pb])
            for ec in range(NC):
                tm = P.op(te, lambda e, ec=ec, ti=ti, pst=pst:
                          e.matmul(pst[:, 0:128], B.xnT[:, ec * T + ti * 128: ec * T + (ti + 1) * 128],
                                   wsl[:, 2048 + ec * 128: 2048 + (ec + 1) * 128],
                                   start=(ec == 0), stop=(ec == NC - 1)), inc="m")
            P.wait(v, "m", tm)
            tv = P.op(v, lambda e, ti=ti, pst=pst:
                      e.tensor_copy(B.Vt_hp[:, ti * 130: ti * 130 + 130].rearrange("p (h d) -> p h d", h=2)[:, :, 0:64],
                                    pst[:, 0:128].rearrange("p (h d) -> p h d", h=2)), inc="v")
            B.ps_free[pb] = ("v", tv)
        barrier(P)
        psS = [B.ps[6], B.ps[7]]
        psc = [B.ps[4], B.ps[5]]
        psy = [B.ps[0], B.ps[1]]
        psr = B.ps[2]
        psk = B.ps_bf[1]
        knat = B.exbp[0][:, 0:512]
        ab = [B.exbp[0][:, 512:1024], B.exbp[1][:, 0:512]]
        free = {"S0": 0, "S1": 0, "y0": 0, "y1": 0, "r": 0,
                "knat": 0, "ab0": 0, "ab1": 0, "Ssb0": 0, "Ssb1": 0, "rs": 0, "den": 0,
                "abm0": 0, "abm1": 0, "knsb": 0, "rrow": 0, "sc0": 0, "sc1": 0,
                "y0m": 0, "y1m": 0}
        for g_ in range(8):
            P.waitp(te, free["knat"])
            for j_ in range(4):
                tkn = P.op(te, lambda e, g_=g_, j_=j_:
                           e.transpose(psk[:, j_ * 128:(j_ + 1) * 128],
                                       B.kT_hp[:, g_ * 512 + j_ * 128: g_ * 512 + (j_ + 1) * 128],
                                       B.iden[:, :]), inc="m")
            P.wait(v, "m", tkn)
            P.waitp(v, free["knsb"])
            tknv = P.op(v, lambda e, knat=knat: e.tensor_copy(knat, psk[:, 0:512]), inc="v")
            free["knat"] = ("v", tknv)
            for h in range(2):
                hb = h * 64
                for j_ in range(4):
                    P.waitp(te, free[f"sc{h}"])
                    tsc_ = P.op(te, lambda e, g_=g_, j_=j_, h=h, hb=hb:
                                e.matmul(psc[h][:, :],
                                         B.kT_hp[hb:hb + 64, g_ * 512 + j_ * 128: g_ * 512 + (j_ + 1) * 128],
                                         B.qT_hp[hb:hb + 64, g_ * 512:(g_ + 1) * 512],
                                         start=True, stop=True, tile_position=(hb, 0)), inc="m")
                    P.wait(v, "m", tsc_)
                    if j_ == 0:
                        P.waitp(v, free[f"abm{h}"])
                    tab = P.op(v, lambda e, j_=j_, h=h:
                               e.tensor_tensor(ab[h][:, j_ * 128: 512],
                                               psc[h][:, j_ * 128: 512],
                                               B.mask_buf[:, 0:512 - j_ * 128], op=ALU.mult), inc="v")
                    free[f"sc{h}"] = ("v", tab)
                    P.wait(te, "v", tab)
                    if j_ == 0:
                        P.waitp(te, free[f"y{h}"])
                    tnum = P.op(te, lambda e, g_=g_, j_=j_, h=h:
                                e.matmul(psy[h][0:65, j_ * 128: 512],
                                         B.Vt_hp[:, (g_ * 4 + j_) * 130 + h * 65: (g_ * 4 + j_) * 130 + h * 65 + 65],
                                         ab[h][:, j_ * 128: 512],
                                         start=(j_ == 0), stop=(g_ == 0 and j_ == 3)), inc="m")
                if g_ > 0:
                    P.waitp(te, free[f"Ssb{h}"])
                    tnum = P.op(te, lambda e, g_=g_, h=h, hb=hb:
                                e.matmul(psy[h][0:65, 0:512], B.S_sb[hb:hb + 64, 0:65],
                                         B.qT_hp[hb:hb + 64, g_ * 512:(g_ + 1) * 512],
                                         start=False, stop=True, tile_position=(hb, 0)), inc="m")
                free[f"abm{h}"] = ("m", tnum)
                free[f"y{h}m"] = ("m", tnum)
                P.wait(te, "v", tknv)
                P.waitp(te, free[f"S{h}"])
                for j_ in range(4):
                    tS = P.op(te, lambda e, g_=g_, j_=j_, h=h, hb=hb, knat=knat:
                              e.matmul(psS[h][0:64, 0:65],
                                       knat[:, j_ * 128 + hb: j_ * 128 + hb + 64],
                                       B.Vt_hp[:, (g_ * 4 + j_) * 130 + h * 65: (g_ * 4 + j_) * 130 + h * 65 + 65],
                                       start=(g_ == 0 and j_ == 0), stop=(g_ == 7 and j_ == 3)), inc="m")
                if h == 1:
                    free["knsb"] = ("m", tS)
                P.wait(v, "m", tS)
                tsnap = P.op(v, lambda e, h=h, hb=hb: e.tensor_copy(B.S_sb[hb:hb + 64, 0:65],
                                                                   psS[h][0:64, 0:65]), inc="v")
                free[f"S{h}"] = ("v", tsnap)
                free[f"Ssb{h}"] = ("v", tsnap)
            for h in range(2):
                hb = h * 64
                P.wait(v, "m", free[f"y{h}m"][1])
                P.waitp(v, free["den"])
                td1 = P.op(v, lambda e, h=h: e.tensor_scalar(B.den_sb[0:1, 0:512], psy[h][64:65, 0:512],
                                                             LA_EPS, None, op0=ALU.add), inc="v")
                P.wait(sc, "v", td1)
                P.waitp(sc, free["rrow"])
                td2 = P.op(sc, lambda e: e.activation(B.den_sb[0:1, 0:512], B.den_sb[0:1, 0:512],
                                                      AF.Ln), inc="s")
                P.wait(sc, "s", td2)
                td2 = P.op(sc, lambda e: e.activation(B.rrow_bf[0:1, 0:512], B.den_sb[0:1, 0:512],
                                                      AF.Exp, scale=-1.0), inc="s")
                free["den"] = ("s", td2)
                P.wait(te, "s", td2)
                P.waitp(te, free["r"])
                tr = P.op(te, lambda e: e.matmul(psr[0:64, 0:512], B.ones_row_bf[0:1, 0:64],
                                                 B.rrow_bf[0:1, 0:512], start=True, stop=True), inc="m")
                free["rrow"] = ("m", tr)
                P.wait(sc, "m", tr)
                P.waitp(sc, free["rs"])
                trs = P.op(sc, lambda e: e.copy(B.r_sb[0:64, 0:512], psr[0:64, 0:512]), inc="s")
                free["r"] = ("s", trs)
                P.wait(v, "s", trs)
                ty = P.op(v, lambda e, h=h, hb=hb, g_=g_, hp=hp:
                          e.tensor_tensor(B.yT[hb:hb + 64, hp * T + g_ * 512: hp * T + (g_ + 1) * 512],
                                          psy[h][0:64, 0:512], B.r_sb[0:64, 0:512],
                                          op=ALU.mult), inc="v")
                free[f"y{h}"] = ("v", ty)
                free["rs"] = ("v", ty)
    B.yT_ready = P.now("v")


def build_wproj_residual(P, B, w_dram, x_dram, x1_dram):
    v, sc, te, g, sy = "vector", "scalar", "tensor", "gpsimd", "sync"
    barrier(P)
    wr = w_dram.rearrange("(ec p) n -> p ec n", p=128)
    P.dma(g, B.wstream[0][:, :].rearrange("p (ec n) -> p ec n", n=1024), wr[:, 0:4, :], inc="dw")
    P.dma(g, B.wstream[1][:, :].rearrange("p (ec n) -> p ec n", n=1024), wr[:, 4:8, :], inc="dw")
    tdw = P.now("dw")
    P.wait(te, "dw", tdw)
    d = "dxf"
    for i in range(NT):
        bi = i % 2
        xt = B.xt[bi]
        P.waitp(sy, B.xt_free2[bi])
        td = P.dma(sy, xt[:, :], x_dram[ts(i), :], inc=d)
        for half in range(2):
            pb = half
            pst = B.ps[pb]
            P.waitp(te, B.ps_free[pb])
            for cc in range(NC):
                tm = P.op(te, lambda e, cc=cc, i=i, half=half, pst=pst:
                          e.matmul(pst[:, :], B.yT[:, cc * T + i * 128: cc * T + (i + 1) * 128],
                                   B.wstream[cc // 4][:, (cc % 4) * 1024 + half * 512: (cc % 4) * 1024 + (half + 1) * 512],
                                   start=(cc == 0), stop=(cc == NC - 1)), inc="m")
            P.wait(v, "m", tm)
            P.wait(v, d, td)
            t1 = P.op(v, lambda e, bi=bi, half=half, pst=pst:
                      e.tensor_tensor(B.x1_sb[bi][:, half * 512:(half + 1) * 512], pst[:, :],
                                      B.xt[bi][:, half * 512:(half + 1) * 512], op=ALU.add), inc="v")
            P.wait(v, "v", t1)
            t2 = P.op(v, lambda e, bi=bi, half=half:
                      e.tensor_tensor(B.x1_sb[bi][:, half * 512:(half + 1) * 512],
                                      B.x1_sb[bi][:, half * 512:(half + 1) * 512],
                                      B.ffn_rep[:, half * 512:(half + 1) * 512], op=ALU.add), inc="v")
            B.ps_free[pb] = ("v", t2)
        tvv = P.now("v")
        P.wait(sy, "v", tvv)
        tso = P.dma(sy, x1_dram[ts(i), :], B.x1_sb[bi][:, :], inc="dx1")
        B.xt_free2[bi] = ("dx1", tso)
    P.wait(sy, "dx1", P.now("dx1"))



def build_rope_tables(P, B, nrope_dram):
    v, sc, te, g, sy = "vector", "scalar", "tensor", "gpsimd", "sync"
    barrier(P)
    td = P.dma(sy, B.nr_sb[0:1, 0:1], nrope_dram[0:1, 0:1], inc="dh")
    P.wait(sc, "dh", td)
    t = P.op(sc, lambda e: e.activation(B.nr_sb[0:1, 1:2], B.nr_sb[0:1, 0:1], AF.Sigmoid), inc="s")
    P.wait(v, "s", t)
    t = P.op(v, lambda e: e.tensor_scalar(B.nr_sb[0:1, 2:3], B.nr_sb[0:1, 1:2], 3400.0, 8300.0,
                                          op0=ALU.mult, op1=ALU.add), inc="v")
    P.wait(sc, "v", t)
    t = P.op(sc, lambda e: e.activation(B.nr_sb[0:1, 3:4], B.nr_sb[0:1, 2:3], AF.Ln), inc="s")
    tg = P.op(g, lambda e: e.iota(B.jrow[0:1, 0:16], [[1, 16]], channel_multiplier=0,
                                  allow_small_or_imprecise_dtypes=True), inc="g")
    P.wait(v, "s", t)
    P.wait(v, "g", tg)
    t = P.op(v, lambda e: e.tensor_scalar(B.jrow[0:1, 16:32], B.jrow[0:1, 0:16], B.nr_sb[0:1, 3:4],
                                          -1.0 / 16.0, op0=ALU.mult, op1=ALU.mult), inc="v")
    P.wait(sc, "v", t)
    t = P.op(sc, lambda e: e.activation(B.jrow[0:1, 32:48], B.jrow[0:1, 16:32], AF.Exp), inc="s")
    P.wait(v, "s", t)
    t = P.op(v, lambda e: e.tensor_scalar(B.jrow[0:1, 48:64], B.jrow[0:1, 32:48], 2.0 * PI / (T - 1), None,
                                          op0=ALU.mult), inc="v")
    P.wait(v, "v", t)
    P.waitp(te, B.ps_free[5])
    P.wait(te, "v", t)
    for i in range(NT):
        pi_ = i % 2
        P.waitp(g, B.posrow_free[pi_])
        tg = P.op(g, lambda e, i=i, pi_=pi_: e.iota(B.posrow[pi_][0:1, :], [[1, 128]], base=i * 128,
                                                    channel_multiplier=0,
                                                    allow_small_or_imprecise_dtypes=True), inc="g")
        P.wait(te, "g", tg)
        tm = P.op(te, lambda e, i=i, pi_=pi_: e.matmul(B.ps[5][:, i * 16:(i + 1) * 16],
                                                       B.posrow[pi_][0:1, :], B.jrow[0:1, 48:64],
                                                       start=True, stop=True), inc="m")
        B.posrow_free[pi_] = ("m", tm)
    tm = P.now("m")
    P.wait(v, "m", tm)
    t = P.op(v, lambda e: e.tensor_scalar(B.gelu_junk[0][:, :], B.ps[5][:, :], PI, None,
                                          op0=ALU.is_gt), inc="v")
    P.wait(v, "v", t)
    t = P.op(v, lambda e: e.scalar_tensor_tensor(B.gelu_junk[1][:, :], B.gelu_junk[0][:, :],
                                                 -2.0 * PI, B.ps[5][:, :], op0=ALU.mult,
                                                 op1=ALU.add), inc="v")
    B.ps_free[5] = ("v", t)
    P.wait(sc, "v", t)
    ts1 = P.op(sc, lambda e: e.activation(B.sin_nat[:, :], B.gelu_junk[1][:, :], AF.Sin), inc="s")
    P.wait(v, "v", t)
    t = P.op(v, lambda e: e.tensor_scalar(B.gelu_junk[0][:, :], B.gelu_junk[1][:, :], PI / 2, None,
                                          op0=ALU.add), inc="v")
    P.wait(v, "v", t)
    t = P.op(v, lambda e: e.tensor_scalar(B.cos_nat[:, :], B.gelu_junk[0][:, :], PI, None,
                                          op0=ALU.is_gt), inc="v")
    P.wait(v, "v", t)
    t = P.op(v, lambda e: e.scalar_tensor_tensor(B.cos_nat[:, :], B.cos_nat[:, :], -2.0 * PI,
                                                 B.gelu_junk[0][:, :], op0=ALU.mult,
                                                 op1=ALU.add), inc="v")
    P.wait(sc, "v", t)
    tc = P.op(sc, lambda e: e.activation(B.cos_nat[:, :], B.cos_nat[:, :], AF.Sin), inc="s")
    B.rope_ready = tc


def build_l1_attn(P, B, wq, wk, wv, bq, bk, bv):
    v, sc, te, g, sy = "vector", "scalar", "tensor", "gpsimd", "sync"
    wqr = wq.rearrange("(ec p) n -> p ec n", p=128)
    wkr = wk.rearrange("(ec p) n -> p ec n", p=128)
    wvr = wv.rearrange("(ec p) n -> p ec n", p=128)
    for hp in range(int(os.environ.get("DEVHP1", "8"))):
        barrier(P)
        wsl = B.wstream[0]
        for j, wr in enumerate((wqr, wkr, wvr)):
            P.dma(g, wsl[:, j * 1024:(j + 1) * 1024].rearrange("p (ec n) -> p ec n", n=128),
                  wr[:, :, hp * 128:(hp + 1) * 128], inc="dw")
        for j, bd in enumerate((bq, bk, bv)):
            P.dma(g, B.ffn_row[0:1, j * 128:(j + 1) * 128], bd[0:1, hp * 128:(hp + 1) * 128], inc="dh")
        tdw = P.now("dw")
        tdh = P.now("dh")
        P.wait(te, "dh", tdh)
        P.waitp(te, B.ps_free[4])
        tm = P.op(te, lambda e: e.matmul(B.ps[4][:, 0:384], B.ones_row_bf[0:1, :], B.ffn_row[0:1, 0:384],
                                         start=True, stop=True), inc="m")
        P.wait(v, "m", tm)
        tbr = P.op(v, lambda e: e.tensor_copy(B.brep[:, :], B.ps[4][:, 0:384]), inc="v")
        B.ps_free[4] = ("v", tbr)
        tmem = P.op(v, lambda e: e.memset(B.qmax2[:, :], 0.0), inc="v")
        P.wait(te, "dw", tdw)
        gj_war = 0
        qn4 = B.exbp[0][:, :].bitcast(F32)
        qbf4 = B.exbp[1][:, 0:512]
        rs4 = [B.gelu_junk[0][:, k * 128:(k + 1) * 128].rearrange("p (a h d) -> p a h d", a=4, h=2)
               for k in range(4)]
        pending_tp = []

        def flush_tp():
            for (dstq, tq_, tcast) in pending_tp:
                P.wait(te, "v", tcast)
                for j in range(4):
                    pbq = j % 2
                    P.waitp(te, B.psbf_free[pbq])
                    tt = P.op(te, lambda e, j=j, pbq=pbq, qbf4=qbf4:
                              e.transpose(B.ps_bf[pbq][:, 0:128], qbf4[:, j * 128:(j + 1) * 128],
                                          B.iden[:, :]), inc="m")
                    B.qbf_free = ("m", tt)
                    P.wait(v, "m", tt)
                    tv2 = P.op(v, lambda e, pbq=pbq, tq_=tq_, j=j, dstq=dstq:
                               e.tensor_copy(dstq[:, (tq_ * 4 + j) * 128:(tq_ * 4 + j + 1) * 128],
                                             B.ps_bf[pbq][:, 0:128]), inc="v")
                    B.psbf_free[pbq] = ("v", tv2)
            del pending_tp[:]
        for which, dst in ((0, B.qT_hp), (1, B.kT_hp)):
            for tq_ in range(8):
                tev = 0
                for j in range(4):
                    ti = tq_ * 4 + j
                    pb = j % 2
                    pst = B.ps[pb]
                    P.waitp(te, B.ps_free[pb])
                    for ec in range(NC):
                        tm = P.op(te, lambda e, ec=ec, ti=ti, which=which, pst=pst:
                                  e.matmul(pst[:, 0:128],
                                           B.xnT[:, ec * T + ti * 128: ec * T + (ti + 1) * 128],
                                           wsl[:, which * 1024 + ec * 128: which * 1024 + (ec + 1) * 128],
                                           start=(ec == 0), stop=(ec == NC - 1)), inc="m")
                    P.wait(v, "m", tm)
                    if j == 0:
                        P.wait(v, "v", tbr)
                        P.waitp(v, B.qn_free)
                    tev = P.op(v, lambda e, pst=pst, which=which, j=j, qn4=qn4:
                               e.tensor_tensor(qn4[:, j * 128:(j + 1) * 128], pst[:, 0:128],
                                               B.brep[:, which * 128:(which + 1) * 128], op=ALU.add), inc="v")
                    B.ps_free[pb] = ("v", tev)
                flush_tp()
                qv = qn4.rearrange("p (a h d) -> p a h d", a=4, h=2)
                P.wait(v, "v", tev)
                P.wait(v, "s", B.rope_ready)
                P.waitp(v, gj_war)
                for a_ in range(4):
                    ti = tq_ * 4 + a_
                    cosb = B.cos_nat[:, ti * 16:(ti + 1) * 16].unsqueeze(1).to_broadcast([128, 2, 16])
                    sinb = B.sin_nat[:, ti * 16:(ti + 1) * 16].unsqueeze(1).to_broadcast([128, 2, 16])
                    x1s = qv[:, a_, :, 32:48]
                    x2s = qv[:, a_, :, 48:64]
                    for k_, aa, b_ in ((0, x1s, cosb), (1, x2s, sinb), (2, x1s, sinb), (3, x2s, cosb)):
                        t = P.op(v, lambda e, k_=k_, aa=aa, b_=b_, a_=a_:
                                 e.tensor_tensor(rs4[k_][:, a_], aa, b_, op=ALU.mult), inc="v")
                P.wait(v, "v", t)
                for a_ in range(4):
                    t = P.op(v, lambda e, qv=qv, a_=a_: e.tensor_tensor(qv[:, a_, :, 32:48], rs4[0][:, a_],
                                                                       rs4[1][:, a_], op=ALU.subtract), inc="v")
                    t = P.op(v, lambda e, qv=qv, a_=a_: e.tensor_tensor(qv[:, a_, :, 48:64], rs4[2][:, a_],
                                                                       rs4[3][:, a_], op=ALU.add), inc="v")
                P.wait(v, "v", t)
                P.wait(sc, "v", t)
                for h in range(2):
                    tq2 = P.op(sc, lambda e, h=h, qv=qv:
                               e.activation(B.gelu_junk[1][:, 0:256].rearrange("p (a d) -> p a d", a=4),
                                            qv[:, :, h, :], AF.Square,
                                            accum_out=B.qn2[:, h:h + 1]), inc="s")
                    P.wait(sc, "s", tq2)
                P.wait(v, "s", tq2)
                tn = P.op(v, lambda e, which=which: e.tensor_tensor(B.qmax2[:, 2 * which:2 * which + 2],
                                                                   B.qmax2[:, 2 * which:2 * which + 2],
                                                                   B.qn2[:, 0:2], op=ALU.max), inc="v")
                gj_war = ("v", tn)
                P.waitp(v, B.qbf_free)
                tcast = P.op(v, lambda e, qn4=qn4, qbf4=qbf4: e.tensor_copy(qbf4, qn4), inc="v")
                B.qn_free = ("v", tcast)
                pending_tp.append((dst, tq_, tcast))
                if os.environ.get("DEVNODEFER", "") == "1":
                    flush_tp()
            flush_tp()
        tmem2 = P.op(v, lambda e: e.memset(B.Vt_hp[:, :], 1.0), inc="v")
        P.wait(v, "v", tmem2)
        P.wait(te, "v", tmem2)
        for ti in range(NT):
            pb = ti % 2
            pst = B.ps[pb]
            P.waitp(te, B.ps_free[pb])
            for ec in range(NC):
                tm = P.op(te, lambda e, ec=ec, ti=ti, pst=pst:
                          e.matmul(pst[:, 0:128], B.xnT[:, ec * T + ti * 128: ec * T + (ti + 1) * 128],
                                   wsl[:, 2048 + ec * 128: 2048 + (ec + 1) * 128],
                                   start=(ec == 0), stop=(ec == NC - 1)), inc="m")
            P.wait(v, "m", tm)
            tv = P.op(v, lambda e, ti=ti, pst=pst:
                      e.tensor_tensor(B.Vt_hp[:, ti * 130: ti * 130 + 130].rearrange("p (h d) -> p h d", h=2)[:, :, 0:64],
                                      pst[:, 0:128].rearrange("p (h d) -> p h d", h=2),
                                      B.brep[:, 256:384].rearrange("p (h d) -> p h d", h=2), op=ALU.add), inc="v")
            B.ps_free[pb] = ("v", tv)
        barrier(P)

        def mq():
            return P.op(v, lambda e: e.tensor_reduce(B.fold_a[:, 0:1], B.qmax2[:, 0:2], axis=AX.X,
                                                     op=ALU.max), inc="v")
        partition_total(P, B, mq, ALU.max, B.nr_sb[0:1, 4:5], 0)

        def mk():
            return P.op(v, lambda e: e.tensor_reduce(B.fold_a[:, 0:1], B.qmax2[:, 2:4], axis=AX.X,
                                                     op=ALU.max), inc="v")
        partition_total(P, B, mk, ALU.max, B.nr_sb[0:1, 5:6], 0)
        t = P.op(v, lambda e: e.tensor_tensor(B.nr_sb[0:1, 6:7], B.nr_sb[0:1, 4:5],
                                              B.nr_sb[0:1, 5:6], op=ALU.mult), inc="v")
        P.wait(sc, "v", t)
        t = P.op(sc, lambda e: e.activation(B.nr_sb[0:1, 6:7], B.nr_sb[0:1, 6:7], AF.Ln), inc="s")
        P.wait(sc, "s", t)
        t = P.op(sc, lambda e: e.activation(B.nr_sb[0:1, 6:7], B.nr_sb[0:1, 6:7], AF.Exp,
                                            scale=0.5), inc="s")
        P.wait(v, "s", t)
        t = P.op(v, lambda e: e.tensor_scalar(B.nr_sb[0:1, 6:7], B.nr_sb[0:1, 6:7], -0.125, None,
                                              op0=ALU.mult), inc="v")
        P.wait(v, "v", t)
        bcast_col(P, B, B.nr_sb[0:1, 6:7], B.negb[:, 0:1], B.bcb)
        barrier(P)
        exb = B.exbp
        psy = [B.ps[0], B.ps[1]]
        psr = B.ps[2]
        tex = [0, 0]
        tav = [[0, 0], [0, 0]]
        tsc_ = [0, 0]
        tyq = [0, 0]
        free = {"r": 0, "rs": 0, "rrow": 0}

        def emit_av(c, qb):
            sb = c & 1
            for h in range(2):
                if c == 0:
                    P.waitp(te, tyq[h])
                P.wait(te, "s", tex[sb])
                tav[h][sb] = ("m", P.op(te, lambda e, c=c, h=h, sb=sb:
                                        e.matmul(psy[h][0:65, :],
                                                 B.Vt_hp[:, c * 130 + h * 65: c * 130 + h * 65 + 65],
                                                 exb[sb][:, h * 512:(h + 1) * 512],
                                                 start=(c == 0), stop=(c == NT - 1)), inc="m"))
        for qb in range(8):
            for c in range(NT):
                sb = c & 1
                for h in range(2):
                    hb = h * 64
                    P.waitp(te, tex[sb] and ("s", tex[sb]))
                    tsc_[h] = P.op(te, lambda e, c=c, h=h, hb=hb, qb=qb, sb=sb:
                                   e.matmul(B.ps_pair[sb][:, h * 512:(h + 1) * 512],
                                            B.kT_hp[hb:hb + 64, c * 128:(c + 1) * 128],
                                            B.qT_hp[hb:hb + 64, qb * 512:(qb + 1) * 512],
                                            start=True, stop=True, tile_position=(hb, 0)), inc="m")
                P.wait(sc, "m", tsc_[1])
                P.waitp(sc, tav[0][sb])
                P.waitp(sc, tav[1][sb])
                tex[sb] = P.op(sc, lambda e, sb=sb:
                               e.activation(exb[sb][:, :], B.ps_pair[sb][:, :], AF.Exp,
                                            bias=B.negb[:, 0:1], scale=0.125), inc="s")
                if c >= 1:
                    emit_av(c - 1, qb)
            emit_av(NT - 1, qb)
            rrow1 = B.gelu_junk[1][0:1, :]
            rsb1 = B.gelu_junk[0][0:64, :]
            for h in range(2):
                hb = h * 64
                P.wait(sc, "m", tav[h][(NT - 1) & 1][1])
                P.waitp(sc, free["rrow"])
                td2 = P.op(sc, lambda e, h=h, rrow1=rrow1: e.activation(rrow1, psy[h][64:65, :],
                                                                       AF.Ln), inc="s")
                P.wait(sc, "s", td2)
                td2 = P.op(sc, lambda e: e.activation(B.rrow_bf[0:1, 0:512], rrow1, AF.Exp,
                                                      scale=-1.0), inc="s")
                P.wait(te, "s", td2)
                P.waitp(te, free["r"])
                tr = P.op(te, lambda e: e.matmul(psr[0:64, :], B.ones_row_bf[0:1, 0:64],
                                                 B.rrow_bf[0:1, 0:512], start=True, stop=True), inc="m")
                free["rrow"] = ("m", tr)
                P.wait(v, "m", tr)
                P.waitp(v, free["rs"])
                trs = P.op(v, lambda e, rsb1=rsb1: e.tensor_copy(rsb1, psr[0:64, :]), inc="v")
                free["r"] = ("v", trs)
                P.wait(v, "v", trs)
                ty = P.op(v, lambda e, h=h, hb=hb, qb=qb, hp=hp, rsb1=rsb1:
                          e.tensor_tensor(B.yT[hb:hb + 64, hp * T + qb * 512: hp * T + (qb + 1) * 512],
                                          psy[h][0:64, :], rsb1, op=ALU.mult), inc="v")
                tyq[h] = ("v", ty)
                free["rs"] = ("v", ty)
    B.yT_ready = P.now("v")


def build_l1_out(P, B, wo, w1t, vr, ct, bo, b1, x1_dram, out_dram):
    v, sc, te, g, sy = "vector", "scalar", "tensor", "gpsimd", "sync"
    barrier(P)
    P.dma(g, B.vr_bf[:, :].rearrange("p (ec r) -> p ec r", r=32),
          vr.rearrange("(ec p) r -> p ec r", p=128), inc="dw")
    P.dma(g, B.ct_bf[0:32, :], ct[0:32, :], inc="dw")
    tdw = P.now("dw")
    P.wait(te, "dw", tdw)
    for tb in range(8):
        pb = tb % 2
        pst = B.ps[pb]
        P.waitp(te, B.ps_free[pb])
        for ec in range(NC):
            tm = P.op(te, lambda e, ec=ec, tb=tb, pst=pst:
                      e.matmul(pst[0:32, :], B.vr_bf[:, ec * 32:(ec + 1) * 32],
                               B.xnT[:, ec * T + tb * 512: ec * T + (tb + 1) * 512],
                               start=(ec == 0), stop=(ec == NC - 1)), inc="m")
        P.wait(v, "m", tm)
        tv = P.op(v, lambda e, tb=tb, pst=pst: e.tensor_copy(B.xvrT[0:32, tb * 512:(tb + 1) * 512],
                                                             pst[0:32, :]), inc="v")
        B.ps_free[pb] = ("v", tv)
    barrier(P)
    wor = wo.rearrange("(ec p) n -> p ec n", p=128)
    w1r = w1t.rearrange("(ec p) n -> p ec n", p=128)
    halves = (1, 0) if os.environ.get("DEVJH", "") == "1" else (0, 1)
    for half in halves:
        P.waitp(g, B.wstream_free[0])
        P.waitp(g, B.wstream_free[1])
        P.dma(g, B.wstream[0][:, :].rearrange("p (ec n) -> p ec n", n=512),
              wor[:, :, half * 512:(half + 1) * 512], inc="dw")
        P.dma(g, B.wstream[1][:, :].rearrange("p (ec n) -> p ec n", n=512),
              w1r[:, :, half * 512:(half + 1) * 512], inc="dw")
        tdw = P.now("dw")
        P.wait(te, "dw", tdw)
        P.waitp(te, B.ps_free[4])
        P.waitp(g, B.frow_free)
        P.dma(g, B.ffn_row[0:1, 0:512], bo[0:1, half * 512:(half + 1) * 512], inc="dh")
        P.dma(g, B.ffn_row[0:1, 512:1024], b1[0:1, half * 512:(half + 1) * 512], inc="dh")
        tdh = P.now("dh")
        P.wait(te, "dh", tdh)
        tm = P.op(te, lambda e: e.matmul(B.ps[4][:, 0:512], B.ones_row_bf[0:1, :], B.ffn_row[0:1, 0:512],
                                         start=True, stop=False), inc="m")
        tm = P.op(te, lambda e: e.matmul(B.ps[4][:, 0:512], B.ones_row_bf[0:1, :], B.ffn_row[0:1, 512:1024],
                                         start=False, stop=True), inc="m")
        B.frow_free = ("m", tm)
        P.wait(v, "m", tm)
        trep = P.op(v, lambda e: e.tensor_copy(B.ffn_rep[:, 0:512], B.ps[4][:, 0:512]), inc="v")
        B.ps_free[4] = ("v", trep)
        d = "dxj"
        for i in range(NT):
            bi = i % 2
            xt = B.xt[bi]
            P.waitp(sy, B.xtj_free[bi])
            td = P.dma(sy, xt[:, 0:512], x1_dram[ts(i), half * 512:(half + 1) * 512], inc=d)
            pb = 2 + (i % 2)
            pst = B.ps[pb]
            P.waitp(te, B.ps_free[pb])
            DEVJ = os.environ.get("DEVJ", "all")
            first = True
            if DEVJ in ("all", "attn"):
                for cc in range(NC):
                    P.op(te, lambda e, cc=cc, i=i, pst=pst, first=first and cc == 0:
                         e.matmul(pst[:, :], B.yT[:, cc * T + i * 128: cc * T + (i + 1) * 128],
                                  B.wstream[0][:, cc * 512:(cc + 1) * 512],
                                  start=first and cc == 0, stop=False), inc="m")
                first = False
            if DEVJ in ("all", "ffn", "base", "corr0"):
                for cc in range(NC):
                    P.op(te, lambda e, cc=cc, i=i, pst=pst, st=first and cc == 0,
                         sp=(DEVJ == "base" and cc == NC - 1):
                         e.matmul(pst[:, :], B.xnT[:, cc * T + i * 128: cc * T + (i + 1) * 128],
                                  B.wstream[1][:, cc * 512:(cc + 1) * 512],
                                  start=st, stop=sp), inc="m")
                first = False
            if DEVJ != "base":
                tm = P.op(te, lambda e, i=i, pst=pst, st=first, half=half:
                          e.matmul(pst[:, :], B.xvrT[0:32, i * 128:(i + 1) * 128],
                                   B.ct_bf[0:32, half * 512:(half + 1) * 512],
                                   start=st, stop=True), inc="m")
            else:
                tm = P.now("m")
            P.wait(v, "m", tm)
            P.wait(v, d, td)
            P.wait(v, "v", trep)
            P.waitp(v, B.x1sbj_free)
            t1 = P.op(v, lambda e, bi=bi, pst=pst: e.tensor_tensor(B.x1_sb[0][:, 0:512], pst[:, :],
                                                                  B.xt[bi][:, 0:512], op=ALU.add), inc="v")
            P.wait(v, "v", t1)
            t2 = P.op(v, lambda e, bi=bi: e.tensor_tensor(B.x1_sb[0][:, 0:512], B.x1_sb[0][:, 0:512],
                                                          B.ffn_rep[:, 0:512], op=ALU.add), inc="v")
            B.ps_free[pb] = ("v", t2)
            P.wait(sy, "v", t2)
            tso = P.dma(sy, out_dram[ts(i), half * 512:(half + 1) * 512], B.x1_sb[0][:, 0:512], inc="dout")
            B.xtj_free[bi] = ("dout", tso)
            B.x1sbj_free = ("dout", tso)
        tlast = P.now("m")
        B.wstream_free = [("m", tlast), ("m", tlast)]
    P.wait(sy, "dout", P.now("dout"))




class Bufs:
    pass


STAGES = int(os.environ.get('STAGES', '99'))


def build_core_program(debug_taps=()):
    nc = bass.Bass("TRN2", target_bir_lowering=False)
    ctx = ExitStack()
    B = Bufs()

    x = nc.dram_tensor("x", [T, E], F32, kind="ExternalInput").ap()
    out = nc.dram_tensor("out", [T, E], F32, kind="ExternalOutput").ap()

    taps = {}

    with ctx:
        B.xnT = ctx.enter_context(nc.sbuf_tensor("xnT", [128, NC * T], BF16))
        B.xt = [ctx.enter_context(nc.sbuf_tensor(f"xt{k}", [128, E], F32)) for k in range(2)]
        B.xn_bf = ctx.enter_context(nc.sbuf_tensor("xn_bf", [128, E], BF16))
        B.ln_m = ctx.enter_context(nc.sbuf_tensor("ln_m", [128, 1], F32))
        B.ln_sb = ctx.enter_context(nc.sbuf_tensor("ln_sb", [128, 1], F32))
        B.ln_s2 = ctx.enter_context(nc.sbuf_tensor("ln_s2", [128, 1], F32))
        B.ln_msq = ctx.enter_context(nc.sbuf_tensor("ln_msq", [128, 1], F32))
        B.ln_var = ctx.enter_context(nc.sbuf_tensor("ln_var", [128, 1], F32))
        B.ln_u = ctx.enter_context(nc.sbuf_tensor("ln_u", [128, 1], F32))
        B.ln_rstd = ctx.enter_context(nc.sbuf_tensor("ln_rstd", [128, 1], F32))
        B.iota_f = ctx.enter_context(nc.sbuf_tensor("iota_f", [128, 128], F32))
        B.iota_p = ctx.enter_context(nc.sbuf_tensor("iota_p", [128, 1], F32))
        B.iden = ctx.enter_context(nc.sbuf_tensor("iden", [128, 128], BF16))
        B.causal = ctx.enter_context(nc.sbuf_tensor("causal", [128, 128], BF16))
        B.ones_col = ctx.enter_context(nc.sbuf_tensor("ones_col", [128, 1], F32))
        B.eps_col = ctx.enter_context(nc.sbuf_tensor("eps_col", [128, 1], F32))

        B.v_sb = ctx.enter_context(nc.sbuf_tensor("v_sb", [128, 64], F32))
        B.iptbuf_free = [0, 0]
        B.gelu_junk = [ctx.enter_context(nc.sbuf_tensor(f"gjunk{k}", [128, 512], F32)) for k in range(2)]
        B.gj_free = [0, 0]
        B.vcol = [ctx.enter_context(nc.sbuf_tensor(f"vcol{k}", [128, 1], F32)) for k in range(2)]
        B.fold_a = ctx.enter_context(nc.sbuf_tensor("fold_a", [128, 1], F32))
        B.fold_b = ctx.enter_context(nc.sbuf_tensor("fold_b", [64, 1], F32))
        B.fold_s = ctx.enter_context(nc.sbuf_tensor("fold_s", [64, 1], F32))
        B.fold_c = ctx.enter_context(nc.sbuf_tensor("fold_c", [32, 1], F32))
        B.fold_t = ctx.enter_context(nc.sbuf_tensor("fold_t", [32, 32], F32))
        B.fold_tt = ctx.enter_context(nc.sbuf_tensor("fold_tt", [32, 32], F32))
        B.bis = ctx.enter_context(nc.sbuf_tensor("bis", [1, 8], F32))
        B.bis_neg = ctx.enter_context(nc.sbuf_tensor("bis_neg", [128, 64], F32))
        B.bis_ge = ctx.enter_context(nc.sbuf_tensor("bis_ge", [128, 64], F32))
        B.bis_mid = ctx.enter_context(nc.sbuf_tensor("bis_mid", [128, 1], F32))
        B.ones_row = ctx.enter_context(nc.sbuf_tensor("ones_row", [1, 128], F32))
        B.thr_rep = ctx.enter_context(nc.sbuf_tensor("thr_rep", [128, 1], F32))
        B.masked_bf = ctx.enter_context(nc.sbuf_tensor("masked_bf", [128, 64], BF16))
        B.mask_buf = ctx.enter_context(nc.sbuf_tensor("mask_buf", [128, 512], BF16))
        B.mask_f = ctx.enter_context(nc.sbuf_tensor("mask_f", [128, 64], F32))
        B.wstream = [ctx.enter_context(nc.sbuf_tensor(f"wstr{k}", [128, 4096], BF16)) for k in range(2)]
        B.wstream_free = [0, 0]
        B.ipt_buf = [B.wstream[k][:, 0:1024] for k in range(2)]
        B.pv_buf = [ctx.enter_context(nc.sbuf_tensor(f"pvb{k}", [128, 512], BF16)) for k in range(2)]
        B.pvbuf_free = [0, 0]
        B.rowbuf = [ctx.enter_context(nc.sbuf_tensor(f"rowb{k}", [1, 512], F32)) for k in range(2)]
        B.rowbuf_free = [0, 0]
        B.p_sb = ctx.enter_context(nc.sbuf_tensor("p_sb", [128, 32], F32))
        B.e_sb = ctx.enter_context(nc.sbuf_tensor("e_sb", [128, 32], F32))
        B.e_bf = ctx.enter_context(nc.sbuf_tensor("e_bf", [128, 32], BF16))
        B.negpmax = ctx.enter_context(nc.sbuf_tensor("negpmax", [128, 1], F32))
        B.sc4 = ctx.enter_context(nc.sbuf_tensor("sc4", [1, 4], F32))
        B.agg_sb = ctx.enter_context(nc.sbuf_tensor("agg_sb", [1, 512], BF16))
        B.agg_col_bf = ctx.enter_context(nc.sbuf_tensor("agg_col_bf", [128, 4], BF16))
        B.o_sb = ctx.enter_context(nc.sbuf_tensor("o_sb", [128, 64], F32))
        B.thr2_rep = ctx.enter_context(nc.sbuf_tensor("thr2_rep", [128, 1], F32))
        B.thr3_rep = ctx.enter_context(nc.sbuf_tensor("thr3_rep", [128, 1], F32))
        B.ffn_row = ctx.enter_context(nc.sbuf_tensor("ffn_row", [1, 1024], BF16))
        B.ones_row_bf = ctx.enter_context(nc.sbuf_tensor("ones_row_bf", [1, 128], BF16))
        B.ffn_rep = ctx.enter_context(nc.sbuf_tensor("ffn_rep", [128, 1024], F32))
        B.ones11 = ctx.enter_context(nc.sbuf_tensor("ones11", [1, 1], F32))
        B.ones11_bf = ctx.enter_context(nc.sbuf_tensor("ones11_bf", [1, 1], BF16))
        B.yT = ctx.enter_context(nc.sbuf_tensor("yT", [128, NC * T], BF16))
        B.qT_hp = ctx.enter_context(nc.sbuf_tensor("qT_hp", [128, T], BF16))
        B.kT_hp = ctx.enter_context(nc.sbuf_tensor("kT_hp", [128, T], BF16))
        B.Vt_hp = ctx.enter_context(nc.sbuf_tensor("Vt_hp", [128, NT * 130], BF16))
        B.knat_sb = ctx.enter_context(nc.sbuf_tensor("knat_sb", [128, 128], BF16))
        B.S_sb = ctx.enter_context(nc.sbuf_tensor("S_sb", [128, 65], BF16))
        B.ab_sb = [ctx.enter_context(nc.sbuf_tensor(f"ab{k}", [128, 128], BF16)) for k in range(2)]
        B.den_sb = B.gelu_junk[1][0:1, 0:512]
        B.rrow_sb = B.wstream[0][:, 3072:4096].bitcast(F32)[0:1, 0:512]
        B.rrow_bf = B.wstream[0][:, 3072:3584][0:1, 0:512]
        B.r_sb = B.gelu_junk[0][0:64, 0:512]
        B.x1_sb = [ctx.enter_context(nc.sbuf_tensor("x1sb0", [128, E], F32))] * 2
        B.xt_free2 = [0, 0]
        B.xtj_free = [0, 0]
        B.x1sbj_free = 0
        B.frow_free = 0
        B.qn_free = 0
        B.qbf_free = 0
        B.nr_sb = ctx.enter_context(nc.sbuf_tensor("nr_sb", [1, 8], F32))
        B.jrow = ctx.enter_context(nc.sbuf_tensor("jrow", [1, 64], F32))
        B.posrow = [ctx.enter_context(nc.sbuf_tensor(f"posr{k}", [1, 128], F32)) for k in range(2)]
        B.posrow_free = [0, 0]
        B.qn2 = ctx.enter_context(nc.sbuf_tensor("qn2", [128, 2], F32))
        B.qmax2 = ctx.enter_context(nc.sbuf_tensor("qmax2", [128, 4], F32))
        B.negb = ctx.enter_context(nc.sbuf_tensor("negb", [128, 2], F32))
        B.cos_nat = B.wstream[1][:, 0:1024].bitcast(F32)
        B.sin_nat = B.wstream[1][:, 1024:2048].bitcast(F32)
        B.exbp = [B.wstream[1][:, 2048:3072], B.wstream[1][:, 3072:4096]]
        B.qn = B.wstream[0][:, 3840:4096].bitcast(F32)
        B.rope_scr = B.iota_f[:, :]
        B.qbf = B.mask_f[:, :].bitcast(BF16)
        B.brep = B.wstream[0][:, 3072:3840].bitcast(F32)
        B.vr_bf = B.kT_hp[:, 0:256]
        B.ct_bf = B.qT_hp[0:32, 0:1024]
        B.xvrT = B.Vt_hp[0:32, 0:4096]

        B.ps = [ctx.enter_context(nc.psum_tensor(f"ps{k}", [128, 512], F32)) for k in range(4)]
        B.ps_pair = [ctx.enter_context(nc.psum_tensor(f"pspair{k}", [128, 1024], F32)) for k in range(2)]
        B.ps += [B.ps_pair[0][:, 0:512], B.ps_pair[0][:, 512:1024],
                 B.ps_pair[1][:, 0:512], B.ps_pair[1][:, 512:1024]]
        B.ps_free = [0] * 8
        B.bcb = B.ps[4]
        B.ps_bf = [B.ps[2][:, :].bitcast(BF16), B.ps[3][:, :].bitcast(BF16)]
        B.psbf_free = [0, 0]
        B.xt_free = [0, 0]
        B.xnbf_free = 0

        P = Prog(nc, ctx)
        build_constants(P, B)
        P.op("vector", lambda e: e.memset(B.ones_row[:, :], 1.0), inc="v")
        P.wait("tensor", "v", B.const_ready)
        build_ln_xnt(P, B, x, tag="0")
        if STAGES >= 2:
            ipt = nc.dram_tensor("ipt", [E, N_IN], F32, kind="ExternalInput").ap()
            build_acts_mean(P, B, ipt)
            build_bisect(P, B, B.v_sb[:, :], 64, K_IN, B.thr_rep[:, 0:1])
            v = "vector"
            t = P.op(v, lambda e: e.tensor_scalar(B.mask_f[:, :], B.v_sb[:, :], B.thr_rep[:, 0:1],
                                                  None, op0=ALU.is_ge), inc="v")
            P.wait(v, "v", t)
            t = P.op(v, lambda e: e.tensor_tensor(B.masked_bf[:, :], B.v_sb[:, :], B.mask_f[:, :],
                                                  op=ALU.mult), inc="v")
            P.wait(v, "v", t)
            B.maskedv_ready = t
        if STAGES >= 3:
            piwt = nc.dram_tensor("piwt", [N_IN, N_PROC], F32, kind="ExternalInput").ap()
            pv = nc.dram_tensor("pv", [N_PROC, D_PV], F32, kind="ExternalInput").ap()
            oiwt = nc.dram_tensor("oiwt", [D_PV, N_OUT], F32, kind="ExternalInput").ap()
            op_d = nc.dram_tensor("op", [N_OUT, E], F32, kind="ExternalInput").ap()
            build_ffn_chain(P, B, piwt, pv, oiwt, op_d)
        if STAGES >= 4:
            w_attn = nc.dram_tensor("w_attn", [E, 3 * E], F32, kind="ExternalInput").ap()
            build_linattn(P, B, w_attn)
        if STAGES >= 5:
            w_proj = nc.dram_tensor("w_proj", [E, E], F32, kind="ExternalInput").ap()
            x1 = nc.dram_tensor("x1", [T, E], F32, kind="Internal").ap()
            build_wproj_residual(P, B, w_proj, x, x1)
            B.x1_ap = x1
        if STAGES >= 6:
            barrier(P)
            build_ln_xnt(P, B, x1, tag="g1")
        if STAGES >= 7:
            nrope = nc.dram_tensor("nrope", [1, 1], F32, kind="ExternalInput").ap()
            wq = nc.dram_tensor("wq", [E, E], F32, kind="ExternalInput").ap()
            wk = nc.dram_tensor("wk", [E, E], F32, kind="ExternalInput").ap()
            wv = nc.dram_tensor("wv", [E, E], F32, kind="ExternalInput").ap()
            bq = nc.dram_tensor("bq", [1, E], F32, kind="ExternalInput").ap()
            bk = nc.dram_tensor("bk", [1, E], F32, kind="ExternalInput").ap()
            bv = nc.dram_tensor("bv", [1, E], F32, kind="ExternalInput").ap()
            build_rope_tables(P, B, nrope)
            build_l1_attn(P, B, wq, wk, wv, bq, bk, bv)
        if STAGES >= 8:
            wo = nc.dram_tensor("wo", [E, E], F32, kind="ExternalInput").ap()
            w1t = nc.dram_tensor("w1t", [E, E], F32, kind="ExternalInput").ap()
            vr = nc.dram_tensor("vr", [E, 32], F32, kind="ExternalInput").ap()
            ct = nc.dram_tensor("ct", [32, E], F32, kind="ExternalInput").ap()
            bo = nc.dram_tensor("bo", [1, E], F32, kind="ExternalInput").ap()
            b1 = nc.dram_tensor("b1", [1, E], F32, kind="ExternalInput").ap()
            build_l1_out(P, B, wo, w1t, vr, ct, bo, b1, x1, out)

        for name in debug_taps:
            if name == "vmean":
                tp = nc.dram_tensor("tap_vmean", [128, 64], F32, kind="ExternalOutput").ap()
                P.wait("sync", "v", P.now("v"))
                P.dma("sync", tp[:, :], B.v_sb[:, :], inc="do")
                tp2 = nc.dram_tensor("tap_thr", [128, 1], F32, kind="ExternalOutput").ap()
                P.dma("sync", tp2[:, :], B.thr_rep[:, :], inc="do")
                tp3 = nc.dram_tensor("tap_masked", [128, 64], F32, kind="ExternalOutput").ap()
                P.wait("gpsimd", "v", P.now("v"))
                P.dma("gpsimd", tp3[:, :], B.masked_bf[:, :], inc="do")
            if name == "ffn":
                tp = nc.dram_tensor("tap_pacts", [128, 32], F32, kind="ExternalOutput").ap()
                tp2 = nc.dram_tensor("tap_agg", [1, 512], F32, kind="ExternalOutput").ap()
                tp3 = nc.dram_tensor("tap_oacts", [128, 64], F32, kind="ExternalOutput").ap()
                tp4 = nc.dram_tensor("tap_ffnrep", [128, 1024], F32, kind="ExternalOutput").ap()
                P.wait("sync", "v", P.now("v"))
                P.wait("sync", "s", P.now("s"))
                P.dma("sync", tp[:, :], B.p_sb[:, :], inc="do")
                P.dma("sync", tp2[:, :], B.agg_sb[:, :], inc="do")
                P.dma("sync", tp3[:, :], B.o_sb[:, :], inc="do")
                P.dma("sync", tp4[:, :], B.ffn_rep[:, :], inc="do")
            if name == "yT":
                tp = nc.dram_tensor("tap_yT", [128, NC * T], F32, kind="ExternalOutput").ap()
                P.wait("gpsimd", "v", B.yT_ready)
                for c in range(NC):
                    P.dma("gpsimd", tp[:, c * T:(c + 1) * T], B.yT[:, c * T:(c + 1) * T], inc="do")
            if name == "x1":
                tp = nc.dram_tensor("tap_x1", [T, E], F32, kind="ExternalOutput").ap()
                P.wait("sync", "dx1", P.now("dx1"))
                for i in range(8):
                    P.dma("sync", tp[ts(i, 512), :], B.x1_ap[ts(i, 512), :], inc="do")
            if name == "qT":
                tp = nc.dram_tensor("tap_qT", [128, T], F32, kind="ExternalOutput").ap()
                tp2 = nc.dram_tensor("tap_kT", [128, T], F32, kind="ExternalOutput").ap()
                P.wait("gpsimd", "v", P.now("v"))
                P.dma("gpsimd", tp[:, :], B.qT_hp[:, :], inc="do")
                P.dma("gpsimd", tp2[:, :], B.kT_hp[:, :], inc="do")
            if name == "xnT":
                tp = nc.dram_tensor("tap_xnT", [128, NC * T], F32, kind="ExternalOutput").ap()
                P.wait("gpsimd", "v", B.xnt_ready)
                for c in range(NC):
                    P.dma("gpsimd", tp[:, c * T:(c + 1) * T], B.xnT[:, c * T:(c + 1) * T], inc="do")
                taps[name] = "tap_xnT"

        if STAGES < 8:
            P.wait("sync", "dx0", B.x_stream_done)
            for i in range(NT):
                P.dma("sync", out[ts(i), :], x[ts(i), :], inc="do")
        P.wait("sync", "do", P.now("do"))
        P.finalize()
    return nc, taps


def _layout_inputs(inputs):
    w = {
        "w_attn": inputs["L0_attn_w_attn"],
        "w_proj": inputs["L0_attn_w_proj"],
        "ipt": np.ascontiguousarray(inputs["L0_ffn_input_patterns"].T),
        "piwt": np.ascontiguousarray(inputs["L0_ffn_process_input_weights"].T),
        "pv": inputs["L0_ffn_process_values"],
        "oiwt": np.ascontiguousarray(inputs["L0_ffn_output_input_weights"].T),
        "op": inputs["L0_ffn_output_patterns"],
        "wq": inputs["L1_attn_Wq"], "wk": inputs["L1_attn_Wk"],
        "wv": inputs["L1_attn_Wv"], "wo": inputs["L1_attn_Wo"],
        "bq": inputs["L1_attn_bq"].reshape(1, E), "bk": inputs["L1_attn_bk"].reshape(1, E),
        "bv": inputs["L1_attn_bv"].reshape(1, E), "bo": inputs["L1_attn_bo"].reshape(1, E),
        "nrope": inputs["L1_attn_nrope_freq"].reshape(1, 1),
        "w1t": np.ascontiguousarray(inputs["L1_ffn_W"].T),
        "b1": inputs["L1_ffn_b"].reshape(1, E),
        "vr": inputs["L1_ffn_V_r"],
        "ct": np.ascontiguousarray(inputs["L1_ffn_C"].T),
    }
    w = {k: np.ascontiguousarray(v, dtype=np.float32) for k, v in w.items()}
    xs = inputs["L0_attn_x"]
    return [dict(w, x=np.ascontiguousarray(xs[b])) for b in range(8)]


def kernel(**inputs):
    in_maps = _layout_inputs(inputs)
    nc, _ = build_core_program()
    res = run_bass_kernel_spmd(nc, in_maps, core_ids=list(range(8)))
    return np.stack([res.results[b]["out"] for b in range(8)], axis=0)


if __name__ == "__main__":
    pass
